# version 04
# baseline (speedup 1.0000x reference)
import contextlib
from collections import defaultdict

import numpy as np
import ml_dtypes

import concourse.bass as bass
import concourse.mybir as mybir
from concourse.bass import AP, IndirectOffsetOnAxis
from concourse.bass_utils import run_bass_kernel_spmd
from concourse.aot_env import allow_ici_all_to_all

F32 = mybir.dt.float32
BF16 = mybir.dt.bfloat16
I32 = mybir.dt.int32
AF = mybir.ActivationFunctionType
ALU = mybir.AluOpType
AXX = mybir.AxisListType.X

NCORES = 8
D = 2048
HID = 2816
T = 2048
NH, DH = 16, 128
KC = D // 128
MC = HID // 128
TC = T // 128
CAP = 2048
BLK = 512
SENT = 30000.0
EPS = 1e-5

DBG = []
STOP = None
GLIMIT = None
STOP_AFTER = None


class Prog:
    ENGS = ("sync", "scalar", "vector", "tensor", "gpsimd")

    def __init__(self):
        self.ops = defaultdict(list)
        self.semval = defaultdict(int)
        self.lastwait = {}
        self.semnames = set()
        self.trace = defaultdict(list)

    @staticmethod
    def canon(name):
        for suf in ("p1", "p3", "at"):
            if name.endswith(suf) and len(name) > len(suf):
                return name[:-len(suf)]
        return name

    def emit(self, eng, fn):
        self.ops[eng].append(fn)

    def inc(self, name, n=1):
        name = self.canon(name)
        self.semnames.add(name)
        self.semval[name] += n
        return self.semval[name]

    def cur(self, name):
        name = self.canon(name)
        self.semnames.add(name)
        return self.semval[name]

    def wait(self, eng, name, val=None):
        name = self.canon(name)
        if val is None:
            val = self.semval[name]
        if val <= 0:
            return
        key = (eng, name)
        if self.lastwait.get(key, -1) >= val:
            return
        self.lastwait[key] = val
        self.semnames.add(name)

        def f(e, _n=name, _v=val):
            e.wait_ge(e._sems[_n], _v)
        self.emit(eng, f)
        self.trace[eng].append(("w", name, val))

    def barrier(self, tag):
        toks = {}
        for eng in self.ENGS:
            ch = "bar_" + eng
            toks[eng] = self.inc(ch, 1)
            def f(e, ch=ch):
                e.nop().then_inc(e._sems[ch], 1)
            self.emit(eng, f)
            self.trace[eng].append(("i", ch, 1))
        for eng in self.ENGS:
            for src in self.ENGS:
                if src != eng:
                    self.wait(eng, "bar_" + src, toks[src])


def _ap(t, offset, dims):
    return AP(t.tensor, offset, [list(d) for d in dims])


def build(consts_vals):
    nc = bass.Bass("TRN2", target_bir_lowering=False, debug=False)
    P = Prog()

    def ein(name, shape, dt=F32):
        return nc.dram_tensor(name, list(shape), dt, kind="ExternalInput").ap()

    x0 = ein("x0", (T, D))
    wq = ein("wq", (NH * D, DH))
    wk = ein("wk", (NH * D, DH))
    wv = ein("wv", (D, D))
    wgm = ein("wgm", (D, D))
    wo = ein("wo", (D, D))
    gnw_i = ein("gnw", (128, NH))
    gnb_i = ein("gnb", (128, NH))
    wgate = ein("wgate", (128, 128))
    w1 = ein("w1", (D, HID))
    w3 = ein("w3", (D, HID))
    w2 = ein("w2", (HID, D))
    sw1 = ein("sw1", (D, HID))
    sw3 = ein("sw3", (D, HID))
    sw2 = ein("sw2", (HID, D))
    cq_t = ein("cq", (128, T))
    sq_t = ein("sq", (128, T))
    ck_t = ein("ck", (128, T))
    sk_t = ein("sk", (128, T))
    gpow_i = ein("gpow", (128, NH))
    gklc_i = ein("gklc", (128, NH))
    ginv_i = ein("ginv", (128, NH))
    maskt_i = ein("maskt", (128, 128), BF16)
    triu_ii = ein("triu_i", (128, 128))
    triu_ei = ein("triu_e", (128, 128))
    ident_i = ein("ident", (128, 128), BF16)
    identf_i = ein("identf", (8, 8))
    rrot_i = ein("rrot", (128, 128), BF16)
    ones1_i = ein("ones1", (1, 128))
    onesc_i = ein("onesc", (128, 1))
    iota8_i = ein("iota8m8", (128, 8))
    iotarc_i = ein("iotarc", (128, 32))
    jone_i = ein("jone", (128, 1))
    wspan_i = ein("wspan", (128, 128))
    consts_i = ein("constsin", (128, 64))

    out_x3 = nc.dram_tensor("out", [T, D], F32, kind="ExternalOutput").ap()

    def din(name, shape, dt):
        return nc.dram_tensor(name, list(shape), dt, kind="Internal").ap()

    xr1 = din("xr1", (T, D), F32)
    xr2 = din("xr2", (T, D), F32)
    xlnr = din("xlnr", (T, D), BF16)
    xlolo = din("xlolo", (T, D), BF16)
    lg_loc = din("lg_loc", (T, 8), F32)
    lg_out = din("lg_out", (NCORES * T, 8), F32)
    tot_rt = din("tot_rt", (128, 8), F32)
    a2a_s = din("a2a_s", (NCORES * BLK, D), BF16)
    a2a_r = din("a2a_r", (NCORES * BLK, D), BF16)
    mt_s = din("mt_s", (NCORES * BLK, 8), F32)
    mt_r = din("mt_r", (NCORES * BLK, 8), F32)
    disp = din("disp", (CAP, D), BF16)
    backp = din("backp", (CAP, 8), F32)
    shr = din("shr", (T, D), BF16)
    b2a_s = din("b2a_s", (NCORES * BLK, D), BF16)
    b2a_r = din("b2a_r", (NCORES * BLK, D), BF16)
    kxf = din("kxf", (D, T), BF16)
    kst = din("kst", (T, D), BF16)
    vst = din("vst", (T, D), BF16)
    st_in = din("st_in", (NH * 128, 128), F32)
    st_out = din("st_out", (NCORES * NH * 128, 128), F32)
    ynr = din("ynr", (T, D), BF16)
    dramv = dict(xr1=xr1, xr2=xr2, xlnr=xlnr, xlolo=xlolo, lg_loc=lg_loc,
                 lg_out=lg_out, tot_rt=tot_rt, a2a_s=a2a_s, a2a_r=a2a_r,
                 mt_s=mt_s, mt_r=mt_r, disp=disp, backp=backp, shr=shr,
                 b2a_s=b2a_s, b2a_r=b2a_r, kxf=kxf, kst=kst, vst=vst,
                 st_in=st_in, st_out=st_out, ynr=ynr)
    dbg = {}
    for name in DBG:
        src = dramv[name]
        dbg[name] = nc.dram_tensor("dbg_" + name, list(src.tensor.shape),
                                   src.tensor.dtype, kind="ExternalOutput").ap()

    es = contextlib.ExitStack()

    def sbt(name, shape, dt):
        return es.enter_context(nc.sbuf_tensor(name, list(shape), dt))

    R1 = sbt("R1", (128, 32768), BF16)
    R2 = sbt("R2", (128, 45056), BF16)
    WS1 = sbt("WS1", (128, 4096), BF16)
    WS3 = sbt("WS3", (128, 4096), BF16)
    DUM = sbt("DUM", (128, 2048), BF16)
    XT = sbt("XT", (128, 2048), F32)
    BT0 = sbt("BT0", (128, 2048), BF16)
    BT1 = sbt("BT1", (128, 2048), BF16)
    SC = sbt("SCR", (128, 2048), F32)
    GN_S = sbt("GN_S", (128, 512), F32)
    CONSTS = sbt("CONSTS", (128, 64), F32)
    STAT = sbt("STAT", (128, 64), F32)
    EQ = sbt("EQ", (128, 32), F32)
    OHB = sbt("OHB", (128, 1024), BF16)
    ONESC_B = sbt("ONESC_Bs", (128, 1), BF16)
    LGT = sbt("LGT", (128, 16), F32)
    TOTS = sbt("TOTS", (128, 8), F32)
    OFFS = sbt("OFFS", (128, 8), F32)
    COFF = sbt("COFF", (128, 8), F32)
    LOC = sbt("LOC", (1, 32), F32)
    LSLOT = sbt("LSLOT", (128, 16), F32)
    LSEND = sbt("LSEND", (128, 16), F32)
    LGK = sbt("LGK", (128, 16), F32)
    MR = sbt("MR", (128, 8), F32)
    BP8 = sbt("BP8", (128, 8), F32)
    IDXS = [sbt(f"IDXS{i}", (128, 1), I32) for i in range(4)]
    OOBT = sbt("OOBT", (128, 128), F32)
    WGH = sbt("WGH", (128, 128), BF16)
    WGL = sbt("WGL", (128, 128), BF16)
    TRIU_I = sbt("TRIU_Is", (128, 128), F32)
    TRIU_E = sbt("TRIU_Es", (128, 128), F32)
    IOTA8 = sbt("IOTA8s", (128, 8), F32)
    IOTARC = sbt("IOTARCs", (128, 32), F32)
    IDENT = sbt("IDENTs", (128, 128), BF16)
    IDENTF = sbt("IDENTFs", (8, 8), F32)
    RROT = sbt("RROTs", (128, 128), BF16)
    MASKT = sbt("MASKTs", (128, 128), BF16)
    GPOW = sbt("GPOWs", (128, NH), F32)
    GKLC = sbt("GKLCs", (128, NH), F32)
    GINV = sbt("GINVs", (128, NH), F32)
    GNW = sbt("GNWs", (128, NH), F32)
    GNB = sbt("GNBs", (128, NH), F32)
    JONE = sbt("JONEs", (128, 1), F32)
    WSPAN = sbt("WSPANs", (128, 128), F32)
    ONES1 = sbt("ONES1s", (1, 128), F32)
    ONESC = sbt("ONESCs", (128, 1), F32)
    ARES = sbt("ARES", (128, 128), BF16)
    S_F = sbt("S_F", (128, 128), F32)
    S_B = sbt("S_B", (128, 128), BF16)
    MTILE = sbt("MTILE", (128, 128), F32)

    A_Q, A_K, A_KS, A_VS, A_T1, A_T2 = 32768, 34816, 36864, 38912, 40960, 43008

    PSA = es.enter_context(nc.psum_tensor("PSA", [128, 2048], F32))
    PSB = es.enter_context(nc.psum_tensor("PSB", [128, 1024], F32))
    PTB_full = [es.enter_context(nc.psum_tensor(f"PTB{i}", [128, 1024], BF16))
                for i in range(2)]
    PTB = [t[:, 0:128] for t in PTB_full]

    def psa(i):
        return PSA[:, i * 512:(i + 1) * 512]

    def psb(i):
        return PSB[:, i * 512:(i + 1) * 512]

    cmap = {}

    def ccol(v):
        v = float(v)
        if v not in cmap:
            assert len(consts_vals) < 64
            cmap[v] = len(consts_vals)
            consts_vals.append(v)
        i = cmap[v]
        return CONSTS[:, i:i + 1]

    gammas = 1.0 - np.exp(np.linspace(np.log(1 / 32), np.log(1 / 512), NH))
    GC = [float(g ** 128) for g in gammas]

    def dma(q, sem, out, in_, transpose=False):
        sem = P.canon(sem)
        tok = P.inc(sem, 16)
        P.trace[q].append(("i", sem, 16))
        def f(e, out=out, in_=in_, tp=transpose, sem=sem):
            if tp:
                e.dma_start(out=out, in_=in_, transpose=True).then_inc(e._sems[sem], 16)
            else:
                e.dma_start(out=out, in_=in_).then_inc(e._sems[sem], 16)
        P.emit(q, f)
        return tok

    def icall(sem, out, out_off, in_, in_off, bound, elem_off=0):
        sem = P.canon(sem)
        tok = P.inc(sem, 16)
        P.trace["gpsimd"].append(("i", sem, 16))
        def f(e, out=out, oo=out_off, in_=in_, io=in_off, b=bound, eo=elem_off,
              sem=sem):
            e.indirect_dma_start(out, oo, in_, io, element_offset=eo,
                                 bounds_check=e._regs[b],
                                 oob_is_err=False).then_inc(e._sems[sem], 16)
        P.emit("gpsimd", f)
        return tok

    def cop(eng, sem, fn, n=1, fence=True):
        sem = P.canon(sem)
        tok = P.inc(sem, n)
        P.trace[eng].append(("i", sem, n))
        def f(e, fn=fn, sem=sem, n=n):
            fn(e).then_inc(e._sems[sem], n)
        P.emit(eng, f)
        if fence and eng in ("vector", "gpsimd", "scalar"):
            P.wait(eng, sem, tok)
        return tok

    def coll(sem, kind, ins, outs):
        sem = P.canon(sem)
        tok = P.inc(sem, 1)
        P.trace["gpsimd"].append(("i", sem, 1))
        def f(e, kind=kind, ins=ins, outs=outs, sem=sem):
            e.collective_compute(kind, ALU.bypass,
                                 replica_groups=[list(range(NCORES))],
                                 ins=ins, outs=outs).then_inc(e._sems[sem], 1)
        P.emit("gpsimd", f)
        return tok

    def wslice(w, m, kchunks, width, dst, dstcol, ncol=None):
        if ncol is None:
            ncol = w.tensor.shape[1]
        src = _ap(w, m * width, [[ncol, 128], [128 * ncol, kchunks], [1, width]])
        return dma("gpsimd", "wsl", dst[:, dstcol:dstcol + kchunks * width], src)

    for dst, src in [
        (CONSTS[:], consts_i), (ONES1[:], ones1_i), (ONESC[:], onesc_i),
        (TRIU_I[:], triu_ii), (TRIU_E[:], triu_ei), (IOTA8[:], iota8_i),
        (IOTARC[:], iotarc_i), (IDENT[:], ident_i), (IDENTF[:], identf_i),
        (RROT[:], rrot_i), (MASKT[:], maskt_i),
        (GPOW[:], gpow_i), (GKLC[:], gklc_i), (GINV[:], ginv_i),
        (GNW[:], gnw_i), (GNB[:], gnb_i), (JONE[:], jone_i),
        (WSPAN[:], wspan_i), (SC[:, 0:128].with_dtype(F32) if False else SC[:, 0:128], wgate),
    ]:
        dma("gpsimd", "boot", dst, src)
    P.wait("vector", "boot")
    cop("vector", "vb", lambda e: e.tensor_copy(WGH[:], SC[:, 0:128]))
    cop("vector", "vb", lambda e: e.tensor_copy(SC[:, 128:256], WGH[:]))
    cop("vector", "vb", lambda e: e.tensor_sub(SC[:, 256:384], SC[:, 0:128], SC[:, 128:256]))
    cop("vector", "vb", lambda e: e.tensor_copy(WGL[:], SC[:, 256:384]))
    cop("vector", "vb", lambda e: e.tensor_copy(ONESC_B[:], ONESC[:]))
    cop("vector", "vb", lambda e: e.memset(OOBT[:], SENT))
    cop("vector", "vb", lambda e: e.memset(BT1[:], 0.0))
    P.wait("gpsimd", "vb")
    for buf, nrow in ((a2a_s, NCORES * BLK), (a2a_r, NCORES * BLK),
                      (b2a_s, NCORES * BLK), (b2a_r, NCORES * BLK), (disp, CAP)):
        for c in range(0, nrow, 128):
            dma("gpsimd", "zero", buf[c:c + 128, :], BT1[:])
    P.barrier("boot")

    def ffn_pass(tag, xin_rows, xout_rows):
        trdone = {}

        def stopped(name):
            return STOP is not None and STOP == name

        V, SCL, TT_, GQ = "v" + tag, "s" + tag, "t" + tag, "g" + tag

        dma("gpsimd", "fl" + tag, _ap(mt_s, 0, [[128, 128], [1, 128]]), OOBT[:])
        dma("gpsimd", "fl" + tag, _ap(mt_s, 128 * 128, [[128, 128], [1, 128]]), OOBT[:])
        dma("gpsimd", "fl" + tag, _ap(backp, 0, [[128, 128], [1, 128]]),
            OOBT[:])

        xt_free = {}
        for c in range(TC):
            cs, cq, cv2, cr = c % 16, 16 + c % 16, 32 + c % 16, 48 + c % 16
            if c >= 1:
                P.wait("sync", V, xt_free[c - 1][0])
                P.wait("sync", SCL, xt_free[c - 1][1])
            ld = dma("sync", "xl" + tag, XT[:], xin_rows[c * 128:(c + 1) * 128, :])
            P.wait("vector", "xl" + tag, ld)
            cop("vector", V, lambda e, cs=cs: e.reduce_sum(
                STAT[:, cs:cs + 1], XT[:], axis=AXX), fence=False)
            P.wait("scalar", "xl" + tag, ld)
            t_sq = cop("scalar", SCL, lambda e, cq=cq: e.activation(
                DUM[:], XT[:], AF.Square, accum_out=STAT[:, cq:cq + 1]))
            if c >= 1:
                P.wait("vector", "xs" + tag, P.cur("xs" + tag) - 32)
            t_xc = cop("vector", V, lambda e: e.tensor_copy(SC[:], XT[:]),
                       fence=False)
            xt_free[c] = (t_xc, t_sq)
            P.wait("vector", SCL, t_sq)
            cop("vector", V, lambda e, cs=cs: e.tensor_scalar(
                STAT[:, cs:cs + 1], STAT[:, cs:cs + 1], ccol(1.0 / D), None,
                ALU.mult))
            cop("vector", V, lambda e, cs=cs, cv2=cv2: e.tensor_mul(
                STAT[:, cv2:cv2 + 1], STAT[:, cs:cs + 1], STAT[:, cs:cs + 1]))
            cop("vector", V, lambda e, cq=cq, cv2=cv2: e.scalar_tensor_tensor(
                STAT[:, cv2:cv2 + 1], STAT[:, cq:cq + 1], ccol(1.0 / D),
                STAT[:, cv2:cv2 + 1], ALU.mult, ALU.subtract))
            cop("vector", V, lambda e, cv2=cv2: e.tensor_scalar(
                STAT[:, cv2:cv2 + 1], STAT[:, cv2:cv2 + 1], ccol(EPS), None,
                ALU.add))
            t_rc = cop("vector", V, lambda e, cv2=cv2, cr=cr: e.reciprocal(
                STAT[:, cr:cr + 1], STAT[:, cv2:cv2 + 1]))
            P.wait("scalar", V, t_rc)
            t_rs = cop("scalar", SCL, lambda e, cr=cr: e.activation(
                STAT[:, cr:cr + 1], STAT[:, cr:cr + 1], AF.Sqrt))
            P.wait("vector", SCL, t_rs)
            cop("vector", V, lambda e, cs=cs, cr=cr: e.tensor_scalar(
                SC[:], SC[:], STAT[:, cs:cs + 1], STAT[:, cr:cr + 1],
                ALU.subtract, ALU.mult), fence=False)
            cop("vector", V, lambda e: e.tensor_copy(BT0[:], SC[:]), fence=False)
            t_n = cop("vector", V, lambda e: e.tensor_tensor(
                BT1[:], SC[:], BT0[:], ALU.subtract), fence=False)
            P.wait("sync", V, t_n)
            dma("sync", "xs" + tag, xlnr[c * 128:(c + 1) * 128, :], BT0[:])
            st = dma("sync", "xs" + tag, xlolo[c * 128:(c + 1) * 128, :], BT1[:])
        P.wait("vector", "xs" + tag, P.cur("xs" + tag))
        ln_done = P.cur("xs" + tag)
        if stopped("ln"):
            return None

        P.wait("sync", "xs" + tag, ln_done)
        P.wait("scalar", "xs" + tag, ln_done)
        for k in range(KC):
            dma("sync", "th" + tag, R1[:, k * T:(k + 1) * T],
                xlnr[:, k * 128:(k + 1) * 128], transpose=True)
        for k in range(KC):
            dma("sync", "tl" + tag, R2[:, k * T:(k + 1) * T],
                xlolo[:, k * 128:(k + 1) * 128], transpose=True)
        thi = P.cur("th" + tag)
        tlo = P.cur("tl" + tag)

        P.wait("tensor", "th" + tag, thi)
        P.wait("tensor", "tl" + tag, tlo)
        tp_ev = {}
        for c in range(TC):
            bk = c % 2
            if c >= 2:
                P.wait("tensor", V, tp_ev[c - 2])
            def rmm(e, c=c, bk=bk):
                o = PSB[0:128, bk * 512:bk * 512 + 8]
                last = None
                for src, wgt in ((R1, WGH), (R1, WGL), (R2, WGH)):
                    for k in range(KC):
                        first = (src is R1 and wgt is WGH and k == 0)
                        lastf = (src is R2 and k == KC - 1)
                        last = e.matmul(
                            o, src[:, k * T + c * 128:k * T + (c + 1) * 128],
                            wgt[:, k * 8:k * 8 + 8],
                            start=first, stop=lastf)
                return last
            t_mm = cop("tensor", TT_, rmm)
            P.wait("vector", TT_, t_mm)
            tp_ev[c] = cop("vector", V, lambda e, c=c, bk=bk: e.tensor_copy(
                LGT[:, 8 * bk:8 * bk + 8], PSB[0:128, bk * 512:bk * 512 + 8]))
            P.wait("sync", V, tp_ev[c])
            stg = dma("sync", "lg" + tag, lg_loc[c * 128:(c + 1) * 128, :],
                      LGT[:, 8 * bk:8 * bk + 8])
            P.wait("vector", "lg" + tag, stg - 16)
        P.wait("gpsimd", "lg" + tag, P.cur("lg" + tag))
        if stopped("rt"):
            return None
        cc_lg = coll("cl" + tag, "AllGather", [lg_loc], [lg_out])

        def h_phase(wA, wB, pref, first_bank_wait=None, per_m_hook=None):
            sil_tok = {}
            tt_tok = {}
            for m in range(MC):
                if per_m_hook is not None:
                    per_m_hook(m)
                buf = m % 2
                if m >= 2:
                    P.wait("gpsimd", pref + "tt", 4 * (m - 1))
                wslice(wA, m, KC, 128, WS1, buf * 2048)
                lB = wslice(wB, m, KC, 128, WS3, buf * 2048)
                P.wait("tensor", "wsl", lB)
                if m == 0 and first_bank_wait is not None:
                    first_bank_wait()
                if m >= 1:
                    for n in range(4):
                        P.wait("tensor", pref + "sl", sil_tok[(m - 1, n)])
                    P.wait("tensor", pref + "tt", tt_tok[(m - 1, 3)])
                def mmA(e, buf=buf):
                    last = None
                    for k in range(KC):
                        for n in range(4):
                            last = e.matmul(
                                psa(n),
                                WS1[:, buf * 2048 + k * 128:buf * 2048 + (k + 1) * 128],
                                R1[:, k * T + n * 512:k * T + n * 512 + 512],
                                start=(k == 0), stop=(k == KC - 1))
                    return last
                tA = cop("tensor", pref + "mm", mmA)
                for n in range(4):
                    P.wait("scalar", pref + "mm", tA)
                    sil_tok[(m, n)] = cop("scalar", pref + "sl",
                                          lambda e, n=n: e.activation(
                                              DUM[:, n * 512:(n + 1) * 512],
                                              psa(n), AF.Silu))
                P.wait("tensor", pref + "sl", sil_tok[(m, 1)])
                if m >= 1:
                    P.wait("tensor", pref + "tt", tt_tok[(m - 1, 1)])
                h3bank = [psb(0), psb(1), psa(0), psa(1)]
                def mmB(e, buf=buf, h3bank=h3bank):
                    last = None
                    for k in range(KC):
                        for n in range(4):
                            last = e.matmul(
                                h3bank[n],
                                WS3[:, buf * 2048 + k * 128:buf * 2048 + (k + 1) * 128],
                                R1[:, k * T + n * 512:k * T + n * 512 + 512],
                                start=(k == 0), stop=(k == KC - 1))
                    return last
                tB = cop("tensor", pref + "mm", mmB)
                for n in range(4):
                    P.wait("vector", pref + "mm", tB)
                    P.wait("vector", pref + "sl", sil_tok[(m, n)])
                    tt_tok[(m, n)] = cop("vector", pref + "tt", fence=False) if False else cop("vector", pref + "tt",
                                         lambda e, m=m, n=n, h3bank=h3bank:
                                         e.tensor_tensor(
                                             R2[:, m * T + n * 512:m * T + n * 512 + 512],
                                             DUM[:, n * 512:(n + 1) * 512],
                                             h3bank[n], ALU.mult))
            return tt_tok

        lread = {}
        f4a_state = {"started": False}

        def emit_f4a_chunk(g):
            if not f4a_state["started"]:
                P.wait("vector", "cl" + tag, cc_lg)
                P.wait("scalar", "lg" + tag, P.cur("lg" + tag))
                P.wait("scalar", "cl" + tag, cc_lg)
                f4a_state["started"] = True
            sl = 8 * (g % 2)
            if g >= 1:
                P.wait("scalar", V, lread[g - 1])
            ld = dma("scalar", "gl" + tag, LGT[:, sl:sl + 8],
                     lg_out[g * 128:(g + 1) * 128, :])
            P.wait("vector", "gl" + tag, ld)
            L = LGT[:, sl:sl + 8]
            O = OHB[:, 8 * g:8 * g + 8]
            cop("vector", V, lambda e, L=L: e.tensor_tensor(
                STAT[:, 28:32], L[:, 0:4], L[:, 4:8], ALU.max))
            cop("vector", V, lambda e: e.tensor_tensor(
                STAT[:, 28:30], STAT[:, 28:30], STAT[:, 30:32], ALU.max))
            cop("vector", V, lambda e: e.tensor_tensor(
                STAT[:, 26:27], STAT[:, 28:29], STAT[:, 29:30], ALU.max))
            lread[g] = cop("vector", V, lambda e, L=L, O=O: e.tensor_scalar(
                O, L, STAT[:, 26:27], None, ALU.is_equal))
            cop("vector", V, lambda e, O=O: e.tensor_mul(EQ[:, 16:24], O,
                                                         IOTA8[:]))
            cop("vector", V, lambda e: e.tensor_tensor(
                STAT[:, 28:32], EQ[:, 16:20], EQ[:, 20:24], ALU.min))
            cop("vector", V, lambda e: e.tensor_tensor(
                STAT[:, 28:30], STAT[:, 28:30], STAT[:, 30:32], ALU.min))
            cop("vector", V, lambda e: e.tensor_tensor(
                STAT[:, 27:28], STAT[:, 28:29], STAT[:, 29:30], ALU.min))
            cop("vector", V, lambda e, O=O: e.tensor_scalar(
                O, IOTA8[:], STAT[:, 27:28], None, ALU.is_equal))

        def f4a_hook(m):
            lo_g = m * 8
            hi_g = min(128, lo_g + 8)
            for g in range(lo_g, hi_g):
                emit_f4a_chunk(g)

        P.wait("vector", TT_, P.cur(TT_))
        h_phase(sw1, sw3, "sh" + tag, per_m_hook=f4a_hook)
        for g in range(8 * MC, 128):
            emit_f4a_chunk(g)
        f4a_done = P.cur(V)
        sh_tt_all = P.cur("sh" + tag + "tt")
        sh_mm_all = P.cur("sh" + tag + "mm")

        P.wait("tensor", V, f4a_done)
        P.wait("tensor", "sh" + tag + "tt", sh_tt_all)
        for g in range(128):
            cop("tensor", "gt" + tag, lambda e, g=g: e.matmul(
                PSB[0:1, 8 * (g % 64) + 512 * (g // 64):
                    8 * (g % 64) + 512 * (g // 64) + 8],
                ONESC_B[:], OHB[:, 8 * g:8 * g + 8],
                start=(g % 64 == 0), stop=(g % 64 == 63)))
        t_tot = P.cur("gt" + tag)
        P.wait("vector", "gt" + tag, t_tot)
        t_te = cop("vector", V, lambda e: e.tensor_copy(SC[0:1, 0:1024],
                                                        PSB[0:1, 0:1024]))
        P.wait("sync", V, t_te)
        tr1 = dma("sync", "tr" + tag, _ap(tot_rt, 0, [[1024, 1], [1, 1024]]),
                  SC[0:1, 0:1024])
        trdone["tok"] = tr1
        P.wait("sync", "tr" + tag, tr1)
        tr2 = dma("sync", "tr" + tag, TOTS[:], tot_rt)
        P.wait("tensor", "tr" + tag, tr2)
        P.wait("tensor", V, t_te)
        t_off = cop("tensor", "gt" + tag, lambda e: e.matmul(
            PSB[0:128, 0:8], TRIU_E[:], TOTS[:], start=True, stop=True))
        P.wait("vector", "gt" + tag, t_off)
        t_offe = cop("vector", V, lambda e: e.tensor_copy(OFFS[:],
                                                          PSB[0:128, 0:8]))
        P.wait("tensor", V, t_offe)
        t_co = cop("tensor", "gt" + tag, lambda e: e.matmul(
            PSB[0:1, 8:16], JONE[:], OFFS[:], start=True, stop=True))
        P.wait("vector", "gt" + tag, t_co)
        t_coe = cop("vector", V, lambda e: e.tensor_copy(LOC[0:1, 16:24],
                                                         PSB[0:1, 8:16]))
        P.wait("tensor", V, t_coe)
        t_cb = cop("tensor", "gt" + tag, lambda e: e.matmul(
            PSB[0:128, 8:16], ONES1[:], LOC[0:1, 16:24], start=True, stop=True))
        P.wait("vector", "gt" + tag, t_cb)
        t_coffe = cop("vector", V, lambda e: e.tensor_copy(COFF[:],
                                                           PSB[0:128, 8:16]))
        cop("vector", V, lambda e: e.memset(LOC[0:1, 0:8], 0.0))

        bpread = {}
        f5_tok = {}

        def emit_f5_chunk(lc):
            if lc == 0:
                P.wait("vector", "tr" + tag, trdone["tok"])
            if lc >= 1:
                P.wait("scalar", V, bpread[lc - 1])
            ld = dma("scalar", "ll" + tag, BP8[:],
                     lg_loc[lc * 128:(lc + 1) * 128, :])
            P.wait("vector", "ll" + tag, ld)
            cop("vector", V, lambda e: e.reduce_max(STAT[:, 8:9], BP8[:],
                                                    axis=AXX))
            cop("vector", V, lambda e: e.tensor_scalar(
                EQ[:, 24:32], BP8[:], STAT[:, 8:9], None, ALU.is_equal))
            cop("vector", V, lambda e: e.tensor_mul(SC[:, 16:24], EQ[:, 24:32],
                                                    IOTA8[:]))
            cop("vector", V, lambda e: e.tensor_reduce(
                STAT[:, 9:10], SC[:, 16:24], axis=AXX, op=ALU.min))
            t_oh = cop("vector", V, lambda e: e.tensor_scalar(
                EQ[:, 24:32], IOTA8[:], STAT[:, 9:10], None, ALU.is_equal))
            bpread[lc] = cop("vector", V, lambda e: e.tensor_scalar(
                SC[:, 0:8], BP8[:], STAT[:, 8:9], None, ALU.subtract))
            P.wait("scalar", V, P.cur(V))
            P.wait("scalar", "sh" + tag + "tt", min(4, sh_tt_all))
            t_e = cop("scalar", SCL, lambda e: e.activation(
                GN_S[:, 0:8], SC[:, 0:8], AF.Exp, accum_out=STAT[:, 10:11]))
            P.wait("vector", SCL, t_e)
            cop("vector", V, lambda e: e.reciprocal(STAT[:, 11:12],
                                                    STAT[:, 10:11]))
            P.wait("tensor", V, t_oh)
            if lc == 0:
                P.wait("tensor", V, t_coffe)
            bk = lc % 2
            t_pc = cop("tensor", "lt" + tag, lambda e, bk=bk: (
                e.matmul(PSB[0:1, 512 * bk + 24:512 * bk + 32], ONESC[:],
                         EQ[:, 24:32], start=True, stop=True),
                e.matmul(PSB[0:128, 512 * bk + 32:512 * bk + 40], TRIU_I[:],
                         EQ[:, 24:32], start=True, stop=False),
                e.matmul(PSB[0:128, 512 * bk + 32:512 * bk + 40], ONES1[:],
                         LOC[0:1, 0:8], start=False, stop=True))[-1])
            P.wait("vector", "lt" + tag, t_pc)
            cop("vector", V, lambda e, bk=bk: e.tensor_tensor(
                LOC[0:1, 0:8], LOC[0:1, 0:8],
                PSB[0:1, 512 * bk + 24:512 * bk + 32], ALU.add))
            cop("vector", V, lambda e, bk=bk: e.tensor_tensor(
                SC[:, 8:16], EQ[:, 24:32],
                PSB[0:128, 512 * bk + 32:512 * bk + 40], ALU.mult))
            cop("vector", V, lambda e: e.reduce_sum(STAT[:, 12:13], SC[:, 8:16],
                                                    axis=AXX))
            cop("vector", V, lambda e: e.tensor_mul(SC[:, 8:16], EQ[:, 24:32],
                                                    COFF[:]))
            cop("vector", V, lambda e: e.reduce_sum(STAT[:, 13:14], SC[:, 8:16],
                                                    axis=AXX))
            cop("vector", V, lambda e: e.tensor_add(STAT[:, 14:15],
                                                    STAT[:, 12:13],
                                                    STAT[:, 13:14]))
            cop("vector", V, lambda e: e.tensor_scalar(
                STAT[:, 15:16], STAT[:, 14:15], ccol(float(CAP)), None,
                ALU.is_le))
            cop("vector", V, lambda e: e.tensor_scalar(
                STAT[:, 17:18], STAT[:, 14:15], ccol(1.0), None, ALU.subtract))
            cop("vector", V, lambda e: e.scalar_tensor_tensor(
                STAT[:, 18:19], STAT[:, 17:18], ccol(SENT), STAT[:, 15:16],
                ALU.subtract, ALU.mult))
            cop("vector", V, lambda e, lc=lc: e.tensor_scalar(
                LSLOT[:, lc:lc + 1], STAT[:, 18:19], ccol(SENT), None, ALU.add))
            cop("vector", V, lambda e: e.tensor_scalar(
                STAT[:, 19:20], STAT[:, 9:10], ccol(512.0), None, ALU.mult))
            cop("vector", V, lambda e: e.scalar_tensor_tensor(
                STAT[:, 20:21], STAT[:, 12:13], ccol(4095.0), STAT[:, 19:20],
                ALU.add, ALU.add))
            cop("vector", V, lambda e: e.tensor_scalar(
                STAT[:, 21:22], STAT[:, 12:13], ccol(512.0), None, ALU.is_le))
            cop("vector", V, lambda e: e.scalar_tensor_tensor(
                STAT[:, 22:23], STAT[:, 20:21], ccol(SENT), STAT[:, 21:22],
                ALU.subtract, ALU.mult))
            cop("vector", V, lambda e, lc=lc: e.tensor_scalar(
                LSEND[:, lc:lc + 1], STAT[:, 22:23], ccol(SENT), None, ALU.add))
            f5_tok[lc] = cop("vector", V, lambda e, lc=lc: e.scalar_tensor_tensor(
                LGK[:, lc:lc + 1], STAT[:, 11:12], ccol(0.5), STAT[:, 15:16],
                ALU.mult, ALU.mult))

        if stopped("f5"):
            for lc in range(TC):
                emit_f5_chunk(lc)
            return None

        P.wait("gpsimd", "sh" + tag + "mm", sh_mm_all)
        P.wait("tensor", "sh" + tag + "tt", sh_tt_all)
        sw2_ev = {}
        shs_tok = {}
        s2m_base = P.cur("s2m" + tag)
        for half in range(2):
            if half == 1:
                P.wait("gpsimd", "s2m" + tag, s2m_base + TC)
            dma("gpsimd", "w2l" + tag, R1[:, 0:MC * 1024],
                _ap(sw2, half * 1024, [[D, 128], [128 * D, MC], [1, 1024]]))
            wl = P.cur("w2l" + tag)
            P.wait("tensor", "w2l" + tag, wl)
            for m in range(TC):
                par = m % 2
                if m >= 2:
                    P.wait("tensor", "s2e" + tag, sw2_ev[(half, m - 2)])
                elif half == 1:
                    P.wait("tensor", "s2e" + tag, sw2_ev[(0, TC - 2 + m)])
                def mm2(e, m=m, par=par):
                    last = None
                    for k in range(MC):
                        for n in range(2):
                            last = e.matmul(
                                psa(par * 2 + n),
                                R2[:, k * T + m * 128:k * T + (m + 1) * 128],
                                R1[:, k * 1024 + n * 512:k * 1024 + (n + 1) * 512],
                                start=(k == 0), stop=(k == MC - 1))
                    return last
                t_m = cop("tensor", "s2m" + tag, mm2)
                P.wait("vector", "s2m" + tag, t_m)
                if (half, m - 2) in shs_tok:
                    P.wait("vector", "shs" + tag, shs_tok[(half, m - 2)])
                elif half == 1 and (0, TC - 2 + m) in shs_tok:
                    P.wait("vector", "shs" + tag, shs_tok[(0, TC - 2 + m)])
                bt = BT0 if par == 0 else BT1
                cop("vector", V, lambda e, par=par, bt=bt: e.tensor_copy(
                    bt[:, 0:512], psa(par * 2)))
                t_ev = cop("vector", "s2e" + tag,
                           lambda e, par=par, bt=bt: e.tensor_copy(
                               bt[:, 512:1024], psa(par * 2 + 1)))
                sw2_ev[(half, m)] = t_ev
                P.wait("sync", "s2e" + tag, t_ev)
                shs_tok[(half, m)] = dma(
                    "sync", "shs" + tag,
                    _ap(shr, m * 128 * D + half * 1024, [[D, 128], [1, 1024]]),
                    bt[:, 0:1024])
                if half == 0:
                    emit_f5_chunk(m)

        if stopped("sh"):
            return None
        ds_tok = {}
        for lc in range(TC):
            ib = lc % 2
            buf = DUM[:] if ib == 0 else XT[:]
            P.wait("vector", V, f5_tok[lc])
            if lc >= 2:
                P.wait("vector", "ds" + tag, ds_tok[lc - 2])
                P.wait("scalar", "ds" + tag, ds_tok[lc - 2])
            t_ix = cop("vector", V, lambda e, lc=lc, ib=ib: e.tensor_copy(
                IDXS[ib][:], LSEND[:, lc:lc + 1]))
            lr = dma("scalar" if ib == 0 else "gpsimd", "dr" + tag, buf,
                     xlnr[lc * 128:(lc + 1) * 128, :])
            P.wait("gpsimd", V, t_ix)
            P.wait("gpsimd", "dr" + tag, lr)
            icall("ds" + tag, a2a_s, IndirectOffsetOnAxis(IDXS[ib][:], 0),
                  buf, None, 4095)
            ds_tok[lc] = icall("ds" + tag, mt_s,
                               IndirectOffsetOnAxis(IDXS[ib][:], 0),
                               LSLOT[:, lc:lc + 1], None, 4095)
        P.wait("gpsimd", "fl" + tag, P.cur("fl" + tag))
        P.wait("gpsimd", "ds" + tag, P.cur("ds" + tag))
        cc_a = coll("ca" + tag, "AllToAll", [a2a_s], [a2a_r])
        cc_a = coll("ca" + tag, "AllToAll", [mt_s], [mt_r])

        P.wait("scalar", "ca" + tag, cc_a)
        cp_tok = {}
        cp_ix = {}
        for rc in range(32):
            ib = 2 + rc % 2
            buf = DUM[:] if rc % 2 == 0 else XT[:]
            if rc >= 2:
                P.wait("scalar", "cp" + tag, cp_tok[rc - 2])
                P.wait("vector", "cp" + tag, cp_tok[rc - 2])
            if rc >= 1:
                P.wait("scalar", V, cp_ix[rc - 1])
            ldm = dma("scalar", "mr" + tag, MR[:],
                      mt_r[rc * 128:(rc + 1) * 128, :])
            ldr = dma("scalar" if rc % 2 == 0 else "gpsimd", "rr" + tag, buf,
                      a2a_r[rc * 128:(rc + 1) * 128, :])
            P.wait("vector", "mr" + tag, ldm)
            t_ix = cop("vector", V, lambda e, ib=ib: e.tensor_copy(
                IDXS[ib][:], MR[:, 0:1]))
            cp_ix[rc] = t_ix
            P.wait("gpsimd", V, t_ix)
            P.wait("gpsimd", "rr" + tag, ldr)
            icall("cp" + tag, disp, IndirectOffsetOnAxis(IDXS[ib][:], 0),
                  buf, None, 2047)
            cp_tok[rc] = icall("cp" + tag, backp,
                               IndirectOffsetOnAxis(IDXS[ib][:], 0),
                               IOTARC[:, rc:rc + 1], None, 2047)
        compact_done = P.cur("cp" + tag)
        if stopped("disp"):
            return None

        P.wait("sync", "s2m" + tag, P.cur("s2m" + tag))
        P.wait("sync", "cp" + tag, compact_done)
        for k in range(KC):
            dma("sync", "xd" + tag, R1[:, k * T:(k + 1) * T],
                disp[:, k * 128:(k + 1) * 128], transpose=True)
        P.wait("tensor", "xd" + tag, P.cur("xd" + tag))
        P.wait("vector", "s2m" + tag, P.cur("s2m" + tag))
        s2e_all = P.cur("s2e" + tag)

        def exp_bank_wait():
            P.wait("tensor", "s2e" + tag, s2e_all)
        h_phase(w1, w3, "ex" + tag, first_bank_wait=exp_bank_wait)
        ex_tt_all = P.cur("ex" + tag + "tt")
        ex_mm_all = P.cur("ex" + tag + "mm")

        P.wait("gpsimd", "ex" + tag + "mm", ex_mm_all)
        e2m_base = P.cur("e2m" + tag)
        e2s_tok = {}
        e2_ix = {"last": None}
        for half in range(2):
            if half == 1:
                P.wait("gpsimd", "e2m" + tag, e2m_base + TC)
            dma("gpsimd", "e2l" + tag, R1[:, 0:MC * 1024],
                _ap(w2, half * 1024, [[D, 128], [128 * D, MC], [1, 1024]]))
            wl = P.cur("e2l" + tag)
            P.wait("tensor", "e2l" + tag, wl)
            for m in range(TC):
                par = m % 2
                ib = par
                if m >= 2:
                    P.wait("tensor", "e2s" + tag, e2s_tok[(half, m - 2)])
                elif half == 1:
                    P.wait("tensor", "e2s" + tag, e2s_tok[(0, TC - 2 + m)])
                if m == 0 and half == 0:
                    P.wait("tensor", "ex" + tag + "tt", ex_tt_all)
                def mm2(e, m=m, par=par):
                    last = None
                    for k in range(MC):
                        for n in range(2):
                            last = e.matmul(
                                psa(par * 2 + n),
                                R2[:, k * T + m * 128:k * T + (m + 1) * 128],
                                R1[:, k * 1024 + n * 512:k * 1024 + (n + 1) * 512],
                                start=(k == 0), stop=(k == MC - 1))
                    return last
                t_m = cop("tensor", "e2m" + tag, mm2)
                if e2_ix.get("last") is not None:
                    P.wait("scalar", V, e2_ix["last"])
                ldb = dma("scalar", "bp" + tag, BP8[:],
                          backp[m * 128:(m + 1) * 128, :])
                P.wait("vector", "bp" + tag, ldb)
                P.wait("vector", "e2m" + tag, t_m)
                if (half, m - 2) in e2s_tok:
                    P.wait("vector", "e2s" + tag, e2s_tok[(half, m - 2)])
                elif half == 1 and (0, TC - 2 + m) in e2s_tok:
                    P.wait("vector", "e2s" + tag, e2s_tok[(0, TC - 2 + m)])
                bt = BT0 if par == 0 else BT1
                cop("vector", V, lambda e, par=par, bt=bt: e.tensor_copy(
                    bt[:, 0:512], psa(par * 2)))
                cop("vector", V, lambda e, par=par, bt=bt: e.tensor_copy(
                    bt[:, 512:1024], psa(par * 2 + 1)))
                t_ix = cop("vector", V, lambda e, ib=ib: e.tensor_copy(
                    IDXS[ib][:], BP8[:, 0:1]))
                e2_ix["last"] = t_ix
                P.wait("gpsimd", V, t_ix)
                e2s_tok[(half, m)] = icall(
                    "e2s" + tag, b2a_s, IndirectOffsetOnAxis(IDXS[ib][:], 0),
                    bt[:, 0:1024], None, 4095, elem_off=half * 1024)
        P.wait("gpsimd", "e2s" + tag, P.cur("e2s" + tag))
        if stopped("exp"):
            return None
        cc_b = coll("cb" + tag, "AllToAll", [b2a_s], [b2a_r])

        P.wait("vector", "cb" + tag, cc_b)
        co_tok = {}
        for lc in range(TC):
            ib = lc % 2
            if lc >= 2:
                P.wait("gpsimd", V, co_tok[lc - 2])
            cop("vector", V, lambda e, lc=lc: e.tensor_scalar(
                STAT[:, 24:25], LSEND[:, lc:lc + 1], ccol(4095.0), None,
                ALU.min))
            t_ix = cop("vector", V, lambda e, ib=ib: e.tensor_copy(
                IDXS[ib][:], STAT[:, 24:25]))
            P.wait("gpsimd", V, t_ix)
            bt = BT0 if ib == 0 else BT1
            t_g = icall("cg" + tag, bt[:], None, b2a_r,
                        IndirectOffsetOnAxis(IDXS[ib][:], 0), 4095)
            lds = dma("scalar", "s2" + tag, DUM[:],
                      shr[lc * 128:(lc + 1) * 128, :])
            if lc >= 1:
                P.wait("sync", "xo" + tag, P.cur("xo" + tag))
            ldx = dma("sync", "xc" + tag, XT[:],
                      xin_rows[lc * 128:(lc + 1) * 128, :])
            P.wait("vector", "cg" + tag, t_g)
            P.wait("vector", "s2" + tag, lds)
            P.wait("vector", "xc" + tag, ldx)
            cop("vector", V, lambda e, bt=bt, lc=lc: e.tensor_scalar(
                SC[:], bt[:], LGK[:, lc:lc + 1], None, ALU.mult))
            cop("vector", V, lambda e: e.scalar_tensor_tensor(
                SC[:], DUM[:], ccol(0.5), SC[:], ALU.mult, ALU.add))
            t_o = cop("vector", V, lambda e: e.tensor_add(XT[:], SC[:], XT[:]))
            co_tok[lc] = t_o
            P.wait("sync", V, t_o)
            dma("sync", "xo" + tag, xout_rows[lc * 128:(lc + 1) * 128, :], XT[:])
            P.wait("scalar", V, t_o)
        return P.cur("xo" + tag)

    def attn_pass(xin_rows, xout_rows):
        tag = "at"
        V, SCL, TT_ = "vat", "sat", "tat"

        def stopped(name):
            return STOP is not None and STOP == name

        xt_free = {}
        for c in range(TC):
            cs, cq, cv2, cr = c % 16, 16 + c % 16, 32 + c % 16, 48 + c % 16
            if c >= 1:
                P.wait("sync", V, xt_free[c - 1][0])
                P.wait("sync", SCL, xt_free[c - 1][1])
            ld = dma("sync", "xl" + tag, XT[:], xin_rows[c * 128:(c + 1) * 128, :])
            P.wait("vector", "xl" + tag, ld)
            cop("vector", V, lambda e, cs=cs: e.reduce_sum(
                STAT[:, cs:cs + 1], XT[:], axis=AXX), fence=False)
            P.wait("scalar", "xl" + tag, ld)
            t_sq = cop("scalar", SCL, lambda e, cq=cq: e.activation(
                DUM[:], XT[:], AF.Square, accum_out=STAT[:, cq:cq + 1]))
            if c >= 1:
                P.wait("vector", "xs" + tag, P.cur("xs" + tag) - 16)
            t_xc = cop("vector", V, lambda e: e.tensor_copy(SC[:], XT[:]),
                       fence=False)
            xt_free[c] = (t_xc, t_sq)
            P.wait("vector", SCL, t_sq)
            cop("vector", V, lambda e, cs=cs: e.tensor_scalar(
                STAT[:, cs:cs + 1], STAT[:, cs:cs + 1], ccol(1.0 / D), None,
                ALU.mult))
            cop("vector", V, lambda e, cs=cs, cv2=cv2: e.tensor_mul(
                STAT[:, cv2:cv2 + 1], STAT[:, cs:cs + 1], STAT[:, cs:cs + 1]))
            cop("vector", V, lambda e, cq=cq, cv2=cv2: e.scalar_tensor_tensor(
                STAT[:, cv2:cv2 + 1], STAT[:, cq:cq + 1], ccol(1.0 / D),
                STAT[:, cv2:cv2 + 1], ALU.mult, ALU.subtract))
            cop("vector", V, lambda e, cv2=cv2: e.tensor_scalar(
                STAT[:, cv2:cv2 + 1], STAT[:, cv2:cv2 + 1], ccol(EPS), None,
                ALU.add))
            t_rc = cop("vector", V, lambda e, cv2=cv2, cr=cr: e.reciprocal(
                STAT[:, cr:cr + 1], STAT[:, cv2:cv2 + 1]))
            P.wait("scalar", V, t_rc)
            t_rs = cop("scalar", SCL, lambda e, cr=cr: e.activation(
                STAT[:, cr:cr + 1], STAT[:, cr:cr + 1], AF.Sqrt))
            P.wait("vector", SCL, t_rs)
            t_n = cop("vector", V, lambda e, cs=cs, cr=cr: e.tensor_scalar(
                BT0[:], SC[:], STAT[:, cs:cs + 1], STAT[:, cr:cr + 1],
                ALU.subtract, ALU.mult), fence=False)
            P.wait("sync", V, t_n)
            st = dma("sync", "xs" + tag, xlnr[c * 128:(c + 1) * 128, :], BT0[:])
            P.wait("vector", "xs" + tag, st - 16)
        P.wait("sync", "xs" + tag, P.cur("xs" + tag))
        for k in range(KC):
            dma("sync", "th" + tag, R1[:, k * T:(k + 1) * T],
                xlnr[:, k * 128:(k + 1) * 128], transpose=True)
        thi = P.cur("th" + tag)
        dma("gpsimd", "tb" + tag, R2[:, A_T1:A_T1 + T], ck_t)
        dma("gpsimd", "tb" + tag, R2[:, A_T2:A_T2 + T], sk_t)
        P.wait("tensor", "th" + tag, thi)
        P.wait("vector", "tb" + tag, P.cur("tb" + tag))
        if stopped("aln"):
            return None

        def proj_rot_xpos(wsrc, h, dst_col, store_ch=None, store_to=None):
            P.wait("gpsimd", TT_, P.cur(TT_))
            lw = dma("gpsimd", "wsl", WS1[:, 0:2048],
                     _ap(wsrc, h * D * DH, [[DH, 128], [128 * DH, KC], [1, 128]]))
            P.wait("tensor", "wsl", lw)
            P.wait("tensor", V, P.cur(V))
            def mmP(e):
                last = None
                for k in range(KC):
                    for n in range(4):
                        last = e.matmul(
                            psa(n), WS1[:, k * 128:(k + 1) * 128],
                            R1[:, k * T + n * 512:k * T + n * 512 + 512],
                            start=(k == 0), stop=(k == KC - 1))
                return last
            t_p = cop("tensor", TT_, mmP)
            for n in range(4):
                P.wait("vector", TT_, t_p)
                if store_ch is not None:
                    P.wait("vector", store_ch, P.cur(store_ch))
                cop("vector", V, lambda e, n=n, dst_col=dst_col: e.tensor_copy(
                    R2[:, dst_col + n * 512:dst_col + (n + 1) * 512], psa(n)))
            t_raw = P.cur(V)
            for n in range(4):
                bk = n % 2
                P.wait("tensor", V, t_raw if n < 2 else P.cur(V))
                t_r = cop("tensor", TT_, lambda e, n=n, bk=bk, dst_col=dst_col:
                          e.matmul(psb(bk), RROT[:],
                                   R2[:, dst_col + n * 512:dst_col + (n + 1) * 512],
                                   start=True, stop=True))
                P.wait("vector", TT_, t_r)
                cop("vector", V, lambda e, n=n, bk=bk: e.tensor_tensor(
                    SC[:, 0:512], psb(bk),
                    R2[:, A_T2 + n * 512:A_T2 + (n + 1) * 512], ALU.mult))
                cop("vector", V, lambda e, n=n, dst_col=dst_col: e.tensor_tensor(
                    SC[:, 512:1024],
                    R2[:, dst_col + n * 512:dst_col + (n + 1) * 512],
                    R2[:, A_T1 + n * 512:A_T1 + (n + 1) * 512], ALU.mult))
                cop("vector", V, lambda e, n=n, dst_col=dst_col: e.tensor_tensor(
                    R2[:, dst_col + n * 512:dst_col + (n + 1) * 512],
                    SC[:, 0:512], SC[:, 512:1024], ALU.add))
            if store_to is not None:
                P.wait("sync", V, P.cur(V))
                dma("sync", store_ch, store_to, R2[:, dst_col:dst_col + T])
            return P.cur(V)

        for h in range(NH):
            proj_rot_xpos(wk, h, A_K, store_ch="kx" + tag,
                          store_to=kxf[h * 128:(h + 1) * 128, :])
            for t in range(TC):
                pt = PTB[t % 2]
                P.wait("tensor", V, P.cur(V))
                t_t = cop("tensor", TT_, lambda e, t=t, pt=pt: e.transpose(
                    pt, R2[:, A_K + t * 128:A_K + (t + 1) * 128], IDENT[:]))
                P.wait("vector", TT_, t_t)
                if t >= 2:
                    P.wait("vector", "ks" + tag, P.cur("ks" + tag))
                cop("vector", V, lambda e, t=t, pt=pt, h=h: e.tensor_scalar(
                    R2[:, A_KS + t * 128:A_KS + (t + 1) * 128], pt,
                    GKLC[:, h:h + 1], None, ALU.mult))
            P.wait("sync", V, P.cur(V))
            ks_st = dma("sync", "ks" + tag,
                        _ap(kst, h * 128, [[D, 128], [128 * D, TC], [1, 128]]),
                        R2[:, A_KS:A_KS + T])
            P.wait("gpsimd", TT_, P.cur(TT_))
            lw = dma("gpsimd", "wsl", WS3[:, 0:2048],
                     _ap(wv, h * 128, [[D, 128], [128 * D, KC], [1, 128]]))
            P.wait("tensor", "wsl", lw)
            P.wait("tensor", V, P.cur(V))
            def mmV(e):
                last = None
                for k in range(KC):
                    for n in range(4):
                        last = e.matmul(
                            psa(n), WS3[:, k * 128:(k + 1) * 128],
                            R1[:, k * T + n * 512:k * T + n * 512 + 512],
                            start=(k == 0), stop=(k == KC - 1))
                return last
            t_pv = cop("tensor", TT_, mmV)
            for n in range(4):
                P.wait("vector", TT_, t_pv)
                P.wait("vector", "kx" + tag, P.cur("kx" + tag))
                cop("vector", V, lambda e, n=n: e.tensor_copy(
                    R2[:, A_K + n * 512:A_K + (n + 1) * 512], psa(n)))
            for t in range(TC):
                pt = PTB[t % 2]
                P.wait("tensor", V, P.cur(V))
                t_t = cop("tensor", TT_, lambda e, t=t, pt=pt: e.transpose(
                    pt, R2[:, A_K + t * 128:A_K + (t + 1) * 128], IDENT[:]))
                P.wait("vector", TT_, t_t)
                if t >= 2:
                    P.wait("vector", "vs" + tag, P.cur("vs" + tag))
                cop("vector", V, lambda e, t=t, pt=pt: e.tensor_copy(
                    R2[:, A_VS + t * 128:A_VS + (t + 1) * 128], pt))
            P.wait("sync", V, P.cur(V))
            dma("sync", "vs" + tag,
                _ap(vst, h * 128, [[D, 128], [128 * D, TC], [1, 128]]),
                R2[:, A_VS:A_VS + T])
            cop("vector", V, lambda e: e.memset(S_F[:], 0.0))
            for t in range(TC):
                P.wait("tensor", V, P.cur(V))
                t_c = cop("tensor", TT_, lambda e, t=t: e.matmul(
                    PSB[0:128, 0:128],
                    R2[:, A_KS + t * 128:A_KS + (t + 1) * 128],
                    R2[:, A_VS + t * 128:A_VS + (t + 1) * 128],
                    start=True, stop=True))
                P.wait("vector", TT_, t_c)
                cop("vector", V, lambda e, h=h: e.scalar_tensor_tensor(
                    S_F[:], S_F[:], ccol(GC[h]), PSB[0:128, 0:128],
                    ALU.mult, ALU.add))
            P.wait("sync", V, P.cur(V))
            dma("sync", "st" + tag, st_in[h * 128:(h + 1) * 128, :], S_F[:])
        P.wait("gpsimd", "st" + tag, P.cur("st" + tag))
        if stopped("asum"):
            return None
        cc_st = coll("cs" + tag, "AllGather", [st_in], [st_out])

        P.wait("gpsimd", V, P.cur(V))
        dma("gpsimd", "tb2" + tag, R2[:, A_T1:A_T1 + T], cq_t)
        dma("gpsimd", "tb2" + tag, R2[:, A_T2:A_T2 + T], sq_t)
        P.wait("vector", "tb2" + tag, P.cur("tb2" + tag))

        P.wait("vector", "cs" + tag, cc_st)
        for h in range(NH):
            proj_rot_xpos(wq, h, A_Q)
            P.wait("sync", TT_, P.cur(TT_))
            ldk = dma("sync", "mk" + tag, R2[:, A_K:A_K + T],
                      kxf[h * 128:(h + 1) * 128, :])
            dma("sync", "mk" + tag, R2[:, A_KS:A_KS + T],
                _ap(kst, h * 128, [[D, 128], [128 * D, TC], [1, 128]]))
            dma("sync", "mk" + tag, R2[:, A_VS:A_VS + T],
                _ap(vst, h * 128, [[D, 128], [128 * D, TC], [1, 128]]))
            cop("vector", V, lambda e: e.memset(S_F[:], 0.0))
            for kk in range(NCORES):
                P.wait("scalar", V, P.cur(V))
                ldm = dma("scalar", "ms" + tag, MTILE[:],
                          st_out[kk * NH * 128 + h * 128:
                                 kk * NH * 128 + (h + 1) * 128, :])
                P.wait("vector", "ms" + tag, ldm)
                cop("vector", V, lambda e, h=h, kk=kk: e.scalar_tensor_tensor(
                    S_F[:], MTILE[:], WSPAN[:, 8 * h + kk:8 * h + kk + 1],
                    S_F[:], ALU.mult, ALU.add))
            t_sb = cop("vector", V, lambda e: e.tensor_copy(S_B[:], S_F[:]))
            P.wait("tensor", "mk" + tag, P.cur("mk" + tag))
            yev = {}
            for t in range(TC):
                bk = t % 2
                ybank = (t // 4) % 2
                P.wait("tensor", V, t_sb)
                t_a = cop("tensor", TT_, lambda e, t=t, bk=bk: e.matmul(
                    PSB[0:128, bk * 512:bk * 512 + 128],
                    R2[:, A_K + t * 128:A_K + (t + 1) * 128],
                    R2[:, A_Q + t * 128:A_Q + (t + 1) * 128],
                    start=True, stop=True))
                P.wait("vector", TT_, t_a)
                t_ar = cop("vector", V, lambda e, bk=bk, h=h:
                           e.scalar_tensor_tensor(
                               ARES[:], PSB[0:128, bk * 512:bk * 512 + 128],
                               GINV[:, h:h + 1], MASKT[:], ALU.mult, ALU.mult))
                P.wait("tensor", V, t_ar)
                if t % 4 == 0 and t >= 8:
                    P.wait("tensor", V, yev[ybank])
                def mmy(e, t=t, ybank=ybank):
                    o = PSA[:, ybank * 512 + (t % 4) * 128:
                            ybank * 512 + (t % 4) * 128 + 128]
                    e.matmul(o, ARES[:],
                             R2[:, A_VS + t * 128:A_VS + (t + 1) * 128],
                             start=True, stop=False)
                    return e.matmul(o, R2[:, A_Q + t * 128:A_Q + (t + 1) * 128],
                                    S_B[:], start=False, stop=True)
                t_y = cop("tensor", TT_, mmy)
                t_c = cop("tensor", TT_, lambda e, t=t, bk=bk: e.matmul(
                    PSB[0:128, bk * 512 + 128:bk * 512 + 256],
                    R2[:, A_KS + t * 128:A_KS + (t + 1) * 128],
                    R2[:, A_VS + t * 128:A_VS + (t + 1) * 128],
                    start=True, stop=True))
                P.wait("vector", TT_, t_c)
                cop("vector", V, lambda e, h=h, bk=bk: e.scalar_tensor_tensor(
                    S_F[:], S_F[:], ccol(GC[h]),
                    PSB[0:128, bk * 512 + 128:bk * 512 + 256], ALU.mult,
                    ALU.add))
                t_sb = cop("vector", V, lambda e: e.tensor_copy(S_B[:], S_F[:]))
                if t % 4 == 3:
                    P.wait("vector", TT_, t_y)
                    q = t - 3
                    for dt_ in range(4):
                        tk = cop("vector", V, lambda e, q=q, dt_=dt_, h=h,
                                 ybank=ybank: e.tensor_scalar(
                                     R2[:, (q + dt_) * T + h * 128:
                                        (q + dt_) * T + (h + 1) * 128],
                                     PSA[:, ybank * 512 + dt_ * 128:
                                         ybank * 512 + dt_ * 128 + 128],
                                     GPOW[:, h:h + 1], None, ALU.mult))
                    yev[ybank] = tk
        heads_done = P.cur(V)
        if stopped("amain"):
            return None

        P.wait("scalar", V, heads_done)
        for tcn in range(TC):
            for h in range(NH):
                col = tcn * NH + h
                cop("vector", V, lambda e, tcn=tcn, h=h, col=col:
                    e.reduce_sum(GN_S[:, col:col + 1],
                                 R2[:, tcn * T + h * 128:
                                    tcn * T + (h + 1) * 128], axis=AXX),
                    fence=False)
                cop("scalar", SCL, lambda e, tcn=tcn, h=h, col=col:
                    e.activation(DUM[:, 0:128],
                                 R2[:, tcn * T + h * 128:tcn * T + (h + 1) * 128],
                                 AF.Square, accum_out=GN_S[:, 256 + col:257 + col]),
                    fence=False)
        P.wait("vector", SCL, P.cur(SCL))
        P.wait("vector", V, P.cur(V))
        cop("vector", V, lambda e: e.tensor_scalar(
            GN_S[:, 0:256], GN_S[:, 0:256], ccol(1.0 / 128), None, ALU.mult))
        cop("vector", V, lambda e: e.tensor_mul(
            SC[:, 0:256], GN_S[:, 0:256], GN_S[:, 0:256]))
        cop("vector", V, lambda e: e.scalar_tensor_tensor(
            GN_S[:, 256:512], GN_S[:, 256:512], ccol(1.0 / 128),
            SC[:, 0:256], ALU.mult, ALU.subtract))
        cop("vector", V, lambda e: e.tensor_scalar(
            GN_S[:, 256:512], GN_S[:, 256:512], ccol(EPS), None, ALU.add))
        t_rc = cop("vector", V, lambda e: e.reciprocal(SC[:, 256:512],
                                                       GN_S[:, 256:512]))
        P.wait("scalar", V, t_rc)
        t_rs = cop("scalar", SCL, lambda e: e.activation(
            SC[:, 256:512], SC[:, 256:512], AF.Sqrt))
        P.wait("vector", SCL, t_rs)
        for tcn in range(TC):
            for h in range(NH):
                col = tcn * NH + h
                cop("vector", V, lambda e, tcn=tcn, h=h, col=col:
                    e.tensor_scalar(
                        BT0[:, h * 128:(h + 1) * 128],
                        R2[:, tcn * T + h * 128:tcn * T + (h + 1) * 128],
                        GN_S[:, col:col + 1], SC[:, 256 + col:257 + col],
                        ALU.subtract, ALU.mult), fence=False)
            t_n = P.cur(V)
            P.wait("sync", V, t_n)
            st = dma("sync", "yn" + tag, ynr[tcn * 128:(tcn + 1) * 128, :],
                     BT0[:])
            P.wait("vector", "yn" + tag, st)
        yn_done = P.cur("yn" + tag)
        if stopped("agn"):
            return None

        P.wait("sync", "yn" + tag, yn_done)
        gw_tok = {}
        gmm_tok = {}
        P.wait("tensor", V, P.cur(V))
        for m in range(NH if GLIMIT is None else GLIMIT):
            if m >= 1:
                P.wait("sync", V, P.cur(V))
            tky = dma("sync", "yf" + tag, R2[:, A_K:A_K + T],
                      ynr[:, m * 128:(m + 1) * 128], transpose=True)
            if m >= 2:
                P.wait("gpsimd", TT_, gmm_tok[m - 1])
            lw = wslice(wgm, m, KC, 128, WS1, (m % 2) * 2048)
            P.wait("vector", "yf" + tag, tky)
            t_aff = cop("vector", V, lambda e, m=m: e.tensor_scalar(
                R2[:, A_K:A_K + T], R2[:, A_K:A_K + T], GNW[:, m:m + 1],
                GNB[:, m:m + 1], ALU.mult, ALU.add))
            P.wait("tensor", "wsl", lw)
            if m >= 1:
                P.wait("tensor", V, gw_tok[m - 1])
            def mmG(e, m=m):
                last = None
                for k in range(KC):
                    for n in range(4):
                        last = e.matmul(
                            psa(n), WS1[:, (m % 2) * 2048 + k * 128:
                                         (m % 2) * 2048 + (k + 1) * 128],
                            R1[:, k * T + n * 512:k * T + n * 512 + 512],
                            start=(k == 0), stop=(k == KC - 1))
                return last
            t_g = cop("tensor", TT_, mmG)
            gmm_tok[m] = t_g
            for n in range(4):
                P.wait("scalar", TT_, t_g)
                t_sil = cop("scalar", SCL, lambda e, n=n: e.activation(
                    DUM[:, n * 512:(n + 1) * 512], psa(n), AF.Silu))
                P.wait("vector", SCL, t_sil)
                P.wait("vector", V, t_aff)
                gw_tok[m] = cop("vector", V, lambda e, m=m, n=n: e.tensor_tensor(
                    R2[:, m * T + n * 512:m * T + (n + 1) * 512],
                    DUM[:, n * 512:(n + 1) * 512],
                    R2[:, A_K + n * 512:A_K + (n + 1) * 512], ALU.mult))
        gated_done = P.cur(V)
        if stopped("agt"):
            return None

        P.wait("gpsimd", TT_, P.cur(TT_))
        for k in range(KC):
            dma("gpsimd", "wo" + tag, R1[:, k * T:(k + 1) * T],
                _ap(wo, k * 128 * D, [[D, 128], [1, D]]))
        wol = P.cur("wo" + tag)
        P.wait("tensor", "wo" + tag, wol)
        P.wait("tensor", V, gated_done)
        wo_ev = {}
        for m in range(TC):
            par = m % 2
            banks = [psa(0), psa(1), psa(2), psa(3)] if par == 0 else \
                    [psb(0), psb(1), psa(0), psa(1)]
            if m >= 1:
                P.wait("tensor", "we" + tag, wo_ev[m - 1])
            def mmO(e, m=m, banks=banks):
                last = None
                for k in range(KC):
                    for n in range(4):
                        last = e.matmul(
                            banks[n],
                            R2[:, k * T + m * 128:k * T + (m + 1) * 128],
                            R1[:, k * T + n * 512:k * T + n * 512 + 512],
                            start=(k == 0), stop=(k == KC - 1))
                return last
            t_m = cop("tensor", "wm" + tag, mmO)
            if m >= 1:
                P.wait("sync", "xo" + tag, P.cur("xo" + tag))
            ldx = dma("sync", "xw" + tag, XT[:],
                      xin_rows[m * 128:(m + 1) * 128, :])
            P.wait("vector", "wm" + tag, t_m)
            P.wait("vector", "xw" + tag, ldx)
            for n in range(4):
                cop("vector", V, lambda e, n=n, banks=banks: e.tensor_tensor(
                    XT[:, n * 512:(n + 1) * 512], banks[n],
                    XT[:, n * 512:(n + 1) * 512], ALU.add))
            wo_ev[m] = cop("vector", "we" + tag, lambda e: e.nop())
            P.wait("sync", "we" + tag, wo_ev[m])
            dma("sync", "xo" + tag, xout_rows[m * 128:(m + 1) * 128, :], XT[:])
        return P.cur("xo" + tag)

    r1 = ffn_pass("p1", x0, xr1)
    fin = None
    if r1 is not None and STOP != "p1":
        P.barrier("after_p1")
        ra = attn_pass(xr1, xr2)
        if ra is not None and STOP != "at":
            P.barrier("after_at")
            fin = ffn_pass("p3", xr2, out_x3)
    if fin is None:
        P.barrier("trunc")
        for c in range(TC):
            dma("gpsimd", "xop3", out_x3[c * 128:(c + 1) * 128, :],
                x0[c * 128:(c + 1) * 128, :])
        fin = P.cur("xop3")

    if DBG:
        P.barrier("dbgbar")
        for name in DBG:
            for c in range(0, dramv[name].tensor.shape[0], 128):
                n = min(128, dramv[name].tensor.shape[0] - c)
                dma("gpsimd", "dbgc", dbg[name][c:c + n, :], dramv[name][c:c + n, :])
        P.wait("gpsimd", "dbgc", P.cur("dbgc"))

    P.wait("sync", "xo", fin)
    P.wait("gpsimd", "xo", fin)

    def _simulate():
        sems = defaultdict(int)
        pcs = {e: 0 for e in Prog.ENGS}
        progress = True
        while progress:
            progress = False
            for e in Prog.ENGS:
                tr = P.trace[e]
                while pcs[e] < len(tr):
                    kind, sem, val = tr[pcs[e]]
                    if kind == "w" and sems[sem] < val:
                        break
                    if kind == "i":
                        sems[sem] += val
                    pcs[e] += 1
                    progress = True
        stuck = {e: pcs[e] for e in Prog.ENGS if pcs[e] < len(P.trace[e])}
        if stuck:
            for e, pc in stuck.items():
                print(f"DEADLOCK {e} at {pc}/{len(P.trace[e])}: "
                      f"{P.trace[e][pc]}  sem={sems[P.trace[e][pc][1]]}")
            raise RuntimeError("deadlock in recorded schedule")
    _simulate()

    with es, nc.Block() as block:
        semctx = contextlib.ExitStack()
        sems = {n: semctx.enter_context(nc.semaphore(n))
                for n in sorted(P.semnames)}
        with semctx:
            def run(e, key):
                e._sems = sems
                if key == "gpsimd":
                    e._regs = {2047: e.to_reg(2047), 4095: e.to_reg(4095)}
                for f in P.ops[key]:
                    f(e)

            @block.sync
            def _(s):
                run(s, "sync")

            @block.scalar
            def _(s):
                run(s, "scalar")

            @block.vector
            def _(v):
                run(v, "vector")

            @block.tensor
            def _(t):
                run(t, "tensor")

            @block.gpsimd
            def _(g):
                run(g, "gpsimd")

    return nc


def _host_constants():
    perm = np.empty(DH, np.int64)
    perm[:64] = np.arange(64) * 2
    perm[64:] = np.arange(64) * 2 + 1
    c = {"perm": perm}
    c["triu_i"] = np.triu(np.ones((128, 128), np.float32))
    c["triu_e"] = np.triu(np.ones((128, 128), np.float32), 1)
    c["ident"] = np.eye(128, dtype=ml_dtypes.bfloat16)
    c["identf"] = np.eye(8, dtype=np.float32)
    R = np.zeros((128, 128), np.float32)
    for i in range(64):
        R[64 + i, i] = -1.0
        R[i, 64 + i] = 1.0
    c["rrot"] = R.astype(ml_dtypes.bfloat16)
    c["ones1"] = np.ones((1, 128), np.float32)
    c["onesc"] = np.ones((128, 1), np.float32)
    c["iota8m8"] = np.broadcast_to(np.arange(8) - 8.0, (128, 8)).astype(np.float32).copy()
    c["iotarc"] = (np.arange(128)[:, None] + 128.0 * np.arange(32)[None, :]).astype(np.float32)
    c["maskt"] = (np.arange(128)[None, :] >= np.arange(128)[:, None]).astype(ml_dtypes.bfloat16)
    g = 1.0 - np.exp(np.linspace(np.log(1 / 32), np.log(1 / 512), NH))
    c["gammas"] = g
    l = np.arange(128)[:, None].astype(np.float64)
    c["gpow"] = (g[None, :] ** l).astype(np.float32)
    c["gklc"] = (g[None, :] ** (128 - l)).astype(np.float32)
    c["ginv"] = (g[None, :] ** (-l)).astype(np.float32)
    return c


def _xpos_tables(span_start):
    d, half = DH, DH // 2
    pos = np.arange(span_start, span_start + T, dtype=np.float64)
    xscale = (np.arange(0, d, 2).astype(np.float64) + 0.4 * d) / (1.4 * d)
    scale = xscale[None, :] ** (pos[:, None] / 512.0)
    inv_freq = 1.0 / 10000.0 ** (np.arange(half).astype(np.float64) / half)
    sinu = pos[:, None] * inv_freq[None, :]
    sin, cos = np.sin(sinu), np.cos(sinu)

    def permfm(v):
        out = np.empty((128, T), np.float64)
        out[:64] = v.T
        out[64:] = v.T
        return out
    return (permfm(cos * scale).astype(np.float32),
            permfm(sin * scale).astype(np.float32),
            permfm(cos / scale).astype(np.float32),
            permfm(sin / scale).astype(np.float32))


def _prep_inputs(inp, consts_vals):
    hc = _host_constants()
    perm = hc["perm"]
    X = np.asarray(inp["L0_attn_X"], np.float32).reshape(-1, D)
    WQ = np.asarray(inp["L0_attn_W_Q"], np.float32)[:, :, perm].reshape(NH * D, DH)
    WK = np.asarray(inp["L0_attn_W_K"], np.float32)[:, :, perm].reshape(NH * D, DH)
    WV = np.transpose(np.asarray(inp["L0_attn_W_V"], np.float32), (1, 0, 2)).reshape(D, D)
    gnw_c = np.asarray(inp["L0_attn_gn_weight"], np.float32).reshape(NH, 128).T.copy()
    gnb_c = np.asarray(inp["L0_attn_gn_bias"], np.float32).reshape(NH, 128).T.copy()
    Wg = np.asarray(inp["L0_ffn_Wg"], np.float32)
    wgate = Wg.reshape(KC, 128, 8).transpose(1, 0, 2).reshape(128, 128).copy()
    consts_arr = np.zeros((128, 64), np.float32)
    for i, v in enumerate(consts_vals):
        consts_arr[:, i] = v
    gam = hc["gammas"]
    L = float(T)

    in_maps = []
    for j in range(NCORES):
        s, b = j % 4, j // 4
        cq, sq, ck, sk = _xpos_tables(s * T)
        jone = np.zeros((128, 1), np.float32)
        jone[16 * j, 0] = 1.0
        wspan = np.zeros((128, 128), np.float32)
        for h in range(NH):
            for k in range(NCORES):
                kb, ks = k // 4, k % 4
                if kb == b and ks < s:
                    wspan[:, 8 * h + k] = gam[h] ** (L * (s - 1 - ks))
        m = dict(
            x0=np.ascontiguousarray(X[j * T:(j + 1) * T]),
            wq=WQ, wk=WK, wv=WV,
            wgm=np.asarray(inp["L0_attn_W_G"], np.float32),
            wo=np.asarray(inp["L0_attn_W_O"], np.float32),
            gnw=gnw_c, gnb=gnb_c, wgate=wgate,
            w1=np.ascontiguousarray(np.asarray(inp["L0_ffn_W1"], np.float32)[j]),
            w3=np.ascontiguousarray(np.asarray(inp["L0_ffn_W3"], np.float32)[j]),
            w2=np.ascontiguousarray(np.asarray(inp["L0_ffn_W2"], np.float32)[j]),
            sw1=np.asarray(inp["L0_ffn_sw1"], np.float32),
            sw3=np.asarray(inp["L0_ffn_sw3"], np.float32),
            sw2=np.asarray(inp["L0_ffn_sw2"], np.float32),
            cq=cq, sq=sq, ck=ck, sk=sk,
            gpow=hc["gpow"], gklc=hc["gklc"], ginv=hc["ginv"],
            maskt=hc["maskt"], triu_i=hc["triu_i"], triu_e=hc["triu_e"],
            ident=hc["ident"], identf=hc["identf"], rrot=hc["rrot"],
            ones1=hc["ones1"], onesc=hc["onesc"], iota8m8=hc["iota8m8"],
            iotarc=hc["iotarc"], jone=jone, wspan=wspan, constsin=consts_arr,
        )
        in_maps.append(m)
    return in_maps


def kernel(**inputs):
    consts_vals = []
    with allow_ici_all_to_all():
        nc = build(consts_vals)
        in_maps = _prep_inputs(inputs, consts_vals)
        res = run_bass_kernel_spmd(nc, in_maps, core_ids=list(range(NCORES)))
    out = np.concatenate([np.asarray(r["out"], np.float32) for r in res.results], 0)
    return out.reshape(2, 8192, D)


# version 06
# speedup vs baseline: 1.0132x; 1.0132x over previous
import contextlib
from collections import defaultdict

import numpy as np
import ml_dtypes

import concourse.bass as bass
import concourse.mybir as mybir
from concourse.bass import AP, IndirectOffsetOnAxis
from concourse.bass_utils import run_bass_kernel_spmd
from concourse.aot_env import allow_ici_all_to_all

F32 = mybir.dt.float32
BF16 = mybir.dt.bfloat16
I32 = mybir.dt.int32
AF = mybir.ActivationFunctionType
ALU = mybir.AluOpType
AXX = mybir.AxisListType.X

NCORES = 8
D = 2048
HID = 2816
T = 2048
NH, DH = 16, 128
KC = D // 128
MC = HID // 128
TC = T // 128
CAP = 2048
BLK = 512
SENT = 30000.0
EPS = 1e-5

DBG = []
STOP = None
GLIMIT = None
STOP_AFTER = None


class Prog:
    ENGS = ("sync", "scalar", "vector", "tensor", "gpsimd")

    def __init__(self):
        self.ops = defaultdict(list)
        self.semval = defaultdict(int)
        self.lastwait = {}
        self.semnames = set()
        self.trace = defaultdict(list)

    @staticmethod
    def canon(name):
        for suf in ("p1", "p3", "at"):
            if name.endswith(suf) and len(name) > len(suf):
                return name[:-len(suf)]
        return name

    def emit(self, eng, fn):
        self.ops[eng].append(fn)

    def inc(self, name, n=1):
        name = self.canon(name)
        self.semnames.add(name)
        self.semval[name] += n
        return self.semval[name]

    def cur(self, name):
        name = self.canon(name)
        self.semnames.add(name)
        return self.semval[name]

    def wait(self, eng, name, val=None):
        name = self.canon(name)
        if val is None:
            val = self.semval[name]
        if val <= 0:
            return
        key = (eng, name)
        if self.lastwait.get(key, -1) >= val:
            return
        self.lastwait[key] = val
        self.semnames.add(name)

        def f(e, _n=name, _v=val):
            e.wait_ge(e._sems[_n], _v)
        self.emit(eng, f)
        self.trace[eng].append(("w", name, val))

    def barrier(self, tag):
        toks = {}
        for eng in self.ENGS:
            ch = "bar_" + eng
            toks[eng] = self.inc(ch, 1)
            def f(e, ch=ch):
                e.nop().then_inc(e._sems[ch], 1)
            self.emit(eng, f)
            self.trace[eng].append(("i", ch, 1))
        for eng in self.ENGS:
            for src in self.ENGS:
                if src != eng:
                    self.wait(eng, "bar_" + src, toks[src])


def _ap(t, offset, dims):
    return AP(t.tensor, offset, [list(d) for d in dims])


def build(consts_vals):
    nc = bass.Bass("TRN2", target_bir_lowering=False, debug=False)
    P = Prog()

    def ein(name, shape, dt=F32):
        return nc.dram_tensor(name, list(shape), dt, kind="ExternalInput").ap()

    x0 = ein("x0", (T, D))
    wq = ein("wq", (NH * D, DH))
    wk = ein("wk", (NH * D, DH))
    wv = ein("wv", (D, D))
    wgm = ein("wgm", (D, D))
    wo = ein("wo", (D, D))
    gnw_i = ein("gnw", (128, NH))
    gnb_i = ein("gnb", (128, NH))
    wgate = ein("wgate", (128, 128))
    w1 = ein("w1", (D, HID))
    w3 = ein("w3", (D, HID))
    w2 = ein("w2", (HID, D))
    sw1 = ein("sw1", (D, HID))
    sw3 = ein("sw3", (D, HID))
    sw2 = ein("sw2", (HID, D))
    cq_t = ein("cq", (128, T))
    sq_t = ein("sq", (128, T))
    ck_t = ein("ck", (128, T))
    sk_t = ein("sk", (128, T))
    gpow_i = ein("gpow", (128, NH))
    gklc_i = ein("gklc", (128, NH))
    ginv_i = ein("ginv", (128, NH))
    maskt_i = ein("maskt", (128, 128), BF16)
    triu_ii = ein("triu_i", (128, 128))
    triu_ei = ein("triu_e", (128, 128))
    ident_i = ein("ident", (128, 128), BF16)
    identf_i = ein("identf", (8, 8))
    rrot_i = ein("rrot", (128, 128), BF16)
    ones1_i = ein("ones1", (1, 128))
    onesc_i = ein("onesc", (128, 1))
    iota8_i = ein("iota8m8", (128, 8))
    iotarc_i = ein("iotarc", (128, 32))
    jone_i = ein("jone", (128, 1))
    wspan_i = ein("wspan", (128, 128))
    consts_i = ein("constsin", (128, 64))

    out_x3 = nc.dram_tensor("out", [T, D], F32, kind="ExternalOutput").ap()

    def din(name, shape, dt):
        return nc.dram_tensor(name, list(shape), dt, kind="Internal").ap()

    xr1 = din("xr1", (T, D), F32)
    xr2 = din("xr2", (T, D), F32)
    xlnr = din("xlnr", (T, D), BF16)
    xlolo = din("xlolo", (T, D), BF16)
    lg_loc = din("lg_loc", (T, 8), F32)
    lg_out = din("lg_out", (NCORES * T, 8), F32)
    tot_rt = din("tot_rt", (128, 8), F32)
    a2a_s = din("a2a_s", (NCORES * BLK, D), BF16)
    a2a_r = din("a2a_r", (NCORES * BLK, D), BF16)
    mt_s = din("mt_s", (NCORES * BLK, 8), F32)
    mt_r = din("mt_r", (NCORES * BLK, 8), F32)
    disp = din("disp", (CAP, D), BF16)
    backp = din("backp", (CAP, 8), F32)
    shr = din("shr", (T, D), BF16)
    b2a_s = din("b2a_s", (NCORES * BLK, D), BF16)
    b2a_r = din("b2a_r", (NCORES * BLK, D), BF16)
    kxf = din("kxf", (D, T), BF16)
    kst = din("kst", (T, D), BF16)
    vst = din("vst", (T, D), BF16)
    st_in = din("st_in", (NH * 128, 128), F32)
    st_out = din("st_out", (NCORES * NH * 128, 128), F32)
    st_in_b = din("st_in_b", (NH * 64, 128), F32)
    st_out_b = din("st_out_b", (NCORES * NH * 64, 128), F32)
    ynr = din("ynr", (T, D), BF16)
    dramv = dict(xr1=xr1, xr2=xr2, xlnr=xlnr, xlolo=xlolo, lg_loc=lg_loc,
                 lg_out=lg_out, tot_rt=tot_rt, a2a_s=a2a_s, a2a_r=a2a_r,
                 mt_s=mt_s, mt_r=mt_r, disp=disp, backp=backp, shr=shr,
                 b2a_s=b2a_s, b2a_r=b2a_r, kxf=kxf, kst=kst, vst=vst,
                 st_in=st_in, st_out=st_out, ynr=ynr)
    dbg = {}
    for name in DBG:
        src = dramv[name]
        dbg[name] = nc.dram_tensor("dbg_" + name, list(src.tensor.shape),
                                   src.tensor.dtype, kind="ExternalOutput").ap()

    es = contextlib.ExitStack()

    def sbt(name, shape, dt):
        return es.enter_context(nc.sbuf_tensor(name, list(shape), dt))

    R1 = sbt("R1", (128, 32768), BF16)
    R2 = sbt("R2", (128, 45056), BF16)
    WS1 = sbt("WS1", (128, 4096), BF16)
    WS3 = sbt("WS3", (128, 4096), BF16)
    DUM = sbt("DUM", (128, 2048), BF16)
    XT = sbt("XT", (128, 2048), F32)
    BT0 = sbt("BT0", (128, 2048), BF16)
    BT1 = sbt("BT1", (128, 2048), BF16)
    SC = sbt("SCR", (128, 2048), F32)
    GN_S = sbt("GN_S", (128, 512), F32)
    CONSTS = sbt("CONSTS", (128, 64), F32)
    STAT = sbt("STAT", (128, 64), F32)
    EQ = sbt("EQ", (128, 32), F32)
    OHB = sbt("OHB", (128, 1024), BF16)
    ONESC_B = sbt("ONESC_Bs", (128, 1), BF16)
    LGT = sbt("LGT", (128, 16), F32)
    TOTS = sbt("TOTS", (128, 8), F32)
    OFFS = sbt("OFFS", (128, 8), F32)
    COFF = sbt("COFF", (128, 8), F32)
    LOC = sbt("LOC", (1, 32), F32)
    LSLOT = sbt("LSLOT", (128, 16), F32)
    LSEND = sbt("LSEND", (128, 16), F32)
    LGK = sbt("LGK", (128, 16), F32)
    MR = sbt("MR", (128, 8), F32)
    BP8 = sbt("BP8", (128, 8), F32)
    IDXS = [sbt(f"IDXS{i}", (128, 1), I32) for i in range(4)]
    OOBT = sbt("OOBT", (128, 128), F32)
    WGH = sbt("WGH", (128, 128), BF16)
    WGL = sbt("WGL", (128, 128), BF16)
    TRIU_I = sbt("TRIU_Is", (128, 128), F32)
    TRIU_E = sbt("TRIU_Es", (128, 128), F32)
    IOTA8 = sbt("IOTA8s", (128, 8), F32)
    IOTARC = sbt("IOTARCs", (128, 32), F32)
    IDENT = sbt("IDENTs", (128, 128), BF16)
    IDENTF = sbt("IDENTFs", (8, 8), F32)
    RROT = sbt("RROTs", (128, 128), BF16)
    MASKT = sbt("MASKTs", (128, 128), BF16)
    GPOW = sbt("GPOWs", (128, NH), F32)
    GKLC = sbt("GKLCs", (128, NH), F32)
    GINV = sbt("GINVs", (128, NH), F32)
    GNW = sbt("GNWs", (128, NH), F32)
    GNB = sbt("GNBs", (128, NH), F32)
    JONE = sbt("JONEs", (128, 1), F32)
    WSPAN = sbt("WSPANs", (128, 128), F32)
    ONES1 = sbt("ONES1s", (1, 128), F32)
    ONESC = sbt("ONESCs", (128, 1), F32)
    ARES = sbt("ARES", (128, 128), BF16)
    ARES2 = sbt("ARES2", (128, 128), BF16)
    S_F = sbt("S_F", (128, 128), F32)
    S_B = sbt("S_B", (128, 128), BF16)
    S_B2 = sbt("S_B2", (128, 128), BF16)
    MTILE = sbt("MTILE", (128, 128), F32)

    A_Q, A_K, A_KS, A_VS, A_T1, A_T2 = 32768, 34816, 36864, 38912, 40960, 43008

    PSA = es.enter_context(nc.psum_tensor("PSA", [128, 2048], F32))
    PSB = es.enter_context(nc.psum_tensor("PSB", [128, 1024], F32))
    PTB_full = [es.enter_context(nc.psum_tensor(f"PTB{i}", [128, 1024], BF16))
                for i in range(2)]
    PTB = [t[:, 0:128] for t in PTB_full]

    def psa(i):
        return PSA[:, i * 512:(i + 1) * 512]

    def psb(i):
        return PSB[:, i * 512:(i + 1) * 512]

    cmap = {}

    def ccol(v):
        v = float(v)
        if v not in cmap:
            assert len(consts_vals) < 64
            cmap[v] = len(consts_vals)
            consts_vals.append(v)
        i = cmap[v]
        return CONSTS[:, i:i + 1]

    gammas = 1.0 - np.exp(np.linspace(np.log(1 / 32), np.log(1 / 512), NH))
    GC = [float(g ** 128) for g in gammas]

    def dma(q, sem, out, in_, transpose=False):
        sem = P.canon(sem)
        tok = P.inc(sem, 16)
        P.trace[q].append(("i", sem, 16))
        def f(e, out=out, in_=in_, tp=transpose, sem=sem):
            if tp:
                e.dma_start(out=out, in_=in_, transpose=True).then_inc(e._sems[sem], 16)
            else:
                e.dma_start(out=out, in_=in_).then_inc(e._sems[sem], 16)
        P.emit(q, f)
        return tok

    def icall(sem, out, out_off, in_, in_off, bound, elem_off=0):
        sem = P.canon(sem)
        tok = P.inc(sem, 16)
        P.trace["gpsimd"].append(("i", sem, 16))
        def f(e, out=out, oo=out_off, in_=in_, io=in_off, b=bound, eo=elem_off,
              sem=sem):
            e.indirect_dma_start(out, oo, in_, io, element_offset=eo,
                                 bounds_check=e._regs[b],
                                 oob_is_err=False).then_inc(e._sems[sem], 16)
        P.emit("gpsimd", f)
        return tok

    def cop(eng, sem, fn, n=1, fence=True):
        sem = P.canon(sem)
        tok = P.inc(sem, n)
        P.trace[eng].append(("i", sem, n))
        def f(e, fn=fn, sem=sem, n=n):
            fn(e).then_inc(e._sems[sem], n)
        P.emit(eng, f)
        if fence and eng in ("vector", "gpsimd", "scalar"):
            P.wait(eng, sem, tok)
        return tok

    def coll(sem, kind, ins, outs):
        sem = P.canon(sem)
        tok = P.inc(sem, 1)
        P.trace["gpsimd"].append(("i", sem, 1))
        def f(e, kind=kind, ins=ins, outs=outs, sem=sem):
            e.collective_compute(kind, ALU.bypass,
                                 replica_groups=[list(range(NCORES))],
                                 ins=ins, outs=outs).then_inc(e._sems[sem], 1)
        P.emit("gpsimd", f)
        return tok

    def wslice(w, m, kchunks, width, dst, dstcol, ncol=None):
        if ncol is None:
            ncol = w.tensor.shape[1]
        src = _ap(w, m * width, [[ncol, 128], [128 * ncol, kchunks], [1, width]])
        return dma("gpsimd", "wsl", dst[:, dstcol:dstcol + kchunks * width], src)

    for dst, src in [
        (CONSTS[:], consts_i), (ONES1[:], ones1_i), (ONESC[:], onesc_i),
        (TRIU_I[:], triu_ii), (TRIU_E[:], triu_ei), (IOTA8[:], iota8_i),
        (IOTARC[:], iotarc_i), (IDENT[:], ident_i), (IDENTF[:], identf_i),
        (RROT[:], rrot_i), (MASKT[:], maskt_i),
        (GPOW[:], gpow_i), (GKLC[:], gklc_i), (GINV[:], ginv_i),
        (GNW[:], gnw_i), (GNB[:], gnb_i), (JONE[:], jone_i),
        (WSPAN[:], wspan_i), (SC[:, 0:128].with_dtype(F32) if False else SC[:, 0:128], wgate),
    ]:
        dma("gpsimd", "boot", dst, src)
    P.wait("vector", "boot")
    cop("vector", "vb", lambda e: e.tensor_copy(WGH[:], SC[:, 0:128]))
    cop("vector", "vb", lambda e: e.tensor_copy(SC[:, 128:256], WGH[:]))
    cop("vector", "vb", lambda e: e.tensor_sub(SC[:, 256:384], SC[:, 0:128], SC[:, 128:256]))
    cop("vector", "vb", lambda e: e.tensor_copy(WGL[:], SC[:, 256:384]))
    cop("vector", "vb", lambda e: e.tensor_copy(ONESC_B[:], ONESC[:]))
    cop("vector", "vb", lambda e: e.memset(OOBT[:], SENT))
    cop("vector", "vb", lambda e: e.memset(BT1[:], 0.0))
    P.wait("gpsimd", "vb")
    for buf, nrow in ((a2a_s, NCORES * BLK), (a2a_r, NCORES * BLK),
                      (b2a_s, NCORES * BLK), (b2a_r, NCORES * BLK), (disp, CAP)):
        for c in range(0, nrow, 128):
            dma("gpsimd", "zero", buf[c:c + 128, :], BT1[:])
    P.barrier("boot")

    def ffn_pass(tag, xin_rows, xout_rows):
        trdone = {}

        def stopped(name):
            return STOP is not None and STOP == name

        V, SCL, TT_, GQ = "v" + tag, "s" + tag, "t" + tag, "g" + tag

        dma("gpsimd", "fl" + tag, _ap(mt_s, 0, [[128, 128], [1, 128]]), OOBT[:])
        dma("gpsimd", "fl" + tag, _ap(mt_s, 128 * 128, [[128, 128], [1, 128]]), OOBT[:])
        dma("gpsimd", "fl" + tag, _ap(backp, 0, [[128, 128], [1, 128]]),
            OOBT[:])

        xt_free = {}
        for c in range(TC):
            cs, cq, cv2, cr = c % 16, 16 + c % 16, 32 + c % 16, 48 + c % 16
            if c >= 1:
                P.wait("sync", V, xt_free[c - 1][0])
                P.wait("sync", SCL, xt_free[c - 1][1])
            ld = dma("sync", "xl" + tag, XT[:], xin_rows[c * 128:(c + 1) * 128, :])
            P.wait("vector", "xl" + tag, ld)
            cop("vector", V, lambda e, cs=cs: e.reduce_sum(
                STAT[:, cs:cs + 1], XT[:], axis=AXX), fence=False)
            P.wait("scalar", "xl" + tag, ld)
            t_sq = cop("scalar", SCL, lambda e, cq=cq: e.activation(
                DUM[:], XT[:], AF.Square, accum_out=STAT[:, cq:cq + 1]))
            if c >= 1:
                P.wait("vector", "xs" + tag, P.cur("xs" + tag) - 32)
            t_xc = cop("vector", V, lambda e: e.tensor_copy(SC[:], XT[:]),
                       fence=False)
            xt_free[c] = (t_xc, t_sq)
            P.wait("vector", SCL, t_sq)
            cop("vector", V, lambda e, cs=cs: e.tensor_scalar(
                STAT[:, cs:cs + 1], STAT[:, cs:cs + 1], ccol(1.0 / D), None,
                ALU.mult))
            cop("vector", V, lambda e, cs=cs, cv2=cv2: e.tensor_mul(
                STAT[:, cv2:cv2 + 1], STAT[:, cs:cs + 1], STAT[:, cs:cs + 1]))
            cop("vector", V, lambda e, cq=cq, cv2=cv2: e.scalar_tensor_tensor(
                STAT[:, cv2:cv2 + 1], STAT[:, cq:cq + 1], ccol(1.0 / D),
                STAT[:, cv2:cv2 + 1], ALU.mult, ALU.subtract))
            cop("vector", V, lambda e, cv2=cv2: e.tensor_scalar(
                STAT[:, cv2:cv2 + 1], STAT[:, cv2:cv2 + 1], ccol(EPS), None,
                ALU.add))
            t_rc = cop("vector", V, lambda e, cv2=cv2, cr=cr: e.reciprocal(
                STAT[:, cr:cr + 1], STAT[:, cv2:cv2 + 1]))
            P.wait("scalar", V, t_rc)
            t_rs = cop("scalar", SCL, lambda e, cr=cr: e.activation(
                STAT[:, cr:cr + 1], STAT[:, cr:cr + 1], AF.Sqrt))
            P.wait("vector", SCL, t_rs)
            cop("vector", V, lambda e, cs=cs, cr=cr: e.tensor_scalar(
                SC[:], SC[:], STAT[:, cs:cs + 1], STAT[:, cr:cr + 1],
                ALU.subtract, ALU.mult), fence=False)
            cop("vector", V, lambda e: e.tensor_copy(BT0[:], SC[:]), fence=False)
            t_n = cop("vector", V, lambda e: e.tensor_tensor(
                BT1[:], SC[:], BT0[:], ALU.subtract), fence=False)
            P.wait("sync", V, t_n)
            dma("sync", "xs" + tag, xlnr[c * 128:(c + 1) * 128, :], BT0[:])
            st = dma("sync", "xs" + tag, xlolo[c * 128:(c + 1) * 128, :], BT1[:])
        P.wait("vector", "xs" + tag, P.cur("xs" + tag))
        ln_done = P.cur("xs" + tag)
        if stopped("ln"):
            return None

        P.wait("sync", "xs" + tag, ln_done)
        P.wait("scalar", "xs" + tag, ln_done)
        for k in range(KC):
            dma("sync", "th" + tag, R1[:, k * T:(k + 1) * T],
                xlnr[:, k * 128:(k + 1) * 128], transpose=True)
        for k in range(KC):
            dma("sync", "tl" + tag, R2[:, k * T:(k + 1) * T],
                xlolo[:, k * 128:(k + 1) * 128], transpose=True)
        thi = P.cur("th" + tag)
        tlo = P.cur("tl" + tag)

        P.wait("tensor", "th" + tag, thi)
        P.wait("tensor", "tl" + tag, tlo)
        tp_ev = {}
        for c in range(TC):
            bk = c % 2
            if c >= 2:
                P.wait("tensor", V, tp_ev[c - 2])
            def rmm(e, c=c, bk=bk):
                o = PSB[0:128, bk * 512:bk * 512 + 8]
                last = None
                for src, wgt in ((R1, WGH), (R1, WGL), (R2, WGH)):
                    for k in range(KC):
                        first = (src is R1 and wgt is WGH and k == 0)
                        lastf = (src is R2 and k == KC - 1)
                        last = e.matmul(
                            o, src[:, k * T + c * 128:k * T + (c + 1) * 128],
                            wgt[:, k * 8:k * 8 + 8],
                            start=first, stop=lastf)
                return last
            t_mm = cop("tensor", TT_, rmm)
            P.wait("vector", TT_, t_mm)
            tp_ev[c] = cop("vector", V, lambda e, c=c, bk=bk: e.tensor_copy(
                LGT[:, 8 * bk:8 * bk + 8], PSB[0:128, bk * 512:bk * 512 + 8]))
            P.wait("sync", V, tp_ev[c])
            stg = dma("sync", "lg" + tag, lg_loc[c * 128:(c + 1) * 128, :],
                      LGT[:, 8 * bk:8 * bk + 8])
            P.wait("vector", "lg" + tag, stg - 16)
        P.wait("gpsimd", "lg" + tag, P.cur("lg" + tag))
        if stopped("rt"):
            return None
        cc_lg = coll("cl" + tag, "AllGather", [lg_loc], [lg_out])

        def h_phase(wA, wB, pref, first_bank_wait=None, per_m_hook=None):
            sil_tok = {}
            tt_tok = {}
            for m in range(MC):
                if per_m_hook is not None:
                    per_m_hook(m)
                buf = m % 2
                if m >= 2:
                    P.wait("gpsimd", pref + "tt", 4 * (m - 1))
                wslice(wA, m, KC, 128, WS1, buf * 2048)
                lB = wslice(wB, m, KC, 128, WS3, buf * 2048)
                P.wait("tensor", "wsl", lB)
                if m == 0 and first_bank_wait is not None:
                    first_bank_wait()
                if m >= 1:
                    for n in range(4):
                        P.wait("tensor", pref + "sl", sil_tok[(m - 1, n)])
                    P.wait("tensor", pref + "tt", tt_tok[(m - 1, 3)])
                def mmA(e, buf=buf):
                    last = None
                    for k in range(KC):
                        for n in range(4):
                            last = e.matmul(
                                psa(n),
                                WS1[:, buf * 2048 + k * 128:buf * 2048 + (k + 1) * 128],
                                R1[:, k * T + n * 512:k * T + n * 512 + 512],
                                start=(k == 0), stop=(k == KC - 1))
                    return last
                tA = cop("tensor", pref + "mm", mmA)
                for n in range(4):
                    P.wait("scalar", pref + "mm", tA)
                    sil_tok[(m, n)] = cop("scalar", pref + "sl",
                                          lambda e, n=n: e.activation(
                                              DUM[:, n * 512:(n + 1) * 512],
                                              psa(n), AF.Silu))
                P.wait("tensor", pref + "sl", sil_tok[(m, 1)])
                if m >= 1:
                    P.wait("tensor", pref + "tt", tt_tok[(m - 1, 1)])
                h3bank = [psb(0), psb(1), psa(0), psa(1)]
                def mmB(e, buf=buf, h3bank=h3bank):
                    last = None
                    for k in range(KC):
                        for n in range(4):
                            last = e.matmul(
                                h3bank[n],
                                WS3[:, buf * 2048 + k * 128:buf * 2048 + (k + 1) * 128],
                                R1[:, k * T + n * 512:k * T + n * 512 + 512],
                                start=(k == 0), stop=(k == KC - 1))
                    return last
                tB = cop("tensor", pref + "mm", mmB)
                for n in range(4):
                    P.wait("vector", pref + "mm", tB)
                    P.wait("vector", pref + "sl", sil_tok[(m, n)])
                    tt_tok[(m, n)] = cop("vector", pref + "tt", fence=False) if False else cop("vector", pref + "tt",
                                         lambda e, m=m, n=n, h3bank=h3bank:
                                         e.tensor_tensor(
                                             R2[:, m * T + n * 512:m * T + n * 512 + 512],
                                             DUM[:, n * 512:(n + 1) * 512],
                                             h3bank[n], ALU.mult))
            return tt_tok

        lread = {}
        f4a_state = {"started": False}

        def emit_f4a_chunk(g):
            if not f4a_state["started"]:
                P.wait("vector", "cl" + tag, cc_lg)
                P.wait("scalar", "lg" + tag, P.cur("lg" + tag))
                P.wait("scalar", "cl" + tag, cc_lg)
                f4a_state["started"] = True
            sl = 8 * (g % 2)
            if g >= 1:
                P.wait("scalar", V, lread[g - 1])
            ld = dma("scalar", "gl" + tag, LGT[:, sl:sl + 8],
                     lg_out[g * 128:(g + 1) * 128, :])
            P.wait("vector", "gl" + tag, ld)
            L = LGT[:, sl:sl + 8]
            O = OHB[:, 8 * g:8 * g + 8]
            cop("vector", V, lambda e, L=L: e.tensor_tensor(
                STAT[:, 28:32], L[:, 0:4], L[:, 4:8], ALU.max))
            cop("vector", V, lambda e: e.tensor_tensor(
                STAT[:, 28:30], STAT[:, 28:30], STAT[:, 30:32], ALU.max))
            cop("vector", V, lambda e: e.tensor_tensor(
                STAT[:, 26:27], STAT[:, 28:29], STAT[:, 29:30], ALU.max))
            lread[g] = cop("vector", V, lambda e, L=L, O=O: e.tensor_scalar(
                O, L, STAT[:, 26:27], None, ALU.is_equal))
            cop("vector", V, lambda e, O=O: e.tensor_mul(EQ[:, 16:24], O,
                                                         IOTA8[:]))
            cop("vector", V, lambda e: e.tensor_tensor(
                STAT[:, 28:32], EQ[:, 16:20], EQ[:, 20:24], ALU.min))
            cop("vector", V, lambda e: e.tensor_tensor(
                STAT[:, 28:30], STAT[:, 28:30], STAT[:, 30:32], ALU.min))
            cop("vector", V, lambda e: e.tensor_tensor(
                STAT[:, 27:28], STAT[:, 28:29], STAT[:, 29:30], ALU.min))
            cop("vector", V, lambda e, O=O: e.tensor_scalar(
                O, IOTA8[:], STAT[:, 27:28], None, ALU.is_equal))

        def f4a_hook(m):
            lo_g = m * 8
            hi_g = min(128, lo_g + 8)
            for g in range(lo_g, hi_g):
                emit_f4a_chunk(g)

        P.wait("vector", TT_, P.cur(TT_))
        h_phase(sw1, sw3, "sh" + tag, per_m_hook=f4a_hook)
        for g in range(8 * MC, 128):
            emit_f4a_chunk(g)
        f4a_done = P.cur(V)
        sh_tt_all = P.cur("sh" + tag + "tt")
        sh_mm_all = P.cur("sh" + tag + "mm")

        P.wait("tensor", V, f4a_done)
        P.wait("tensor", "sh" + tag + "tt", sh_tt_all)
        for g in range(128):
            cop("tensor", "gt" + tag, lambda e, g=g: e.matmul(
                PSB[0:1, 8 * (g % 64) + 512 * (g // 64):
                    8 * (g % 64) + 512 * (g // 64) + 8],
                ONESC_B[:], OHB[:, 8 * g:8 * g + 8],
                start=(g % 64 == 0), stop=(g % 64 == 63)))
        t_tot = P.cur("gt" + tag)
        P.wait("vector", "gt" + tag, t_tot)
        t_te = cop("vector", V, lambda e: e.tensor_copy(SC[0:1, 0:1024],
                                                        PSB[0:1, 0:1024]))
        P.wait("sync", V, t_te)
        tr1 = dma("sync", "tr" + tag, _ap(tot_rt, 0, [[1024, 1], [1, 1024]]),
                  SC[0:1, 0:1024])
        trdone["tok"] = tr1
        P.wait("sync", "tr" + tag, tr1)
        tr2 = dma("sync", "tr" + tag, TOTS[:], tot_rt)
        P.wait("tensor", "tr" + tag, tr2)
        P.wait("tensor", V, t_te)
        t_off = cop("tensor", "gt" + tag, lambda e: e.matmul(
            PSB[0:128, 0:8], TRIU_E[:], TOTS[:], start=True, stop=True))
        P.wait("vector", "gt" + tag, t_off)
        t_offe = cop("vector", V, lambda e: e.tensor_copy(OFFS[:],
                                                          PSB[0:128, 0:8]))
        P.wait("tensor", V, t_offe)
        t_co = cop("tensor", "gt" + tag, lambda e: e.matmul(
            PSB[0:1, 8:16], JONE[:], OFFS[:], start=True, stop=True))
        P.wait("vector", "gt" + tag, t_co)
        t_coe = cop("vector", V, lambda e: e.tensor_copy(LOC[0:1, 16:24],
                                                         PSB[0:1, 8:16]))
        P.wait("tensor", V, t_coe)
        t_cb = cop("tensor", "gt" + tag, lambda e: e.matmul(
            PSB[0:128, 8:16], ONES1[:], LOC[0:1, 16:24], start=True, stop=True))
        P.wait("vector", "gt" + tag, t_cb)
        t_coffe = cop("vector", V, lambda e: e.tensor_copy(COFF[:],
                                                           PSB[0:128, 8:16]))
        cop("vector", V, lambda e: e.memset(LOC[0:1, 0:8], 0.0))

        bpread = {}
        f5_tok = {}

        def emit_f5_chunk(lc):
            if lc == 0:
                P.wait("vector", "tr" + tag, trdone["tok"])
            if lc >= 1:
                P.wait("scalar", V, bpread[lc - 1])
            ld = dma("scalar", "ll" + tag, BP8[:],
                     lg_loc[lc * 128:(lc + 1) * 128, :])
            P.wait("vector", "ll" + tag, ld)
            cop("vector", V, lambda e: e.reduce_max(STAT[:, 8:9], BP8[:],
                                                    axis=AXX))
            cop("vector", V, lambda e: e.tensor_scalar(
                EQ[:, 24:32], BP8[:], STAT[:, 8:9], None, ALU.is_equal))
            cop("vector", V, lambda e: e.tensor_mul(SC[:, 16:24], EQ[:, 24:32],
                                                    IOTA8[:]))
            cop("vector", V, lambda e: e.tensor_reduce(
                STAT[:, 9:10], SC[:, 16:24], axis=AXX, op=ALU.min))
            t_oh = cop("vector", V, lambda e: e.tensor_scalar(
                EQ[:, 24:32], IOTA8[:], STAT[:, 9:10], None, ALU.is_equal))
            bpread[lc] = cop("vector", V, lambda e: e.tensor_scalar(
                SC[:, 0:8], BP8[:], STAT[:, 8:9], None, ALU.subtract))
            P.wait("scalar", V, P.cur(V))
            P.wait("scalar", "sh" + tag + "tt", min(4, sh_tt_all))
            t_e = cop("scalar", SCL, lambda e: e.activation(
                GN_S[:, 0:8], SC[:, 0:8], AF.Exp, accum_out=STAT[:, 10:11]))
            P.wait("vector", SCL, t_e)
            cop("vector", V, lambda e: e.reciprocal(STAT[:, 11:12],
                                                    STAT[:, 10:11]))
            P.wait("tensor", V, t_oh)
            if lc == 0:
                P.wait("tensor", V, t_coffe)
            bk = lc % 2
            t_pc = cop("tensor", "lt" + tag, lambda e, bk=bk: (
                e.matmul(PSB[0:1, 512 * bk + 24:512 * bk + 32], ONESC[:],
                         EQ[:, 24:32], start=True, stop=True),
                e.matmul(PSB[0:128, 512 * bk + 32:512 * bk + 40], TRIU_I[:],
                         EQ[:, 24:32], start=True, stop=False),
                e.matmul(PSB[0:128, 512 * bk + 32:512 * bk + 40], ONES1[:],
                         LOC[0:1, 0:8], start=False, stop=True))[-1])
            P.wait("vector", "lt" + tag, t_pc)
            cop("vector", V, lambda e, bk=bk: e.tensor_tensor(
                LOC[0:1, 0:8], LOC[0:1, 0:8],
                PSB[0:1, 512 * bk + 24:512 * bk + 32], ALU.add))
            cop("vector", V, lambda e, bk=bk: e.tensor_tensor(
                SC[:, 8:16], EQ[:, 24:32],
                PSB[0:128, 512 * bk + 32:512 * bk + 40], ALU.mult))
            cop("vector", V, lambda e: e.reduce_sum(STAT[:, 12:13], SC[:, 8:16],
                                                    axis=AXX))
            cop("vector", V, lambda e: e.tensor_mul(SC[:, 8:16], EQ[:, 24:32],
                                                    COFF[:]))
            cop("vector", V, lambda e: e.reduce_sum(STAT[:, 13:14], SC[:, 8:16],
                                                    axis=AXX))
            cop("vector", V, lambda e: e.tensor_add(STAT[:, 14:15],
                                                    STAT[:, 12:13],
                                                    STAT[:, 13:14]))
            cop("vector", V, lambda e: e.tensor_scalar(
                STAT[:, 15:16], STAT[:, 14:15], ccol(float(CAP)), None,
                ALU.is_le))
            cop("vector", V, lambda e: e.tensor_scalar(
                STAT[:, 17:18], STAT[:, 14:15], ccol(1.0), None, ALU.subtract))
            cop("vector", V, lambda e: e.scalar_tensor_tensor(
                STAT[:, 18:19], STAT[:, 17:18], ccol(SENT), STAT[:, 15:16],
                ALU.subtract, ALU.mult))
            cop("vector", V, lambda e, lc=lc: e.tensor_scalar(
                LSLOT[:, lc:lc + 1], STAT[:, 18:19], ccol(SENT), None, ALU.add))
            cop("vector", V, lambda e: e.tensor_scalar(
                STAT[:, 19:20], STAT[:, 9:10], ccol(512.0), None, ALU.mult))
            cop("vector", V, lambda e: e.scalar_tensor_tensor(
                STAT[:, 20:21], STAT[:, 12:13], ccol(4095.0), STAT[:, 19:20],
                ALU.add, ALU.add))
            cop("vector", V, lambda e: e.tensor_scalar(
                STAT[:, 21:22], STAT[:, 12:13], ccol(512.0), None, ALU.is_le))
            cop("vector", V, lambda e: e.scalar_tensor_tensor(
                STAT[:, 22:23], STAT[:, 20:21], ccol(SENT), STAT[:, 21:22],
                ALU.subtract, ALU.mult))
            cop("vector", V, lambda e, lc=lc: e.tensor_scalar(
                LSEND[:, lc:lc + 1], STAT[:, 22:23], ccol(SENT), None, ALU.add))
            f5_tok[lc] = cop("vector", V, lambda e, lc=lc: e.scalar_tensor_tensor(
                LGK[:, lc:lc + 1], STAT[:, 11:12], ccol(0.5), STAT[:, 15:16],
                ALU.mult, ALU.mult))

        if stopped("f5"):
            for lc in range(TC):
                emit_f5_chunk(lc)
            return None

        P.wait("gpsimd", "sh" + tag + "mm", sh_mm_all)
        P.wait("tensor", "sh" + tag + "tt", sh_tt_all)
        sw2_ev = {}
        shs_tok = {}
        s2m_base = P.cur("s2m" + tag)
        for half in range(2):
            if half == 1:
                P.wait("gpsimd", "s2m" + tag, s2m_base + TC)
            dma("gpsimd", "w2l" + tag, R1[:, 0:MC * 1024],
                _ap(sw2, half * 1024, [[D, 128], [128 * D, MC], [1, 1024]]))
            wl = P.cur("w2l" + tag)
            P.wait("tensor", "w2l" + tag, wl)
            for m in range(TC):
                par = m % 2
                if m >= 2:
                    P.wait("tensor", "s2e" + tag, sw2_ev[(half, m - 2)])
                elif half == 1:
                    P.wait("tensor", "s2e" + tag, sw2_ev[(0, TC - 2 + m)])
                def mm2(e, m=m, par=par):
                    last = None
                    for k in range(MC):
                        for n in range(2):
                            last = e.matmul(
                                psa(par * 2 + n),
                                R2[:, k * T + m * 128:k * T + (m + 1) * 128],
                                R1[:, k * 1024 + n * 512:k * 1024 + (n + 1) * 512],
                                start=(k == 0), stop=(k == MC - 1))
                    return last
                t_m = cop("tensor", "s2m" + tag, mm2)
                P.wait("vector", "s2m" + tag, t_m)
                if (half, m - 2) in shs_tok:
                    P.wait("vector", "shs" + tag, shs_tok[(half, m - 2)])
                elif half == 1 and (0, TC - 2 + m) in shs_tok:
                    P.wait("vector", "shs" + tag, shs_tok[(0, TC - 2 + m)])
                bt = BT0 if par == 0 else BT1
                cop("vector", V, lambda e, par=par, bt=bt: e.tensor_copy(
                    bt[:, 0:512], psa(par * 2)))
                t_ev = cop("vector", "s2e" + tag,
                           lambda e, par=par, bt=bt: e.tensor_copy(
                               bt[:, 512:1024], psa(par * 2 + 1)))
                sw2_ev[(half, m)] = t_ev
                P.wait("sync", "s2e" + tag, t_ev)
                shs_tok[(half, m)] = dma(
                    "sync", "shs" + tag,
                    _ap(shr, m * 128 * D + half * 1024, [[D, 128], [1, 1024]]),
                    bt[:, 0:1024])
                if half == 0:
                    emit_f5_chunk(m)

        if stopped("sh"):
            return None
        ds_tok = {}
        for lc in range(TC):
            ib = lc % 2
            buf = DUM[:] if ib == 0 else XT[:]
            P.wait("vector", V, f5_tok[lc])
            if lc >= 2:
                P.wait("vector", "ds" + tag, ds_tok[lc - 2])
                P.wait("scalar", "ds" + tag, ds_tok[lc - 2])
            t_ix = cop("vector", V, lambda e, lc=lc, ib=ib: e.tensor_copy(
                IDXS[ib][:], LSEND[:, lc:lc + 1]))
            lr = dma("scalar" if ib == 0 else "gpsimd", "dr" + tag, buf,
                     xlnr[lc * 128:(lc + 1) * 128, :])
            P.wait("gpsimd", V, t_ix)
            P.wait("gpsimd", "dr" + tag, lr)
            icall("ds" + tag, a2a_s, IndirectOffsetOnAxis(IDXS[ib][:], 0),
                  buf, None, 4095)
            ds_tok[lc] = icall("ds" + tag, mt_s,
                               IndirectOffsetOnAxis(IDXS[ib][:], 0),
                               LSLOT[:, lc:lc + 1], None, 4095)
        P.wait("gpsimd", "fl" + tag, P.cur("fl" + tag))
        P.wait("gpsimd", "ds" + tag, P.cur("ds" + tag))
        cc_a = coll("ca" + tag, "AllToAll", [a2a_s], [a2a_r])
        cc_a = coll("ca" + tag, "AllToAll", [mt_s], [mt_r])

        P.wait("scalar", "ca" + tag, cc_a)
        cp_tok = {}
        cp_ix = {}
        for rc in range(32):
            ib = 2 + rc % 2
            buf = DUM[:] if rc % 2 == 0 else XT[:]
            if rc >= 2:
                P.wait("scalar", "cp" + tag, cp_tok[rc - 2])
                P.wait("vector", "cp" + tag, cp_tok[rc - 2])
            if rc >= 1:
                P.wait("scalar", V, cp_ix[rc - 1])
            ldm = dma("scalar", "mr" + tag, MR[:],
                      mt_r[rc * 128:(rc + 1) * 128, :])
            ldr = dma("scalar" if rc % 2 == 0 else "gpsimd", "rr" + tag, buf,
                      a2a_r[rc * 128:(rc + 1) * 128, :])
            P.wait("vector", "mr" + tag, ldm)
            t_ix = cop("vector", V, lambda e, ib=ib: e.tensor_copy(
                IDXS[ib][:], MR[:, 0:1]))
            cp_ix[rc] = t_ix
            P.wait("gpsimd", V, t_ix)
            P.wait("gpsimd", "rr" + tag, ldr)
            icall("cp" + tag, disp, IndirectOffsetOnAxis(IDXS[ib][:], 0),
                  buf, None, 2047)
            cp_tok[rc] = icall("cp" + tag, backp,
                               IndirectOffsetOnAxis(IDXS[ib][:], 0),
                               IOTARC[:, rc:rc + 1], None, 2047)
        compact_done = P.cur("cp" + tag)
        if stopped("disp"):
            return None

        P.wait("sync", "s2m" + tag, P.cur("s2m" + tag))
        P.wait("sync", "cp" + tag, compact_done)
        for k in range(KC):
            dma("sync", "xd" + tag, R1[:, k * T:(k + 1) * T],
                disp[:, k * 128:(k + 1) * 128], transpose=True)
        P.wait("tensor", "xd" + tag, P.cur("xd" + tag))
        P.wait("vector", "s2m" + tag, P.cur("s2m" + tag))
        s2e_all = P.cur("s2e" + tag)

        def exp_bank_wait():
            P.wait("tensor", "s2e" + tag, s2e_all)
        h_phase(w1, w3, "ex" + tag, first_bank_wait=exp_bank_wait)
        ex_tt_all = P.cur("ex" + tag + "tt")
        ex_mm_all = P.cur("ex" + tag + "mm")

        P.wait("gpsimd", "ex" + tag + "mm", ex_mm_all)
        e2m_base = P.cur("e2m" + tag)
        e2s_tok = {}
        e2_ix = {"last": None}
        for half in range(2):
            if half == 1:
                P.wait("gpsimd", "e2m" + tag, e2m_base + TC)
            dma("gpsimd", "e2l" + tag, R1[:, 0:MC * 1024],
                _ap(w2, half * 1024, [[D, 128], [128 * D, MC], [1, 1024]]))
            wl = P.cur("e2l" + tag)
            P.wait("tensor", "e2l" + tag, wl)
            for m in range(TC):
                par = m % 2
                ib = par
                if m >= 2:
                    P.wait("tensor", "e2s" + tag, e2s_tok[(half, m - 2)])
                elif half == 1:
                    P.wait("tensor", "e2s" + tag, e2s_tok[(0, TC - 2 + m)])
                if m == 0 and half == 0:
                    P.wait("tensor", "ex" + tag + "tt", ex_tt_all)
                def mm2(e, m=m, par=par):
                    last = None
                    for k in range(MC):
                        for n in range(2):
                            last = e.matmul(
                                psa(par * 2 + n),
                                R2[:, k * T + m * 128:k * T + (m + 1) * 128],
                                R1[:, k * 1024 + n * 512:k * 1024 + (n + 1) * 512],
                                start=(k == 0), stop=(k == MC - 1))
                    return last
                t_m = cop("tensor", "e2m" + tag, mm2)
                if e2_ix.get("last") is not None:
                    P.wait("scalar", V, e2_ix["last"])
                ldb = dma("scalar", "bp" + tag, BP8[:],
                          backp[m * 128:(m + 1) * 128, :])
                P.wait("vector", "bp" + tag, ldb)
                P.wait("vector", "e2m" + tag, t_m)
                if (half, m - 2) in e2s_tok:
                    P.wait("vector", "e2s" + tag, e2s_tok[(half, m - 2)])
                elif half == 1 and (0, TC - 2 + m) in e2s_tok:
                    P.wait("vector", "e2s" + tag, e2s_tok[(0, TC - 2 + m)])
                bt = BT0 if par == 0 else BT1
                cop("vector", V, lambda e, par=par, bt=bt: e.tensor_copy(
                    bt[:, 0:512], psa(par * 2)))
                cop("vector", V, lambda e, par=par, bt=bt: e.tensor_copy(
                    bt[:, 512:1024], psa(par * 2 + 1)))
                t_ix = cop("vector", V, lambda e, ib=ib: e.tensor_copy(
                    IDXS[ib][:], BP8[:, 0:1]))
                e2_ix["last"] = t_ix
                P.wait("gpsimd", V, t_ix)
                e2s_tok[(half, m)] = icall(
                    "e2s" + tag, b2a_s, IndirectOffsetOnAxis(IDXS[ib][:], 0),
                    bt[:, 0:1024], None, 4095, elem_off=half * 1024)
        P.wait("gpsimd", "e2s" + tag, P.cur("e2s" + tag))
        if stopped("exp"):
            return None
        cc_b = coll("cb" + tag, "AllToAll", [b2a_s], [b2a_r])

        P.wait("vector", "cb" + tag, cc_b)
        co_tok = {}
        for lc in range(TC):
            ib = lc % 2
            if lc >= 2:
                P.wait("gpsimd", V, co_tok[lc - 2])
            cop("vector", V, lambda e, lc=lc: e.tensor_scalar(
                STAT[:, 24:25], LSEND[:, lc:lc + 1], ccol(4095.0), None,
                ALU.min))
            t_ix = cop("vector", V, lambda e, ib=ib: e.tensor_copy(
                IDXS[ib][:], STAT[:, 24:25]))
            P.wait("gpsimd", V, t_ix)
            bt = BT0 if ib == 0 else BT1
            t_g = icall("cg" + tag, bt[:], None, b2a_r,
                        IndirectOffsetOnAxis(IDXS[ib][:], 0), 4095)
            lds = dma("scalar", "s2" + tag, DUM[:],
                      shr[lc * 128:(lc + 1) * 128, :])
            if lc >= 1:
                P.wait("sync", "xo" + tag, P.cur("xo" + tag))
            ldx = dma("sync", "xc" + tag, XT[:],
                      xin_rows[lc * 128:(lc + 1) * 128, :])
            P.wait("vector", "cg" + tag, t_g)
            P.wait("vector", "s2" + tag, lds)
            P.wait("vector", "xc" + tag, ldx)
            cop("vector", V, lambda e, bt=bt, lc=lc: e.tensor_scalar(
                SC[:], bt[:], LGK[:, lc:lc + 1], None, ALU.mult))
            cop("vector", V, lambda e: e.scalar_tensor_tensor(
                SC[:], DUM[:], ccol(0.5), SC[:], ALU.mult, ALU.add))
            t_o = cop("vector", V, lambda e: e.tensor_add(XT[:], SC[:], XT[:]))
            co_tok[lc] = t_o
            P.wait("sync", V, t_o)
            dma("sync", "xo" + tag, xout_rows[lc * 128:(lc + 1) * 128, :], XT[:])
            P.wait("scalar", V, t_o)
        return P.cur("xo" + tag)

    def attn_pass(xin_rows, xout_rows):
        tag = "at"
        V, SCL, TT_ = "vat", "sat", "tat"

        def stopped(name):
            return STOP is not None and STOP == name

        xt_free = {}
        for c in range(TC):
            cs, cq, cv2, cr = c % 16, 16 + c % 16, 32 + c % 16, 48 + c % 16
            if c >= 1:
                P.wait("sync", V, xt_free[c - 1][0])
                P.wait("sync", SCL, xt_free[c - 1][1])
            ld = dma("sync", "xl" + tag, XT[:], xin_rows[c * 128:(c + 1) * 128, :])
            P.wait("vector", "xl" + tag, ld)
            cop("vector", V, lambda e, cs=cs: e.reduce_sum(
                STAT[:, cs:cs + 1], XT[:], axis=AXX), fence=False)
            P.wait("scalar", "xl" + tag, ld)
            t_sq = cop("scalar", SCL, lambda e, cq=cq: e.activation(
                DUM[:], XT[:], AF.Square, accum_out=STAT[:, cq:cq + 1]))
            if c >= 1:
                P.wait("vector", "xs" + tag, P.cur("xs" + tag) - 16)
            t_xc = cop("vector", V, lambda e: e.tensor_copy(SC[:], XT[:]),
                       fence=False)
            xt_free[c] = (t_xc, t_sq)
            P.wait("vector", SCL, t_sq)
            cop("vector", V, lambda e, cs=cs: e.tensor_scalar(
                STAT[:, cs:cs + 1], STAT[:, cs:cs + 1], ccol(1.0 / D), None,
                ALU.mult))
            cop("vector", V, lambda e, cs=cs, cv2=cv2: e.tensor_mul(
                STAT[:, cv2:cv2 + 1], STAT[:, cs:cs + 1], STAT[:, cs:cs + 1]))
            cop("vector", V, lambda e, cq=cq, cv2=cv2: e.scalar_tensor_tensor(
                STAT[:, cv2:cv2 + 1], STAT[:, cq:cq + 1], ccol(1.0 / D),
                STAT[:, cv2:cv2 + 1], ALU.mult, ALU.subtract))
            cop("vector", V, lambda e, cv2=cv2: e.tensor_scalar(
                STAT[:, cv2:cv2 + 1], STAT[:, cv2:cv2 + 1], ccol(EPS), None,
                ALU.add))
            t_rc = cop("vector", V, lambda e, cv2=cv2, cr=cr: e.reciprocal(
                STAT[:, cr:cr + 1], STAT[:, cv2:cv2 + 1]))
            P.wait("scalar", V, t_rc)
            t_rs = cop("scalar", SCL, lambda e, cr=cr: e.activation(
                STAT[:, cr:cr + 1], STAT[:, cr:cr + 1], AF.Sqrt))
            P.wait("vector", SCL, t_rs)
            t_n = cop("vector", V, lambda e, cs=cs, cr=cr: e.tensor_scalar(
                BT0[:], SC[:], STAT[:, cs:cs + 1], STAT[:, cr:cr + 1],
                ALU.subtract, ALU.mult), fence=False)
            P.wait("sync", V, t_n)
            st = dma("sync", "xs" + tag, xlnr[c * 128:(c + 1) * 128, :], BT0[:])
            P.wait("vector", "xs" + tag, st - 16)
        P.wait("sync", "xs" + tag, P.cur("xs" + tag))
        for k in range(KC):
            dma("sync", "th" + tag, R1[:, k * T:(k + 1) * T],
                xlnr[:, k * 128:(k + 1) * 128], transpose=True)
        thi = P.cur("th" + tag)
        dma("gpsimd", "tb" + tag, R2[:, A_T1:A_T1 + T], ck_t)
        dma("gpsimd", "tb" + tag, R2[:, A_T2:A_T2 + T], sk_t)
        P.wait("tensor", "th" + tag, thi)
        P.wait("vector", "tb" + tag, P.cur("tb" + tag))
        if stopped("aln"):
            return None

        def proj_rot_xpos(wsrc, h, dst_col, store_ch=None, store_to=None):
            P.wait("gpsimd", TT_, P.cur(TT_))
            lw = dma("gpsimd", "wsl", WS1[:, 0:2048],
                     _ap(wsrc, h * D * DH, [[DH, 128], [128 * DH, KC], [1, 128]]))
            P.wait("tensor", "wsl", lw)
            P.wait("tensor", V, P.cur(V))
            def mmP(e):
                last = None
                for k in range(KC):
                    for n in range(4):
                        last = e.matmul(
                            psa(n), WS1[:, k * 128:(k + 1) * 128],
                            R1[:, k * T + n * 512:k * T + n * 512 + 512],
                            start=(k == 0), stop=(k == KC - 1))
                return last
            t_p = cop("tensor", TT_, mmP)
            for n in range(4):
                P.wait("vector", TT_, t_p)
                if store_ch is not None:
                    P.wait("vector", store_ch, P.cur(store_ch))
                cop("vector", V, lambda e, n=n, dst_col=dst_col: e.tensor_copy(
                    R2[:, dst_col + n * 512:dst_col + (n + 1) * 512], psa(n)))
            t_raw = P.cur(V)
            for n in range(4):
                bk = n % 2
                P.wait("tensor", V, t_raw if n < 2 else P.cur(V))
                t_r = cop("tensor", TT_, lambda e, n=n, bk=bk, dst_col=dst_col:
                          e.matmul(psb(bk), RROT[:],
                                   R2[:, dst_col + n * 512:dst_col + (n + 1) * 512],
                                   start=True, stop=True))
                P.wait("vector", TT_, t_r)
                cop("vector", V, lambda e, n=n, bk=bk: e.tensor_tensor(
                    SC[:, 0:512], psb(bk),
                    R2[:, A_T2 + n * 512:A_T2 + (n + 1) * 512], ALU.mult))
                cop("vector", V, lambda e, n=n, dst_col=dst_col: e.tensor_tensor(
                    SC[:, 512:1024],
                    R2[:, dst_col + n * 512:dst_col + (n + 1) * 512],
                    R2[:, A_T1 + n * 512:A_T1 + (n + 1) * 512], ALU.mult))
                cop("vector", V, lambda e, n=n, dst_col=dst_col: e.tensor_tensor(
                    R2[:, dst_col + n * 512:dst_col + (n + 1) * 512],
                    SC[:, 0:512], SC[:, 512:1024], ALU.add))
            if store_to is not None:
                P.wait("sync", V, P.cur(V))
                dma("sync", store_ch, store_to, R2[:, dst_col:dst_col + T])
            return P.cur(V)

        for h in range(NH):
            proj_rot_xpos(wk, h, A_K, store_ch="kx" + tag,
                          store_to=kxf[h * 128:(h + 1) * 128, :])
            for t in range(TC):
                pt = PTB[t % 2]
                P.wait("tensor", V, P.cur(V))
                t_t = cop("tensor", TT_, lambda e, t=t, pt=pt: e.transpose(
                    pt, R2[:, A_K + t * 128:A_K + (t + 1) * 128], IDENT[:]))
                P.wait("vector", TT_, t_t)
                if t >= 2:
                    P.wait("vector", "ks" + tag, P.cur("ks" + tag))
                cop("vector", V, lambda e, t=t, pt=pt, h=h: e.tensor_scalar(
                    R2[:, A_KS + t * 128:A_KS + (t + 1) * 128], pt,
                    GKLC[:, h:h + 1], None, ALU.mult))
            P.wait("sync", V, P.cur(V))
            ks_st = dma("sync", "ks" + tag,
                        _ap(kst, h * 128, [[D, 128], [128 * D, TC], [1, 128]]),
                        R2[:, A_KS:A_KS + T])
            P.wait("gpsimd", TT_, P.cur(TT_))
            lw = dma("gpsimd", "wsl", WS3[:, 0:2048],
                     _ap(wv, h * 128, [[D, 128], [128 * D, KC], [1, 128]]))
            P.wait("tensor", "wsl", lw)
            P.wait("tensor", V, P.cur(V))
            def mmV(e):
                last = None
                for k in range(KC):
                    for n in range(4):
                        last = e.matmul(
                            psa(n), WS3[:, k * 128:(k + 1) * 128],
                            R1[:, k * T + n * 512:k * T + n * 512 + 512],
                            start=(k == 0), stop=(k == KC - 1))
                return last
            t_pv = cop("tensor", TT_, mmV)
            for n in range(4):
                P.wait("vector", TT_, t_pv)
                P.wait("vector", "kx" + tag, P.cur("kx" + tag))
                cop("vector", V, lambda e, n=n: e.tensor_copy(
                    R2[:, A_K + n * 512:A_K + (n + 1) * 512], psa(n)))
            for t in range(TC):
                pt = PTB[t % 2]
                P.wait("tensor", V, P.cur(V))
                t_t = cop("tensor", TT_, lambda e, t=t, pt=pt: e.transpose(
                    pt, R2[:, A_K + t * 128:A_K + (t + 1) * 128], IDENT[:]))
                P.wait("vector", TT_, t_t)
                if t >= 2:
                    P.wait("vector", "vs" + tag, P.cur("vs" + tag))
                cop("vector", V, lambda e, t=t, pt=pt: e.tensor_copy(
                    R2[:, A_VS + t * 128:A_VS + (t + 1) * 128], pt))
            P.wait("sync", V, P.cur(V))
            dma("sync", "vs" + tag,
                _ap(vst, h * 128, [[D, 128], [128 * D, TC], [1, 128]]),
                R2[:, A_VS:A_VS + T])
            P.wait("tensor", V, P.cur(V))
            tc_tok = {}
            for t in range(TC):
                tc_tok[t] = cop("tensor", TT_, lambda e, t=t: e.matmul(
                    PSA[:, t * 128:(t + 1) * 128],
                    R2[:, A_KS + t * 128:A_KS + (t + 1) * 128],
                    R2[:, A_VS + t * 128:A_VS + (t + 1) * 128],
                    start=True, stop=True))
            cop("vector", V, lambda e: e.memset(S_F[:], 0.0))
            for t in range(TC):
                P.wait("vector", TT_, tc_tok[min(t | 3, TC - 1)])
                cop("vector", V, lambda e, t=t, h=h: e.scalar_tensor_tensor(
                    S_F[:], S_F[:], ccol(GC[h]),
                    PSA[:, t * 128:(t + 1) * 128], ALU.mult, ALU.add))
            P.wait("sync", V, P.cur(V))
            if h < 8:
                dma("sync", "st" + tag, st_in[h * 128:(h + 1) * 128, :], S_F[:])
            else:
                dma("sync", "st" + tag,
                    st_in_b[(h - 8) * 128:(h - 7) * 128, :], S_F[:])
            P.wait("tensor", V, P.cur(V))
            if h == 7:
                P.wait("gpsimd", "st" + tag, P.cur("st" + tag))
                cc_st_a = coll("cs" + tag, "AllGather",
                               [st_in[0:1024, :]], [st_out[0:8192, :]])
        P.wait("gpsimd", "st" + tag, P.cur("st" + tag))
        if stopped("asum"):
            return None
        cc_st_b = coll("cs" + tag, "AllGather", [st_in_b], [st_out_b])

        P.wait("gpsimd", V, P.cur(V))
        dma("gpsimd", "tb2" + tag, R2[:, A_T1:A_T1 + T], cq_t)
        dma("gpsimd", "tb2" + tag, R2[:, A_T2:A_T2 + T], sq_t)
        P.wait("vector", "tb2" + tag, P.cur("tb2" + tag))

        for h in range(NH):
            proj_rot_xpos(wq, h, A_Q)
            P.wait("sync", TT_, P.cur(TT_))
            ldk = dma("sync", "mk" + tag, R2[:, A_K:A_K + T],
                      kxf[h * 128:(h + 1) * 128, :])
            dma("sync", "mk" + tag, R2[:, A_KS:A_KS + T],
                _ap(kst, h * 128, [[D, 128], [128 * D, TC], [1, 128]]))
            dma("sync", "mk" + tag, R2[:, A_VS:A_VS + T],
                _ap(vst, h * 128, [[D, 128], [128 * D, TC], [1, 128]]))
            cop("vector", V, lambda e: e.memset(S_F[:], 0.0))
            P.wait("scalar", "cs" + tag, cc_st_a if h < 8 else cc_st_b)
            st_src = st_out if h < 8 else st_out_b
            hh = h if h < 8 else h - 8
            for kk in range(NCORES):
                P.wait("scalar", V, P.cur(V))
                ldm = dma("scalar", "ms" + tag, MTILE[:],
                          st_src[kk * 8 * 128 + hh * 128:
                                 kk * 8 * 128 + (hh + 1) * 128, :])
                P.wait("vector", "ms" + tag, ldm)
                cop("vector", V, lambda e, h=h, kk=kk: e.scalar_tensor_tensor(
                    S_F[:], MTILE[:], WSPAN[:, 8 * h + kk:8 * h + kk + 1],
                    S_F[:], ALU.mult, ALU.add))
            sb_buf = [S_B, S_B2]
            ar_buf = [ARES, ARES2]
            t_sb = {}
            t_sb[-1] = cop("vector", V, lambda e: e.tensor_copy(S_B[:], S_F[:]))
            P.wait("tensor", "mk" + tag, P.cur("mk" + tag))
            yev = {}
            ymm = {}
            armm = {}
            tc_tok = {}
            for t in range(TC):
                half = t // 8
                if t % 8 == 0:
                    P.wait("tensor", V, P.cur(V) if t == 0 else t_sb[7])
                    for u in range(8):
                        tt = half * 8 + u
                        tc_tok[tt] = cop("tensor", TT_, lambda e, tt=tt, u=u:
                                         e.matmul(
                            PSB[0:128, u * 128:(u + 1) * 128],
                            R2[:, A_KS + tt * 128:A_KS + (tt + 1) * 128],
                            R2[:, A_VS + tt * 128:A_VS + (tt + 1) * 128],
                            start=True, stop=True))
                P.wait("vector", TT_, tc_tok[(t // 8) * 8 + 7])
                if t >= 1:
                    P.wait("vector", TT_, ymm[t - 1])
                cop("vector", V, lambda e, h=h, u=t % 8:
                    e.scalar_tensor_tensor(
                        S_F[:], S_F[:], ccol(GC[h]),
                        PSB[0:128, u * 128:u * 128 + 128],
                        ALU.mult, ALU.add))
                t_sb[t] = cop("vector", V,
                              lambda e, t=t: e.tensor_copy(
                                  sb_buf[(t + 1) % 2][:], S_F[:]))
                bk = t % 2
                ybank = (t // 4) % 2
                abank = 2 + t % 2
                if t >= 2:
                    P.wait("tensor", V, armm[t - 2])
                t_a = cop("tensor", TT_, lambda e, t=t, abank=abank: e.matmul(
                    PSA[:, abank * 512:abank * 512 + 128],
                    R2[:, A_K + t * 128:A_K + (t + 1) * 128],
                    R2[:, A_Q + t * 128:A_Q + (t + 1) * 128],
                    start=True, stop=True))
                P.wait("vector", TT_, t_a)
                t_ar = cop("vector", V, lambda e, t=t, abank=abank, h=h:
                           e.scalar_tensor_tensor(
                               ar_buf[t % 2][:],
                               PSA[:, abank * 512:abank * 512 + 128],
                               GINV[:, h:h + 1], MASKT[:], ALU.mult, ALU.mult))
                armm[t] = t_ar
                P.wait("tensor", V, t_ar)
                P.wait("tensor", V, t_sb[t - 1])
                if t % 4 == 0 and t >= 8:
                    P.wait("tensor", V, yev[ybank])
                def mmy(e, t=t, ybank=ybank):
                    o = PSA[:, ybank * 512 + (t % 4) * 128:
                            ybank * 512 + (t % 4) * 128 + 128]
                    e.matmul(o, ar_buf[t % 2][:],
                             R2[:, A_VS + t * 128:A_VS + (t + 1) * 128],
                             start=True, stop=False)
                    return e.matmul(o, R2[:, A_Q + t * 128:A_Q + (t + 1) * 128],
                                    sb_buf[t % 2][:], start=False, stop=True)
                t_y = cop("tensor", TT_, mmy)
                ymm[t] = t_y
                if t % 4 == 3:
                    P.wait("vector", TT_, t_y)
                    q = t - 3
                    for dt_ in range(4):
                        tk = cop("vector", V, lambda e, q=q, dt_=dt_, h=h,
                                 ybank=ybank: e.tensor_scalar(
                                     R2[:, (q + dt_) * T + h * 128:
                                        (q + dt_) * T + (h + 1) * 128],
                                     PSA[:, ybank * 512 + dt_ * 128:
                                         ybank * 512 + dt_ * 128 + 128],
                                     GPOW[:, h:h + 1], None, ALU.mult))
                    yev[ybank] = tk
        heads_done = P.cur(V)
        if stopped("amain"):
            return None

        P.wait("scalar", V, heads_done)
        for tcn in range(TC):
            for h in range(NH):
                col = tcn * NH + h
                cop("vector", V, lambda e, tcn=tcn, h=h, col=col:
                    e.reduce_sum(GN_S[:, col:col + 1],
                                 R2[:, tcn * T + h * 128:
                                    tcn * T + (h + 1) * 128], axis=AXX),
                    fence=False)
                cop("scalar", SCL, lambda e, tcn=tcn, h=h, col=col:
                    e.activation(DUM[:, 0:128],
                                 R2[:, tcn * T + h * 128:tcn * T + (h + 1) * 128],
                                 AF.Square, accum_out=GN_S[:, 256 + col:257 + col]),
                    fence=False)
        P.wait("vector", SCL, P.cur(SCL))
        P.wait("vector", V, P.cur(V))
        cop("vector", V, lambda e: e.tensor_scalar(
            GN_S[:, 0:256], GN_S[:, 0:256], ccol(1.0 / 128), None, ALU.mult))
        cop("vector", V, lambda e: e.tensor_mul(
            SC[:, 0:256], GN_S[:, 0:256], GN_S[:, 0:256]))
        cop("vector", V, lambda e: e.scalar_tensor_tensor(
            GN_S[:, 256:512], GN_S[:, 256:512], ccol(1.0 / 128),
            SC[:, 0:256], ALU.mult, ALU.subtract))
        cop("vector", V, lambda e: e.tensor_scalar(
            GN_S[:, 256:512], GN_S[:, 256:512], ccol(EPS), None, ALU.add))
        t_rc = cop("vector", V, lambda e: e.reciprocal(SC[:, 256:512],
                                                       GN_S[:, 256:512]))
        P.wait("scalar", V, t_rc)
        t_rs = cop("scalar", SCL, lambda e: e.activation(
            SC[:, 256:512], SC[:, 256:512], AF.Sqrt))
        P.wait("vector", SCL, t_rs)
        for tcn in range(TC):
            for h in range(NH):
                col = tcn * NH + h
                cop("vector", V, lambda e, tcn=tcn, h=h, col=col:
                    e.tensor_scalar(
                        BT0[:, h * 128:(h + 1) * 128],
                        R2[:, tcn * T + h * 128:tcn * T + (h + 1) * 128],
                        GN_S[:, col:col + 1], SC[:, 256 + col:257 + col],
                        ALU.subtract, ALU.mult), fence=False)
            t_n = P.cur(V)
            P.wait("sync", V, t_n)
            st = dma("sync", "yn" + tag, ynr[tcn * 128:(tcn + 1) * 128, :],
                     BT0[:])
            P.wait("vector", "yn" + tag, st)
        yn_done = P.cur("yn" + tag)
        if stopped("agn"):
            return None

        P.wait("sync", "yn" + tag, yn_done)
        gw_tok = {}
        gmm_tok = {}
        P.wait("tensor", V, P.cur(V))
        for m in range(NH if GLIMIT is None else GLIMIT):
            if m >= 1:
                P.wait("sync", V, P.cur(V))
            tky = dma("sync", "yf" + tag, R2[:, A_K:A_K + T],
                      ynr[:, m * 128:(m + 1) * 128], transpose=True)
            if m >= 2:
                P.wait("gpsimd", TT_, gmm_tok[m - 1])
            lw = wslice(wgm, m, KC, 128, WS1, (m % 2) * 2048)
            P.wait("vector", "yf" + tag, tky)
            t_aff = cop("vector", V, lambda e, m=m: e.tensor_scalar(
                R2[:, A_K:A_K + T], R2[:, A_K:A_K + T], GNW[:, m:m + 1],
                GNB[:, m:m + 1], ALU.mult, ALU.add))
            P.wait("tensor", "wsl", lw)
            if m >= 1:
                P.wait("tensor", V, gw_tok[m - 1])
            def mmG(e, m=m):
                last = None
                for k in range(KC):
                    for n in range(4):
                        last = e.matmul(
                            psa(n), WS1[:, (m % 2) * 2048 + k * 128:
                                         (m % 2) * 2048 + (k + 1) * 128],
                            R1[:, k * T + n * 512:k * T + n * 512 + 512],
                            start=(k == 0), stop=(k == KC - 1))
                return last
            t_g = cop("tensor", TT_, mmG)
            gmm_tok[m] = t_g
            for n in range(4):
                P.wait("scalar", TT_, t_g)
                t_sil = cop("scalar", SCL, lambda e, n=n: e.activation(
                    DUM[:, n * 512:(n + 1) * 512], psa(n), AF.Silu))
                P.wait("vector", SCL, t_sil)
                P.wait("vector", V, t_aff)
                gw_tok[m] = cop("vector", V, lambda e, m=m, n=n: e.tensor_tensor(
                    R2[:, m * T + n * 512:m * T + (n + 1) * 512],
                    DUM[:, n * 512:(n + 1) * 512],
                    R2[:, A_K + n * 512:A_K + (n + 1) * 512], ALU.mult))
        gated_done = P.cur(V)
        if stopped("agt"):
            return None

        P.wait("gpsimd", TT_, P.cur(TT_))
        for k in range(KC):
            dma("gpsimd", "wo" + tag, R1[:, k * T:(k + 1) * T],
                _ap(wo, k * 128 * D, [[D, 128], [1, D]]))
        wol = P.cur("wo" + tag)
        P.wait("tensor", "wo" + tag, wol)
        P.wait("tensor", V, gated_done)
        wo_ev = {}
        for m in range(TC):
            par = m % 2
            banks = [psa(0), psa(1), psa(2), psa(3)] if par == 0 else \
                    [psb(0), psb(1), psa(0), psa(1)]
            if m >= 1:
                P.wait("tensor", "we" + tag, wo_ev[m - 1])
            def mmO(e, m=m, banks=banks):
                last = None
                for k in range(KC):
                    for n in range(4):
                        last = e.matmul(
                            banks[n],
                            R2[:, k * T + m * 128:k * T + (m + 1) * 128],
                            R1[:, k * T + n * 512:k * T + n * 512 + 512],
                            start=(k == 0), stop=(k == KC - 1))
                return last
            t_m = cop("tensor", "wm" + tag, mmO)
            if m >= 1:
                P.wait("sync", "xo" + tag, P.cur("xo" + tag))
            ldx = dma("sync", "xw" + tag, XT[:],
                      xin_rows[m * 128:(m + 1) * 128, :])
            P.wait("vector", "wm" + tag, t_m)
            P.wait("vector", "xw" + tag, ldx)
            for n in range(4):
                cop("vector", V, lambda e, n=n, banks=banks: e.tensor_tensor(
                    XT[:, n * 512:(n + 1) * 512], banks[n],
                    XT[:, n * 512:(n + 1) * 512], ALU.add))
            wo_ev[m] = cop("vector", "we" + tag, lambda e: e.nop())
            P.wait("sync", "we" + tag, wo_ev[m])
            dma("sync", "xo" + tag, xout_rows[m * 128:(m + 1) * 128, :], XT[:])
        return P.cur("xo" + tag)

    r1 = ffn_pass("p1", x0, xr1)
    fin = None
    if r1 is not None and STOP != "p1":
        P.barrier("after_p1")
        ra = attn_pass(xr1, xr2)
        if ra is not None and STOP != "at":
            P.barrier("after_at")
            fin = ffn_pass("p3", xr2, out_x3)
    if fin is None:
        P.barrier("trunc")
        for c in range(TC):
            dma("gpsimd", "xop3", out_x3[c * 128:(c + 1) * 128, :],
                x0[c * 128:(c + 1) * 128, :])
        fin = P.cur("xop3")

    if DBG:
        P.barrier("dbgbar")
        for name in DBG:
            for c in range(0, dramv[name].tensor.shape[0], 128):
                n = min(128, dramv[name].tensor.shape[0] - c)
                dma("gpsimd", "dbgc", dbg[name][c:c + n, :], dramv[name][c:c + n, :])
        P.wait("gpsimd", "dbgc", P.cur("dbgc"))

    P.wait("sync", "xo", fin)
    P.wait("gpsimd", "xo", fin)

    def _simulate():
        sems = defaultdict(int)
        pcs = {e: 0 for e in Prog.ENGS}
        progress = True
        while progress:
            progress = False
            for e in Prog.ENGS:
                tr = P.trace[e]
                while pcs[e] < len(tr):
                    kind, sem, val = tr[pcs[e]]
                    if kind == "w" and sems[sem] < val:
                        break
                    if kind == "i":
                        sems[sem] += val
                    pcs[e] += 1
                    progress = True
        stuck = {e: pcs[e] for e in Prog.ENGS if pcs[e] < len(P.trace[e])}
        if stuck:
            for e, pc in stuck.items():
                print(f"DEADLOCK {e} at {pc}/{len(P.trace[e])}: "
                      f"{P.trace[e][pc]}  sem={sems[P.trace[e][pc][1]]}")
            raise RuntimeError("deadlock in recorded schedule")
    _simulate()

    with es, nc.Block() as block:
        semctx = contextlib.ExitStack()
        sems = {n: semctx.enter_context(nc.semaphore(n))
                for n in sorted(P.semnames)}
        with semctx:
            def run(e, key):
                e._sems = sems
                if key == "gpsimd":
                    e._regs = {2047: e.to_reg(2047), 4095: e.to_reg(4095)}
                for f in P.ops[key]:
                    f(e)

            @block.sync
            def _(s):
                run(s, "sync")

            @block.scalar
            def _(s):
                run(s, "scalar")

            @block.vector
            def _(v):
                run(v, "vector")

            @block.tensor
            def _(t):
                run(t, "tensor")

            @block.gpsimd
            def _(g):
                run(g, "gpsimd")

    return nc


def _host_constants():
    perm = np.empty(DH, np.int64)
    perm[:64] = np.arange(64) * 2
    perm[64:] = np.arange(64) * 2 + 1
    c = {"perm": perm}
    c["triu_i"] = np.triu(np.ones((128, 128), np.float32))
    c["triu_e"] = np.triu(np.ones((128, 128), np.float32), 1)
    c["ident"] = np.eye(128, dtype=ml_dtypes.bfloat16)
    c["identf"] = np.eye(8, dtype=np.float32)
    R = np.zeros((128, 128), np.float32)
    for i in range(64):
        R[64 + i, i] = -1.0
        R[i, 64 + i] = 1.0
    c["rrot"] = R.astype(ml_dtypes.bfloat16)
    c["ones1"] = np.ones((1, 128), np.float32)
    c["onesc"] = np.ones((128, 1), np.float32)
    c["iota8m8"] = np.broadcast_to(np.arange(8) - 8.0, (128, 8)).astype(np.float32).copy()
    c["iotarc"] = (np.arange(128)[:, None] + 128.0 * np.arange(32)[None, :]).astype(np.float32)
    c["maskt"] = (np.arange(128)[None, :] >= np.arange(128)[:, None]).astype(ml_dtypes.bfloat16)
    g = 1.0 - np.exp(np.linspace(np.log(1 / 32), np.log(1 / 512), NH))
    c["gammas"] = g
    l = np.arange(128)[:, None].astype(np.float64)
    c["gpow"] = (g[None, :] ** l).astype(np.float32)
    c["gklc"] = (g[None, :] ** (128 - l)).astype(np.float32)
    c["ginv"] = (g[None, :] ** (-l)).astype(np.float32)
    return c


def _xpos_tables(span_start):
    d, half = DH, DH // 2
    pos = np.arange(span_start, span_start + T, dtype=np.float64)
    xscale = (np.arange(0, d, 2).astype(np.float64) + 0.4 * d) / (1.4 * d)
    scale = xscale[None, :] ** (pos[:, None] / 512.0)
    inv_freq = 1.0 / 10000.0 ** (np.arange(half).astype(np.float64) / half)
    sinu = pos[:, None] * inv_freq[None, :]
    sin, cos = np.sin(sinu), np.cos(sinu)

    def permfm(v):
        out = np.empty((128, T), np.float64)
        out[:64] = v.T
        out[64:] = v.T
        return out
    return (permfm(cos * scale).astype(np.float32),
            permfm(sin * scale).astype(np.float32),
            permfm(cos / scale).astype(np.float32),
            permfm(sin / scale).astype(np.float32))


def _prep_inputs(inp, consts_vals):
    hc = _host_constants()
    perm = hc["perm"]
    X = np.asarray(inp["L0_attn_X"], np.float32).reshape(-1, D)
    WQ = np.asarray(inp["L0_attn_W_Q"], np.float32)[:, :, perm].reshape(NH * D, DH)
    WK = np.asarray(inp["L0_attn_W_K"], np.float32)[:, :, perm].reshape(NH * D, DH)
    WV = np.transpose(np.asarray(inp["L0_attn_W_V"], np.float32), (1, 0, 2)).reshape(D, D)
    gnw_c = np.asarray(inp["L0_attn_gn_weight"], np.float32).reshape(NH, 128).T.copy()
    gnb_c = np.asarray(inp["L0_attn_gn_bias"], np.float32).reshape(NH, 128).T.copy()
    Wg = np.asarray(inp["L0_ffn_Wg"], np.float32)
    wgate = Wg.reshape(KC, 128, 8).transpose(1, 0, 2).reshape(128, 128).copy()
    consts_arr = np.zeros((128, 64), np.float32)
    for i, v in enumerate(consts_vals):
        consts_arr[:, i] = v
    gam = hc["gammas"]
    L = float(T)

    in_maps = []
    for j in range(NCORES):
        s, b = j % 4, j // 4
        cq, sq, ck, sk = _xpos_tables(s * T)
        jone = np.zeros((128, 1), np.float32)
        jone[16 * j, 0] = 1.0
        wspan = np.zeros((128, 128), np.float32)
        for h in range(NH):
            for k in range(NCORES):
                kb, ks = k // 4, k % 4
                if kb == b and ks < s:
                    wspan[:, 8 * h + k] = gam[h] ** (L * (s - 1 - ks))
        m = dict(
            x0=np.ascontiguousarray(X[j * T:(j + 1) * T]),
            wq=WQ, wk=WK, wv=WV,
            wgm=np.asarray(inp["L0_attn_W_G"], np.float32),
            wo=np.asarray(inp["L0_attn_W_O"], np.float32),
            gnw=gnw_c, gnb=gnb_c, wgate=wgate,
            w1=np.ascontiguousarray(np.asarray(inp["L0_ffn_W1"], np.float32)[j]),
            w3=np.ascontiguousarray(np.asarray(inp["L0_ffn_W3"], np.float32)[j]),
            w2=np.ascontiguousarray(np.asarray(inp["L0_ffn_W2"], np.float32)[j]),
            sw1=np.asarray(inp["L0_ffn_sw1"], np.float32),
            sw3=np.asarray(inp["L0_ffn_sw3"], np.float32),
            sw2=np.asarray(inp["L0_ffn_sw2"], np.float32),
            cq=cq, sq=sq, ck=ck, sk=sk,
            gpow=hc["gpow"], gklc=hc["gklc"], ginv=hc["ginv"],
            maskt=hc["maskt"], triu_i=hc["triu_i"], triu_e=hc["triu_e"],
            ident=hc["ident"], identf=hc["identf"], rrot=hc["rrot"],
            ones1=hc["ones1"], onesc=hc["onesc"], iota8m8=hc["iota8m8"],
            iotarc=hc["iotarc"], jone=jone, wspan=wspan, constsin=consts_arr,
        )
        in_maps.append(m)
    return in_maps


def kernel(**inputs):
    consts_vals = []
    with allow_ici_all_to_all():
        nc = build(consts_vals)
        in_maps = _prep_inputs(inputs, consts_vals)
        res = run_bass_kernel_spmd(nc, in_maps, core_ids=list(range(NCORES)))
    out = np.concatenate([np.asarray(r["out"], np.float32) for r in res.results], 0)
    return out.reshape(2, 8192, D)


# version 07
# speedup vs baseline: 1.0650x; 1.0511x over previous
import contextlib
from collections import defaultdict

import numpy as np
import ml_dtypes

import concourse.bass as bass
import concourse.mybir as mybir
from concourse.bass import AP, IndirectOffsetOnAxis
from concourse.bass_utils import run_bass_kernel_spmd
from concourse.aot_env import allow_ici_all_to_all

F32 = mybir.dt.float32
BF16 = mybir.dt.bfloat16
I32 = mybir.dt.int32
AF = mybir.ActivationFunctionType
ALU = mybir.AluOpType
AXX = mybir.AxisListType.X

NCORES = 8
D = 2048
HID = 2816
T = 2048
NH, DH = 16, 128
KC = D // 128
MC = HID // 128
TC = T // 128
CAP = 2048
BLK = 512
SENT = 30000.0
EPS = 1e-5

DBG = []
STOP = None
GLIMIT = None
STOP_AFTER = None


class Prog:
    ENGS = ("sync", "scalar", "vector", "tensor", "gpsimd")

    def __init__(self):
        self.ops = defaultdict(list)
        self.semval = defaultdict(int)
        self.lastwait = {}
        self.semnames = set()
        self.trace = defaultdict(list)

    @staticmethod
    def canon(name):
        for suf in ("p1", "p3", "at"):
            if name.endswith(suf) and len(name) > len(suf):
                return name[:-len(suf)]
        return name

    def emit(self, eng, fn):
        self.ops[eng].append(fn)

    def inc(self, name, n=1):
        name = self.canon(name)
        self.semnames.add(name)
        self.semval[name] += n
        return self.semval[name]

    def cur(self, name):
        name = self.canon(name)
        self.semnames.add(name)
        return self.semval[name]

    def wait(self, eng, name, val=None):
        name = self.canon(name)
        if val is None:
            val = self.semval[name]
        if val <= 0:
            return
        key = (eng, name)
        if self.lastwait.get(key, -1) >= val:
            return
        self.lastwait[key] = val
        self.semnames.add(name)

        def f(e, _n=name, _v=val):
            e.wait_ge(e._sems[_n], _v)
        self.emit(eng, f)
        self.trace[eng].append(("w", name, val))

    def barrier(self, tag):
        toks = {}
        for eng in self.ENGS:
            ch = "bar_" + eng
            toks[eng] = self.inc(ch, 1)
            def f(e, ch=ch):
                e.nop().then_inc(e._sems[ch], 1)
            self.emit(eng, f)
            self.trace[eng].append(("i", ch, 1))
        for eng in self.ENGS:
            for src in self.ENGS:
                if src != eng:
                    self.wait(eng, "bar_" + src, toks[src])


def _ap(t, offset, dims):
    return AP(t.tensor, offset, [list(d) for d in dims])


def build(consts_vals):
    nc = bass.Bass("TRN2", target_bir_lowering=False, debug=False)
    P = Prog()

    def ein(name, shape, dt=F32):
        return nc.dram_tensor(name, list(shape), dt, kind="ExternalInput").ap()

    x0 = ein("x0", (T, D))
    wq = ein("wq", (NH * D, DH))
    wk = ein("wk", (NH * D, DH))
    wv = ein("wv", (D, D))
    wgm = ein("wgm", (D, D))
    wo = ein("wo", (D, D))
    gnw_i = ein("gnw", (128, NH))
    gnb_i = ein("gnb", (128, NH))
    wgate = ein("wgate", (128, 128))
    w1 = ein("w1", (D, HID))
    w3 = ein("w3", (D, HID))
    w2 = ein("w2", (HID, D))
    sw1 = ein("sw1", (D, HID))
    sw3 = ein("sw3", (D, HID))
    sw2 = ein("sw2", (HID, D))
    cq_t = ein("cq", (128, T))
    sq_t = ein("sq", (128, T))
    ck_t = ein("ck", (128, T))
    sk_t = ein("sk", (128, T))
    gpow_i = ein("gpow", (128, NH))
    gklc_i = ein("gklc", (128, NH))
    ginv_i = ein("ginv", (128, NH))
    maskt_i = ein("maskt", (128, 128), BF16)
    triu_ii = ein("triu_i", (128, 128))
    triu_ei = ein("triu_e", (128, 128))
    ident_i = ein("ident", (128, 128), BF16)
    identf_i = ein("identf", (8, 8))
    rrot_i = ein("rrot", (128, 128), BF16)
    ones1_i = ein("ones1", (1, 128))
    onesc_i = ein("onesc", (128, 1))
    iota8_i = ein("iota8m8", (128, 8))
    iotarc_i = ein("iotarc", (128, 32))
    jone_i = ein("jone", (128, 1))
    wspan_i = ein("wspan", (128, 128))
    consts_i = ein("constsin", (128, 64))

    out_x3 = nc.dram_tensor("out", [T, D], F32, kind="ExternalOutput").ap()

    def din(name, shape, dt):
        return nc.dram_tensor(name, list(shape), dt, kind="Internal").ap()

    xr1 = din("xr1", (T, D), F32)
    xr2 = din("xr2", (T, D), F32)
    xlnr = din("xlnr", (T, D), BF16)
    xlolo = din("xlolo", (T, D), BF16)
    lg_loc = din("lg_loc", (T, 8), F32)
    lg_out = din("lg_out", (NCORES * T, 8), F32)
    tot_rt = din("tot_rt", (128, 8), F32)
    a2a_s = din("a2a_s", (NCORES * BLK, D), BF16)
    a2a_r = din("a2a_r", (NCORES * BLK, D), BF16)
    mt_s = din("mt_s", (NCORES * BLK, 8), F32)
    mt_r = din("mt_r", (NCORES * BLK, 8), F32)
    disp = din("disp", (CAP, D), BF16)
    backp = din("backp", (CAP, 8), F32)
    shr = din("shr", (T, D), BF16)
    b2a_s = din("b2a_s", (NCORES * BLK, D), BF16)
    b2a_r = din("b2a_r", (NCORES * BLK, D), BF16)
    kxf = din("kxf", (D, T), BF16)
    kst = din("kst", (T, D), BF16)
    vst = din("vst", (T, D), BF16)
    st_in = din("st_in", (NH * 128, 128), F32)
    st_out = din("st_out", (NCORES * NH * 128, 128), F32)
    st_in_b = din("st_in_b", (NH * 64, 128), F32)
    st_out_b = din("st_out_b", (NCORES * NH * 64, 128), F32)
    ynr = din("ynr", (T, D), BF16)
    dramv = dict(xr1=xr1, xr2=xr2, xlnr=xlnr, xlolo=xlolo, lg_loc=lg_loc,
                 lg_out=lg_out, tot_rt=tot_rt, a2a_s=a2a_s, a2a_r=a2a_r,
                 mt_s=mt_s, mt_r=mt_r, disp=disp, backp=backp, shr=shr,
                 b2a_s=b2a_s, b2a_r=b2a_r, kxf=kxf, kst=kst, vst=vst,
                 st_in=st_in, st_out=st_out, ynr=ynr)
    dbg = {}
    for name in DBG:
        src = dramv[name]
        dbg[name] = nc.dram_tensor("dbg_" + name, list(src.tensor.shape),
                                   src.tensor.dtype, kind="ExternalOutput").ap()

    es = contextlib.ExitStack()

    def sbt(name, shape, dt):
        return es.enter_context(nc.sbuf_tensor(name, list(shape), dt))

    R1 = sbt("R1", (128, 32768), BF16)
    R2 = sbt("R2", (128, 45056), BF16)
    WS1 = sbt("WS1", (128, 4096), BF16)
    WS3 = sbt("WS3", (128, 4096), BF16)
    DUM = sbt("DUM", (128, 2048), BF16)
    XT = sbt("XT", (128, 2048), F32)
    BT0 = sbt("BT0", (128, 2048), BF16)
    BT1 = sbt("BT1", (128, 2048), BF16)
    SC = sbt("SCR", (128, 2048), F32)
    GN_S = sbt("GN_S", (128, 512), F32)
    CONSTS = sbt("CONSTS", (128, 64), F32)
    STAT = sbt("STAT", (128, 64), F32)
    EQ = sbt("EQ", (128, 32), F32)
    OHB = sbt("OHB", (128, 1024), BF16)
    ONESC_B = sbt("ONESC_Bs", (128, 1), BF16)
    LGT = sbt("LGT", (128, 16), F32)
    TOTS = sbt("TOTS", (128, 8), F32)
    OFFS = sbt("OFFS", (128, 8), F32)
    COFF = sbt("COFF", (128, 8), F32)
    LOC = sbt("LOC", (1, 32), F32)
    LSLOT = sbt("LSLOT", (128, 16), F32)
    LSEND = sbt("LSEND", (128, 16), F32)
    LGK = sbt("LGK", (128, 16), F32)
    MR = sbt("MR", (128, 8), F32)
    BP8 = sbt("BP8", (128, 8), F32)
    IDXS = [sbt(f"IDXS{i}", (128, 1), I32) for i in range(4)]
    OOBT = sbt("OOBT", (128, 128), F32)
    WGH = sbt("WGH", (128, 128), BF16)
    WGL = sbt("WGL", (128, 128), BF16)
    TRIU_I = sbt("TRIU_Is", (128, 128), F32)
    TRIU_E = sbt("TRIU_Es", (128, 128), F32)
    IOTA8 = sbt("IOTA8s", (128, 8), F32)
    IOTARC = sbt("IOTARCs", (128, 32), F32)
    IDENT = sbt("IDENTs", (128, 128), BF16)
    IDENTF = sbt("IDENTFs", (8, 8), F32)
    RROT = sbt("RROTs", (128, 128), BF16)
    MASKT = sbt("MASKTs", (128, 128), BF16)
    GPOW = sbt("GPOWs", (128, NH), F32)
    GKLC = sbt("GKLCs", (128, NH), F32)
    GINV = sbt("GINVs", (128, NH), F32)
    GNW = sbt("GNWs", (128, NH), F32)
    GNB = sbt("GNBs", (128, NH), F32)
    JONE = sbt("JONEs", (128, 1), F32)
    WSPAN = sbt("WSPANs", (128, 128), F32)
    ONES1 = sbt("ONES1s", (1, 128), F32)
    ONESC = sbt("ONESCs", (128, 1), F32)
    ARES = sbt("ARES", (128, 128), BF16)
    ARES2 = sbt("ARES2", (128, 128), BF16)
    S_F = sbt("S_F", (128, 128), F32)
    S_B = sbt("S_B", (128, 128), BF16)
    S_B2 = sbt("S_B2", (128, 128), BF16)
    MTILE = sbt("MTILE", (128, 128), F32)

    A_Q, A_K, A_KS, A_VS, A_T1, A_T2 = 32768, 34816, 36864, 38912, 40960, 43008

    PSA = es.enter_context(nc.psum_tensor("PSA", [128, 2048], F32))
    PSB = es.enter_context(nc.psum_tensor("PSB", [128, 1024], F32))
    PTB_full = [es.enter_context(nc.psum_tensor(f"PTB{i}", [128, 1024], BF16))
                for i in range(2)]
    PTB = [t[:, 0:128] for t in PTB_full]

    def psa(i):
        return PSA[:, i * 512:(i + 1) * 512]

    def psb(i):
        return PSB[:, i * 512:(i + 1) * 512]

    cmap = {}

    def ccol(v):
        v = float(v)
        if v not in cmap:
            assert len(consts_vals) < 64
            cmap[v] = len(consts_vals)
            consts_vals.append(v)
        i = cmap[v]
        return CONSTS[:, i:i + 1]

    gammas = 1.0 - np.exp(np.linspace(np.log(1 / 32), np.log(1 / 512), NH))
    GC = [float(g ** 128) for g in gammas]

    def dma(q, sem, out, in_, transpose=False):
        sem = P.canon(sem)
        tok = P.inc(sem, 16)
        P.trace[q].append(("i", sem, 16))
        def f(e, out=out, in_=in_, tp=transpose, sem=sem):
            if tp:
                e.dma_start(out=out, in_=in_, transpose=True).then_inc(e._sems[sem], 16)
            else:
                e.dma_start(out=out, in_=in_).then_inc(e._sems[sem], 16)
        P.emit(q, f)
        return tok

    def icall(sem, out, out_off, in_, in_off, bound, elem_off=0):
        sem = P.canon(sem)
        tok = P.inc(sem, 16)
        P.trace["gpsimd"].append(("i", sem, 16))
        def f(e, out=out, oo=out_off, in_=in_, io=in_off, b=bound, eo=elem_off,
              sem=sem):
            e.indirect_dma_start(out, oo, in_, io, element_offset=eo,
                                 bounds_check=e._regs[b],
                                 oob_is_err=False).then_inc(e._sems[sem], 16)
        P.emit("gpsimd", f)
        return tok

    def cop(eng, sem, fn, n=1, fence=True):
        sem = P.canon(sem)
        tok = P.inc(sem, n)
        P.trace[eng].append(("i", sem, n))
        def f(e, fn=fn, sem=sem, n=n):
            fn(e).then_inc(e._sems[sem], n)
        P.emit(eng, f)
        if fence and eng in ("vector", "gpsimd", "scalar"):
            P.wait(eng, sem, tok)
        return tok

    def coll(sem, kind, ins, outs):
        sem = P.canon(sem)
        tok = P.inc(sem, 1)
        P.trace["gpsimd"].append(("i", sem, 1))
        def f(e, kind=kind, ins=ins, outs=outs, sem=sem):
            e.collective_compute(kind, ALU.bypass,
                                 replica_groups=[list(range(NCORES))],
                                 ins=ins, outs=outs).then_inc(e._sems[sem], 1)
        P.emit("gpsimd", f)
        return tok

    def wslice(w, m, kchunks, width, dst, dstcol, ncol=None):
        if ncol is None:
            ncol = w.tensor.shape[1]
        src = _ap(w, m * width, [[ncol, 128], [128 * ncol, kchunks], [1, width]])
        return dma("gpsimd", "wsl", dst[:, dstcol:dstcol + kchunks * width], src)

    for dst, src in [
        (CONSTS[:], consts_i), (ONES1[:], ones1_i), (ONESC[:], onesc_i),
        (TRIU_I[:], triu_ii), (TRIU_E[:], triu_ei), (IOTA8[:], iota8_i),
        (IOTARC[:], iotarc_i), (IDENT[:], ident_i), (IDENTF[:], identf_i),
        (RROT[:], rrot_i), (MASKT[:], maskt_i),
        (GPOW[:], gpow_i), (GKLC[:], gklc_i), (GINV[:], ginv_i),
        (GNW[:], gnw_i), (GNB[:], gnb_i), (JONE[:], jone_i),
        (WSPAN[:], wspan_i), (SC[:, 0:128].with_dtype(F32) if False else SC[:, 0:128], wgate),
    ]:
        dma("gpsimd", "boot", dst, src)
    P.wait("vector", "boot")
    cop("vector", "vb", lambda e: e.tensor_copy(WGH[:], SC[:, 0:128]))
    cop("vector", "vb", lambda e: e.tensor_copy(SC[:, 128:256], WGH[:]))
    cop("vector", "vb", lambda e: e.tensor_sub(SC[:, 256:384], SC[:, 0:128], SC[:, 128:256]))
    cop("vector", "vb", lambda e: e.tensor_copy(WGL[:], SC[:, 256:384]))
    cop("vector", "vb", lambda e: e.tensor_copy(ONESC_B[:], ONESC[:]))
    cop("vector", "vb", lambda e: e.memset(OOBT[:], SENT))
    cop("vector", "vb", lambda e: e.memset(BT1[:], 0.0))
    P.wait("gpsimd", "vb")
    for buf, nrow in ((a2a_s, NCORES * BLK), (a2a_r, NCORES * BLK),
                      (b2a_s, NCORES * BLK), (b2a_r, NCORES * BLK), (disp, CAP)):
        for c in range(0, nrow, 128):
            dma("gpsimd", "zero", buf[c:c + 128, :], BT1[:])
    P.barrier("boot")

    def ffn_pass(tag, xin_rows, xout_rows):
        trdone = {}

        def stopped(name):
            return STOP is not None and STOP == name

        V, SCL, TT_, GQ = "v" + tag, "s" + tag, "t" + tag, "g" + tag

        dma("gpsimd", "fl" + tag, _ap(mt_s, 0, [[128, 128], [1, 128]]), OOBT[:])
        dma("gpsimd", "fl" + tag, _ap(mt_s, 128 * 128, [[128, 128], [1, 128]]), OOBT[:])
        dma("gpsimd", "fl" + tag, _ap(backp, 0, [[128, 128], [1, 128]]),
            OOBT[:])

        xt_free = {}
        for c in range(TC):
            cs, cq, cv2, cr = c % 16, 16 + c % 16, 32 + c % 16, 48 + c % 16
            if c >= 1:
                P.wait("sync", V, xt_free[c - 1][0])
                P.wait("sync", SCL, xt_free[c - 1][1])
            ld = dma("sync", "xl" + tag, XT[:], xin_rows[c * 128:(c + 1) * 128, :])
            P.wait("vector", "xl" + tag, ld)
            cop("vector", V, lambda e, cs=cs: e.reduce_sum(
                STAT[:, cs:cs + 1], XT[:], axis=AXX), fence=False)
            P.wait("scalar", "xl" + tag, ld)
            t_sq = cop("scalar", SCL, lambda e, cq=cq: e.activation(
                DUM[:], XT[:], AF.Square, accum_out=STAT[:, cq:cq + 1]))
            if c >= 1:
                P.wait("vector", "xs" + tag, P.cur("xs" + tag) - 32)
            t_xc = cop("vector", V, lambda e: e.tensor_copy(SC[:], XT[:]),
                       fence=False)
            xt_free[c] = (t_xc, t_sq)
            P.wait("vector", SCL, t_sq)
            cop("vector", V, lambda e, cs=cs: e.tensor_scalar(
                STAT[:, cs:cs + 1], STAT[:, cs:cs + 1], ccol(1.0 / D), None,
                ALU.mult))
            cop("vector", V, lambda e, cs=cs, cv2=cv2: e.tensor_mul(
                STAT[:, cv2:cv2 + 1], STAT[:, cs:cs + 1], STAT[:, cs:cs + 1]))
            cop("vector", V, lambda e, cq=cq, cv2=cv2: e.scalar_tensor_tensor(
                STAT[:, cv2:cv2 + 1], STAT[:, cq:cq + 1], ccol(1.0 / D),
                STAT[:, cv2:cv2 + 1], ALU.mult, ALU.subtract))
            cop("vector", V, lambda e, cv2=cv2: e.tensor_scalar(
                STAT[:, cv2:cv2 + 1], STAT[:, cv2:cv2 + 1], ccol(EPS), None,
                ALU.add))
            t_rc = cop("vector", V, lambda e, cv2=cv2, cr=cr: e.reciprocal(
                STAT[:, cr:cr + 1], STAT[:, cv2:cv2 + 1]))
            P.wait("scalar", V, t_rc)
            t_rs = cop("scalar", SCL, lambda e, cr=cr: e.activation(
                STAT[:, cr:cr + 1], STAT[:, cr:cr + 1], AF.Sqrt))
            P.wait("vector", SCL, t_rs)
            cop("vector", V, lambda e, cs=cs, cr=cr: e.tensor_scalar(
                SC[:], SC[:], STAT[:, cs:cs + 1], STAT[:, cr:cr + 1],
                ALU.subtract, ALU.mult), fence=False)
            cop("vector", V, lambda e: e.tensor_copy(BT0[:], SC[:]), fence=False)
            t_n = cop("vector", V, lambda e: e.tensor_tensor(
                BT1[:], SC[:], BT0[:], ALU.subtract), fence=False)
            P.wait("sync", V, t_n)
            dma("sync", "xs" + tag, xlnr[c * 128:(c + 1) * 128, :], BT0[:])
            st = dma("sync", "xs" + tag, xlolo[c * 128:(c + 1) * 128, :], BT1[:])
        P.wait("vector", "xs" + tag, P.cur("xs" + tag))
        ln_done = P.cur("xs" + tag)
        if stopped("ln"):
            return None

        P.wait("sync", "xs" + tag, ln_done)
        P.wait("scalar", "xs" + tag, ln_done)
        for k in range(KC):
            dma("sync", "th" + tag, R1[:, k * T:(k + 1) * T],
                xlnr[:, k * 128:(k + 1) * 128], transpose=True)
        for k in range(KC):
            dma("sync", "tl" + tag, R2[:, k * T:(k + 1) * T],
                xlolo[:, k * 128:(k + 1) * 128], transpose=True)
        thi = P.cur("th" + tag)
        tlo = P.cur("tl" + tag)

        P.wait("tensor", "th" + tag, thi)
        P.wait("tensor", "tl" + tag, tlo)
        tp_ev = {}
        for c in range(TC):
            bk = c % 2
            if c >= 2:
                P.wait("tensor", V, tp_ev[c - 2])
            def rmm(e, c=c, bk=bk):
                o = PSB[0:128, bk * 512:bk * 512 + 8]
                last = None
                for src, wgt in ((R1, WGH), (R1, WGL), (R2, WGH)):
                    for k in range(KC):
                        first = (src is R1 and wgt is WGH and k == 0)
                        lastf = (src is R2 and k == KC - 1)
                        last = e.matmul(
                            o, src[:, k * T + c * 128:k * T + (c + 1) * 128],
                            wgt[:, k * 8:k * 8 + 8],
                            start=first, stop=lastf)
                return last
            t_mm = cop("tensor", TT_, rmm)
            P.wait("vector", TT_, t_mm)
            tp_ev[c] = cop("vector", V, lambda e, c=c, bk=bk: e.tensor_copy(
                LGT[:, 8 * bk:8 * bk + 8], PSB[0:128, bk * 512:bk * 512 + 8]))
            P.wait("sync", V, tp_ev[c])
            stg = dma("sync", "lg" + tag, lg_loc[c * 128:(c + 1) * 128, :],
                      LGT[:, 8 * bk:8 * bk + 8])
            P.wait("vector", "lg" + tag, stg - 16)
        P.wait("gpsimd", "lg" + tag, P.cur("lg" + tag))
        if stopped("rt"):
            return None
        cc_lg = coll("cl" + tag, "AllGather", [lg_loc], [lg_out])

        def h_phase(wA, wB, pref, first_bank_wait=None, per_m_hook=None):
            sil_tok = {}
            tt_tok = {}
            for m in range(MC):
                if per_m_hook is not None:
                    per_m_hook(m)
                buf = m % 2
                if m >= 2:
                    P.wait("gpsimd", pref + "tt", 4 * (m - 1))
                wslice(wA, m, KC, 128, WS1, buf * 2048)
                lB = wslice(wB, m, KC, 128, WS3, buf * 2048)
                P.wait("tensor", "wsl", lB)
                if m == 0 and first_bank_wait is not None:
                    first_bank_wait()
                if m >= 1:
                    for n in range(4):
                        P.wait("tensor", pref + "sl", sil_tok[(m - 1, n)])
                    P.wait("tensor", pref + "tt", tt_tok[(m - 1, 3)])
                def mmA(e, buf=buf):
                    last = None
                    for k in range(KC):
                        for n in range(4):
                            last = e.matmul(
                                psa(n),
                                WS1[:, buf * 2048 + k * 128:buf * 2048 + (k + 1) * 128],
                                R1[:, k * T + n * 512:k * T + n * 512 + 512],
                                start=(k == 0), stop=(k == KC - 1))
                    return last
                tA = cop("tensor", pref + "mm", mmA)
                for n in range(4):
                    P.wait("scalar", pref + "mm", tA)
                    sil_tok[(m, n)] = cop("scalar", pref + "sl",
                                          lambda e, n=n: e.activation(
                                              DUM[:, n * 512:(n + 1) * 512],
                                              psa(n), AF.Silu))
                P.wait("tensor", pref + "sl", sil_tok[(m, 1)])
                if m >= 1:
                    P.wait("tensor", pref + "tt", tt_tok[(m - 1, 1)])
                h3bank = [psb(0), psb(1), psa(0), psa(1)]
                def mmB(e, buf=buf, h3bank=h3bank):
                    last = None
                    for k in range(KC):
                        for n in range(4):
                            last = e.matmul(
                                h3bank[n],
                                WS3[:, buf * 2048 + k * 128:buf * 2048 + (k + 1) * 128],
                                R1[:, k * T + n * 512:k * T + n * 512 + 512],
                                start=(k == 0), stop=(k == KC - 1))
                    return last
                tB = cop("tensor", pref + "mm", mmB)
                for n in range(4):
                    P.wait("vector", pref + "mm", tB)
                    P.wait("vector", pref + "sl", sil_tok[(m, n)])
                    tt_tok[(m, n)] = cop("vector", pref + "tt", fence=False) if False else cop("vector", pref + "tt",
                                         lambda e, m=m, n=n, h3bank=h3bank:
                                         e.tensor_tensor(
                                             R2[:, m * T + n * 512:m * T + n * 512 + 512],
                                             DUM[:, n * 512:(n + 1) * 512],
                                             h3bank[n], ALU.mult))
            return tt_tok

        lread = {}
        f4a_state = {"started": False}

        def emit_f4a_chunk(g):
            if not f4a_state["started"]:
                P.wait("vector", "cl" + tag, cc_lg)
                P.wait("scalar", "lg" + tag, P.cur("lg" + tag))
                P.wait("scalar", "cl" + tag, cc_lg)
                f4a_state["started"] = True
            sl = 8 * (g % 2)
            if g >= 1:
                P.wait("scalar", V, lread[g - 1])
            ld = dma("scalar", "gl" + tag, LGT[:, sl:sl + 8],
                     lg_out[g * 128:(g + 1) * 128, :])
            P.wait("vector", "gl" + tag, ld)
            L = LGT[:, sl:sl + 8]
            O = OHB[:, 8 * g:8 * g + 8]
            cop("vector", V, lambda e, L=L: e.tensor_tensor(
                STAT[:, 28:32], L[:, 0:4], L[:, 4:8], ALU.max))
            cop("vector", V, lambda e: e.tensor_tensor(
                STAT[:, 28:30], STAT[:, 28:30], STAT[:, 30:32], ALU.max))
            cop("vector", V, lambda e: e.tensor_tensor(
                STAT[:, 26:27], STAT[:, 28:29], STAT[:, 29:30], ALU.max))
            lread[g] = cop("vector", V, lambda e, L=L, O=O: e.tensor_scalar(
                O, L, STAT[:, 26:27], None, ALU.is_equal))
            cop("vector", V, lambda e, O=O: e.tensor_mul(EQ[:, 16:24], O,
                                                         IOTA8[:]))
            cop("vector", V, lambda e: e.tensor_tensor(
                STAT[:, 28:32], EQ[:, 16:20], EQ[:, 20:24], ALU.min))
            cop("vector", V, lambda e: e.tensor_tensor(
                STAT[:, 28:30], STAT[:, 28:30], STAT[:, 30:32], ALU.min))
            cop("vector", V, lambda e: e.tensor_tensor(
                STAT[:, 27:28], STAT[:, 28:29], STAT[:, 29:30], ALU.min))
            cop("vector", V, lambda e, O=O: e.tensor_scalar(
                O, IOTA8[:], STAT[:, 27:28], None, ALU.is_equal))

        def f4a_hook(m):
            lo_g = m * 8
            hi_g = min(128, lo_g + 8)
            for g in range(lo_g, hi_g):
                emit_f4a_chunk(g)

        P.wait("vector", TT_, P.cur(TT_))
        h_phase(sw1, sw3, "sh" + tag, per_m_hook=f4a_hook)
        for g in range(8 * MC, 128):
            emit_f4a_chunk(g)
        f4a_done = P.cur(V)
        sh_tt_all = P.cur("sh" + tag + "tt")
        sh_mm_all = P.cur("sh" + tag + "mm")

        P.wait("tensor", V, f4a_done)
        P.wait("tensor", "sh" + tag + "tt", sh_tt_all)
        for g in range(128):
            cop("tensor", "gt" + tag, lambda e, g=g: e.matmul(
                PSB[0:1, 8 * (g % 64) + 512 * (g // 64):
                    8 * (g % 64) + 512 * (g // 64) + 8],
                ONESC_B[:], OHB[:, 8 * g:8 * g + 8],
                start=(g % 64 == 0), stop=(g % 64 == 63)))
        t_tot = P.cur("gt" + tag)
        P.wait("vector", "gt" + tag, t_tot)
        t_te = cop("vector", V, lambda e: e.tensor_copy(SC[0:1, 0:1024],
                                                        PSB[0:1, 0:1024]))
        P.wait("sync", V, t_te)
        tr1 = dma("sync", "tr" + tag, _ap(tot_rt, 0, [[1024, 1], [1, 1024]]),
                  SC[0:1, 0:1024])
        trdone["tok"] = tr1
        P.wait("sync", "tr" + tag, tr1)
        tr2 = dma("sync", "tr" + tag, TOTS[:], tot_rt)
        P.wait("tensor", "tr" + tag, tr2)
        P.wait("tensor", V, t_te)
        t_off = cop("tensor", "gt" + tag, lambda e: e.matmul(
            PSB[0:128, 0:8], TRIU_E[:], TOTS[:], start=True, stop=True))
        P.wait("vector", "gt" + tag, t_off)
        t_offe = cop("vector", V, lambda e: e.tensor_copy(OFFS[:],
                                                          PSB[0:128, 0:8]))
        P.wait("tensor", V, t_offe)
        t_co = cop("tensor", "gt" + tag, lambda e: e.matmul(
            PSB[0:1, 8:16], JONE[:], OFFS[:], start=True, stop=True))
        P.wait("vector", "gt" + tag, t_co)
        t_coe = cop("vector", V, lambda e: e.tensor_copy(LOC[0:1, 16:24],
                                                         PSB[0:1, 8:16]))
        P.wait("tensor", V, t_coe)
        t_cb = cop("tensor", "gt" + tag, lambda e: e.matmul(
            PSB[0:128, 8:16], ONES1[:], LOC[0:1, 16:24], start=True, stop=True))
        P.wait("vector", "gt" + tag, t_cb)
        t_coffe = cop("vector", V, lambda e: e.tensor_copy(COFF[:],
                                                           PSB[0:128, 8:16]))
        cop("vector", V, lambda e: e.memset(LOC[0:1, 0:8], 0.0))

        bpread = {}
        f5_tok = {}

        def emit_f5_chunk(lc):
            if lc == 0:
                P.wait("vector", "tr" + tag, trdone["tok"])
            if lc >= 1:
                P.wait("scalar", V, bpread[lc - 1])
            ld = dma("scalar", "ll" + tag, BP8[:],
                     lg_loc[lc * 128:(lc + 1) * 128, :])
            P.wait("vector", "ll" + tag, ld)
            cop("vector", V, lambda e: e.reduce_max(STAT[:, 8:9], BP8[:],
                                                    axis=AXX))
            cop("vector", V, lambda e: e.tensor_scalar(
                EQ[:, 24:32], BP8[:], STAT[:, 8:9], None, ALU.is_equal))
            cop("vector", V, lambda e: e.tensor_mul(SC[:, 16:24], EQ[:, 24:32],
                                                    IOTA8[:]))
            cop("vector", V, lambda e: e.tensor_reduce(
                STAT[:, 9:10], SC[:, 16:24], axis=AXX, op=ALU.min))
            t_oh = cop("vector", V, lambda e: e.tensor_scalar(
                EQ[:, 24:32], IOTA8[:], STAT[:, 9:10], None, ALU.is_equal))
            bpread[lc] = cop("vector", V, lambda e: e.tensor_scalar(
                SC[:, 0:8], BP8[:], STAT[:, 8:9], None, ALU.subtract))
            P.wait("scalar", V, P.cur(V))
            P.wait("scalar", "sh" + tag + "tt", min(4, sh_tt_all))
            t_e = cop("scalar", SCL, lambda e: e.activation(
                GN_S[:, 0:8], SC[:, 0:8], AF.Exp, accum_out=STAT[:, 10:11]))
            P.wait("vector", SCL, t_e)
            cop("vector", V, lambda e: e.reciprocal(STAT[:, 11:12],
                                                    STAT[:, 10:11]))
            P.wait("tensor", V, t_oh)
            if lc == 0:
                P.wait("tensor", V, t_coffe)
            bk = lc % 2
            t_pc = cop("tensor", "lt" + tag, lambda e, bk=bk: (
                e.matmul(PSB[0:1, 512 * bk + 24:512 * bk + 32], ONESC[:],
                         EQ[:, 24:32], start=True, stop=True),
                e.matmul(PSB[0:128, 512 * bk + 32:512 * bk + 40], TRIU_I[:],
                         EQ[:, 24:32], start=True, stop=False),
                e.matmul(PSB[0:128, 512 * bk + 32:512 * bk + 40], ONES1[:],
                         LOC[0:1, 0:8], start=False, stop=True))[-1])
            P.wait("vector", "lt" + tag, t_pc)
            cop("vector", V, lambda e, bk=bk: e.tensor_tensor(
                LOC[0:1, 0:8], LOC[0:1, 0:8],
                PSB[0:1, 512 * bk + 24:512 * bk + 32], ALU.add))
            cop("vector", V, lambda e, bk=bk: e.tensor_tensor(
                SC[:, 8:16], EQ[:, 24:32],
                PSB[0:128, 512 * bk + 32:512 * bk + 40], ALU.mult))
            cop("vector", V, lambda e: e.reduce_sum(STAT[:, 12:13], SC[:, 8:16],
                                                    axis=AXX))
            cop("vector", V, lambda e: e.tensor_mul(SC[:, 8:16], EQ[:, 24:32],
                                                    COFF[:]))
            cop("vector", V, lambda e: e.reduce_sum(STAT[:, 13:14], SC[:, 8:16],
                                                    axis=AXX))
            cop("vector", V, lambda e: e.tensor_add(STAT[:, 14:15],
                                                    STAT[:, 12:13],
                                                    STAT[:, 13:14]))
            cop("vector", V, lambda e: e.tensor_scalar(
                STAT[:, 15:16], STAT[:, 14:15], ccol(float(CAP)), None,
                ALU.is_le))
            cop("vector", V, lambda e: e.tensor_scalar(
                STAT[:, 17:18], STAT[:, 14:15], ccol(1.0), None, ALU.subtract))
            cop("vector", V, lambda e: e.scalar_tensor_tensor(
                STAT[:, 18:19], STAT[:, 17:18], ccol(SENT), STAT[:, 15:16],
                ALU.subtract, ALU.mult))
            cop("vector", V, lambda e, lc=lc: e.tensor_scalar(
                LSLOT[:, lc:lc + 1], STAT[:, 18:19], ccol(SENT), None, ALU.add))
            cop("vector", V, lambda e: e.tensor_scalar(
                STAT[:, 19:20], STAT[:, 9:10], ccol(512.0), None, ALU.mult))
            cop("vector", V, lambda e: e.scalar_tensor_tensor(
                STAT[:, 20:21], STAT[:, 12:13], ccol(4095.0), STAT[:, 19:20],
                ALU.add, ALU.add))
            cop("vector", V, lambda e: e.tensor_scalar(
                STAT[:, 21:22], STAT[:, 12:13], ccol(512.0), None, ALU.is_le))
            cop("vector", V, lambda e: e.scalar_tensor_tensor(
                STAT[:, 22:23], STAT[:, 20:21], ccol(SENT), STAT[:, 21:22],
                ALU.subtract, ALU.mult))
            cop("vector", V, lambda e, lc=lc: e.tensor_scalar(
                LSEND[:, lc:lc + 1], STAT[:, 22:23], ccol(SENT), None, ALU.add))
            f5_tok[lc] = cop("vector", V, lambda e, lc=lc: e.scalar_tensor_tensor(
                LGK[:, lc:lc + 1], STAT[:, 11:12], ccol(0.5), STAT[:, 15:16],
                ALU.mult, ALU.mult))

        if stopped("f5"):
            for lc in range(TC):
                emit_f5_chunk(lc)
            return None

        ds_tok = {}
        cp_tok = {}
        cp_ix = {}
        cc_a_box = {}

        def emit_f8_send(lc):
            ib = lc % 2
            buf = DUM[:] if ib == 0 else XT[:]
            P.wait("vector", V, f5_tok[lc])
            if lc >= 2:
                P.wait("vector", "ds" + tag, ds_tok[lc - 2])
                P.wait("scalar", "ds" + tag, ds_tok[lc - 2])
            t_ix = cop("vector", V, lambda e, lc=lc, ib=ib: e.tensor_copy(
                IDXS[ib][:], LSEND[:, lc:lc + 1]))
            lr = dma("scalar" if ib == 0 else "gpsimd", "dr" + tag, buf,
                     xlnr[lc * 128:(lc + 1) * 128, :])
            P.wait("gpsimd", V, t_ix)
            P.wait("gpsimd", "dr" + tag, lr)
            icall("ds" + tag, a2a_s, IndirectOffsetOnAxis(IDXS[ib][:], 0),
                  buf, None, 4095)
            ds_tok[lc] = icall("ds" + tag, mt_s,
                               IndirectOffsetOnAxis(IDXS[ib][:], 0),
                               LSLOT[:, lc:lc + 1], None, 4095)

        def emit_a2a():
            P.wait("gpsimd", "fl" + tag, P.cur("fl" + tag))
            P.wait("gpsimd", "ds" + tag, P.cur("ds" + tag))
            cc_a = coll("ca" + tag, "AllToAll", [a2a_s], [a2a_r])
            cc_a = coll("ca" + tag, "AllToAll", [mt_s], [mt_r])
            cc_a_box["t"] = cc_a

        def emit_f9_chunk(rc):
            ib = 2 + rc % 2
            buf = DUM[:] if rc % 2 == 0 else XT[:]
            if rc == 0:
                P.wait("scalar", "ca" + tag, cc_a_box["t"])
            if rc >= 2:
                P.wait("scalar", "cp" + tag, cp_tok[rc - 2])
                P.wait("vector", "cp" + tag, cp_tok[rc - 2])
            if rc >= 1:
                P.wait("scalar", V, cp_ix[rc - 1])
            ldm = dma("scalar", "mr" + tag, MR[:],
                      mt_r[rc * 128:(rc + 1) * 128, :])
            ldr = dma("scalar" if rc % 2 == 0 else "gpsimd", "rr" + tag, buf,
                      a2a_r[rc * 128:(rc + 1) * 128, :])
            P.wait("vector", "mr" + tag, ldm)
            t_ix = cop("vector", V, lambda e, ib=ib: e.tensor_copy(
                IDXS[ib][:], MR[:, 0:1]))
            cp_ix[rc] = t_ix
            P.wait("gpsimd", V, t_ix)
            P.wait("gpsimd", "rr" + tag, ldr)
            icall("cp" + tag, disp, IndirectOffsetOnAxis(IDXS[ib][:], 0),
                  buf, None, 2047)
            cp_tok[rc] = icall("cp" + tag, backp,
                               IndirectOffsetOnAxis(IDXS[ib][:], 0),
                               IOTARC[:, rc:rc + 1], None, 2047)

        P.wait("gpsimd", "sh" + tag + "mm", sh_mm_all)
        P.wait("tensor", "sh" + tag + "tt", sh_tt_all)
        sw2_ev = {}
        shs_tok = {}
        s2m_base = P.cur("s2m" + tag)
        for half in range(2):
            if half == 1:
                P.wait("gpsimd", "s2m" + tag, s2m_base + TC)
            dma("gpsimd", "w2l" + tag, R1[:, 0:MC * 1024],
                _ap(sw2, half * 1024, [[D, 128], [128 * D, MC], [1, 1024]]))
            wl = P.cur("w2l" + tag)
            P.wait("tensor", "w2l" + tag, wl)
            for m in range(TC):
                par = m % 2
                if m >= 2:
                    P.wait("tensor", "s2e" + tag, sw2_ev[(half, m - 2)])
                elif half == 1:
                    P.wait("tensor", "s2e" + tag, sw2_ev[(0, TC - 2 + m)])
                def mm2(e, m=m, par=par):
                    last = None
                    for k in range(MC):
                        for n in range(2):
                            last = e.matmul(
                                psa(par * 2 + n),
                                R2[:, k * T + m * 128:k * T + (m + 1) * 128],
                                R1[:, k * 1024 + n * 512:k * 1024 + (n + 1) * 512],
                                start=(k == 0), stop=(k == MC - 1))
                    return last
                t_m = cop("tensor", "s2m" + tag, mm2)
                P.wait("vector", "s2m" + tag, t_m)
                if (half, m - 2) in shs_tok:
                    P.wait("vector", "shs" + tag, shs_tok[(half, m - 2)])
                elif half == 1 and (0, TC - 2 + m) in shs_tok:
                    P.wait("vector", "shs" + tag, shs_tok[(0, TC - 2 + m)])
                bt = BT0 if par == 0 else BT1
                cop("vector", V, lambda e, par=par, bt=bt: e.tensor_copy(
                    bt[:, 0:512], psa(par * 2)))
                t_ev = cop("vector", "s2e" + tag,
                           lambda e, par=par, bt=bt: e.tensor_copy(
                               bt[:, 512:1024], psa(par * 2 + 1)))
                sw2_ev[(half, m)] = t_ev
                P.wait("sync", "s2e" + tag, t_ev)
                shs_tok[(half, m)] = dma(
                    "sync", "shs" + tag,
                    _ap(shr, m * 128 * D + half * 1024, [[D, 128], [1, 1024]]),
                    bt[:, 0:1024])
                if half == 0:
                    emit_f5_chunk(m)
                    emit_f8_send(m)
                    if m == TC - 1:
                        emit_a2a()
                else:
                    emit_f9_chunk(2 * m)
                    emit_f9_chunk(2 * m + 1)

        compact_done = P.cur("cp" + tag)
        if stopped("sh"):
            return None
        if stopped("disp"):
            return None

        P.wait("sync", "s2m" + tag, P.cur("s2m" + tag))
        P.wait("sync", "cp" + tag, compact_done)
        for k in range(KC):
            dma("sync", "xd" + tag, R1[:, k * T:(k + 1) * T],
                disp[:, k * 128:(k + 1) * 128], transpose=True)
        P.wait("tensor", "xd" + tag, P.cur("xd" + tag))
        P.wait("vector", "s2m" + tag, P.cur("s2m" + tag))
        s2e_all = P.cur("s2e" + tag)

        def exp_bank_wait():
            P.wait("tensor", "s2e" + tag, s2e_all)
        h_phase(w1, w3, "ex" + tag, first_bank_wait=exp_bank_wait)
        ex_tt_all = P.cur("ex" + tag + "tt")
        ex_mm_all = P.cur("ex" + tag + "mm")

        P.wait("gpsimd", "ex" + tag + "mm", ex_mm_all)
        e2m_base = P.cur("e2m" + tag)
        e2s_tok = {}
        e2_ix = {"last": None}
        for half in range(2):
            if half == 1:
                P.wait("gpsimd", "e2m" + tag, e2m_base + TC)
            dma("gpsimd", "e2l" + tag, R1[:, 0:MC * 1024],
                _ap(w2, half * 1024, [[D, 128], [128 * D, MC], [1, 1024]]))
            wl = P.cur("e2l" + tag)
            P.wait("tensor", "e2l" + tag, wl)
            for m in range(TC):
                par = m % 2
                ib = par
                if m >= 2:
                    P.wait("tensor", "e2s" + tag, e2s_tok[(half, m - 2)])
                elif half == 1:
                    P.wait("tensor", "e2s" + tag, e2s_tok[(0, TC - 2 + m)])
                if m == 0 and half == 0:
                    P.wait("tensor", "ex" + tag + "tt", ex_tt_all)
                def mm2(e, m=m, par=par):
                    last = None
                    for k in range(MC):
                        for n in range(2):
                            last = e.matmul(
                                psa(par * 2 + n),
                                R2[:, k * T + m * 128:k * T + (m + 1) * 128],
                                R1[:, k * 1024 + n * 512:k * 1024 + (n + 1) * 512],
                                start=(k == 0), stop=(k == MC - 1))
                    return last
                t_m = cop("tensor", "e2m" + tag, mm2)
                if e2_ix.get("last") is not None:
                    P.wait("scalar", V, e2_ix["last"])
                ldb = dma("scalar", "bp" + tag, BP8[:],
                          backp[m * 128:(m + 1) * 128, :])
                P.wait("vector", "bp" + tag, ldb)
                P.wait("vector", "e2m" + tag, t_m)
                if (half, m - 2) in e2s_tok:
                    P.wait("vector", "e2s" + tag, e2s_tok[(half, m - 2)])
                elif half == 1 and (0, TC - 2 + m) in e2s_tok:
                    P.wait("vector", "e2s" + tag, e2s_tok[(0, TC - 2 + m)])
                bt = BT0 if par == 0 else BT1
                cop("vector", V, lambda e, par=par, bt=bt: e.tensor_copy(
                    bt[:, 0:512], psa(par * 2)))
                cop("vector", V, lambda e, par=par, bt=bt: e.tensor_copy(
                    bt[:, 512:1024], psa(par * 2 + 1)))
                t_ix = cop("vector", V, lambda e, ib=ib: e.tensor_copy(
                    IDXS[ib][:], BP8[:, 0:1]))
                e2_ix["last"] = t_ix
                P.wait("gpsimd", V, t_ix)
                e2s_tok[(half, m)] = icall(
                    "e2s" + tag, b2a_s, IndirectOffsetOnAxis(IDXS[ib][:], 0),
                    bt[:, 0:1024], None, 4095, elem_off=half * 1024)
        P.wait("gpsimd", "e2s" + tag, P.cur("e2s" + tag))
        if stopped("exp"):
            return None
        cc_b = coll("cb" + tag, "AllToAll", [b2a_s], [b2a_r])

        P.wait("vector", "cb" + tag, cc_b)
        co_tok = {}
        for lc in range(TC):
            ib = lc % 2
            if lc >= 2:
                P.wait("gpsimd", V, co_tok[lc - 2])
            cop("vector", V, lambda e, lc=lc: e.tensor_scalar(
                STAT[:, 24:25], LSEND[:, lc:lc + 1], ccol(4095.0), None,
                ALU.min))
            t_ix = cop("vector", V, lambda e, ib=ib: e.tensor_copy(
                IDXS[ib][:], STAT[:, 24:25]))
            P.wait("gpsimd", V, t_ix)
            bt = BT0 if ib == 0 else BT1
            t_g = icall("cg" + tag, bt[:], None, b2a_r,
                        IndirectOffsetOnAxis(IDXS[ib][:], 0), 4095)
            lds = dma("scalar", "s2" + tag, DUM[:],
                      shr[lc * 128:(lc + 1) * 128, :])
            if lc >= 1:
                P.wait("sync", "xo" + tag, P.cur("xo" + tag))
            ldx = dma("sync", "xc" + tag, XT[:],
                      xin_rows[lc * 128:(lc + 1) * 128, :])
            P.wait("vector", "cg" + tag, t_g)
            P.wait("vector", "s2" + tag, lds)
            P.wait("vector", "xc" + tag, ldx)
            cop("vector", V, lambda e, bt=bt, lc=lc: e.tensor_scalar(
                SC[:], bt[:], LGK[:, lc:lc + 1], None, ALU.mult))
            cop("vector", V, lambda e: e.scalar_tensor_tensor(
                SC[:], DUM[:], ccol(0.5), SC[:], ALU.mult, ALU.add))
            t_o = cop("vector", V, lambda e: e.tensor_add(XT[:], SC[:], XT[:]))
            co_tok[lc] = t_o
            P.wait("sync", V, t_o)
            dma("sync", "xo" + tag, xout_rows[lc * 128:(lc + 1) * 128, :], XT[:])
            P.wait("scalar", V, t_o)
        return P.cur("xo" + tag)

    def attn_pass(xin_rows, xout_rows):
        tag = "at"
        V, SCL, TT_ = "vat", "sat", "tat"

        def stopped(name):
            return STOP is not None and STOP == name

        xt_free = {}
        for c in range(TC):
            cs, cq, cv2, cr = c % 16, 16 + c % 16, 32 + c % 16, 48 + c % 16
            if c >= 1:
                P.wait("sync", V, xt_free[c - 1][0])
                P.wait("sync", SCL, xt_free[c - 1][1])
            ld = dma("sync", "xl" + tag, XT[:], xin_rows[c * 128:(c + 1) * 128, :])
            P.wait("vector", "xl" + tag, ld)
            cop("vector", V, lambda e, cs=cs: e.reduce_sum(
                STAT[:, cs:cs + 1], XT[:], axis=AXX), fence=False)
            P.wait("scalar", "xl" + tag, ld)
            t_sq = cop("scalar", SCL, lambda e, cq=cq: e.activation(
                DUM[:], XT[:], AF.Square, accum_out=STAT[:, cq:cq + 1]))
            if c >= 1:
                P.wait("vector", "xs" + tag, P.cur("xs" + tag) - 16)
            t_xc = cop("vector", V, lambda e: e.tensor_copy(SC[:], XT[:]),
                       fence=False)
            xt_free[c] = (t_xc, t_sq)
            P.wait("vector", SCL, t_sq)
            cop("vector", V, lambda e, cs=cs: e.tensor_scalar(
                STAT[:, cs:cs + 1], STAT[:, cs:cs + 1], ccol(1.0 / D), None,
                ALU.mult))
            cop("vector", V, lambda e, cs=cs, cv2=cv2: e.tensor_mul(
                STAT[:, cv2:cv2 + 1], STAT[:, cs:cs + 1], STAT[:, cs:cs + 1]))
            cop("vector", V, lambda e, cq=cq, cv2=cv2: e.scalar_tensor_tensor(
                STAT[:, cv2:cv2 + 1], STAT[:, cq:cq + 1], ccol(1.0 / D),
                STAT[:, cv2:cv2 + 1], ALU.mult, ALU.subtract))
            cop("vector", V, lambda e, cv2=cv2: e.tensor_scalar(
                STAT[:, cv2:cv2 + 1], STAT[:, cv2:cv2 + 1], ccol(EPS), None,
                ALU.add))
            t_rc = cop("vector", V, lambda e, cv2=cv2, cr=cr: e.reciprocal(
                STAT[:, cr:cr + 1], STAT[:, cv2:cv2 + 1]))
            P.wait("scalar", V, t_rc)
            t_rs = cop("scalar", SCL, lambda e, cr=cr: e.activation(
                STAT[:, cr:cr + 1], STAT[:, cr:cr + 1], AF.Sqrt))
            P.wait("vector", SCL, t_rs)
            t_n = cop("vector", V, lambda e, cs=cs, cr=cr: e.tensor_scalar(
                BT0[:], SC[:], STAT[:, cs:cs + 1], STAT[:, cr:cr + 1],
                ALU.subtract, ALU.mult), fence=False)
            P.wait("sync", V, t_n)
            st = dma("sync", "xs" + tag, xlnr[c * 128:(c + 1) * 128, :], BT0[:])
            P.wait("vector", "xs" + tag, st - 16)
        P.wait("sync", "xs" + tag, P.cur("xs" + tag))
        for k in range(KC):
            dma("sync", "th" + tag, R1[:, k * T:(k + 1) * T],
                xlnr[:, k * 128:(k + 1) * 128], transpose=True)
        thi = P.cur("th" + tag)
        dma("gpsimd", "tb" + tag, R2[:, A_T1:A_T1 + T], ck_t)
        dma("gpsimd", "tb" + tag, R2[:, A_T2:A_T2 + T], sk_t)
        P.wait("tensor", "th" + tag, thi)
        P.wait("vector", "tb" + tag, P.cur("tb" + tag))
        if stopped("aln"):
            return None

        def proj_rot_xpos(wsrc, h, dst_col, store_ch=None, store_to=None):
            P.wait("gpsimd", TT_, P.cur(TT_))
            lw = dma("gpsimd", "wsl", WS1[:, 0:2048],
                     _ap(wsrc, h * D * DH, [[DH, 128], [128 * DH, KC], [1, 128]]))
            P.wait("tensor", "wsl", lw)
            P.wait("tensor", V, P.cur(V))
            def mmP(e):
                last = None
                for k in range(KC):
                    for n in range(4):
                        last = e.matmul(
                            psa(n), WS1[:, k * 128:(k + 1) * 128],
                            R1[:, k * T + n * 512:k * T + n * 512 + 512],
                            start=(k == 0), stop=(k == KC - 1))
                return last
            t_p = cop("tensor", TT_, mmP)
            for n in range(4):
                P.wait("vector", TT_, t_p)
                if store_ch is not None:
                    P.wait("vector", store_ch, P.cur(store_ch))
                cop("vector", V, lambda e, n=n, dst_col=dst_col: e.tensor_copy(
                    R2[:, dst_col + n * 512:dst_col + (n + 1) * 512], psa(n)))
            t_raw = P.cur(V)
            for n in range(4):
                bk = n % 2
                P.wait("tensor", V, t_raw if n < 2 else P.cur(V))
                t_r = cop("tensor", TT_, lambda e, n=n, bk=bk, dst_col=dst_col:
                          e.matmul(psb(bk), RROT[:],
                                   R2[:, dst_col + n * 512:dst_col + (n + 1) * 512],
                                   start=True, stop=True))
                P.wait("vector", TT_, t_r)
                cop("vector", V, lambda e, n=n, bk=bk: e.tensor_tensor(
                    SC[:, 0:512], psb(bk),
                    R2[:, A_T2 + n * 512:A_T2 + (n + 1) * 512], ALU.mult))
                cop("vector", V, lambda e, n=n, dst_col=dst_col: e.tensor_tensor(
                    SC[:, 512:1024],
                    R2[:, dst_col + n * 512:dst_col + (n + 1) * 512],
                    R2[:, A_T1 + n * 512:A_T1 + (n + 1) * 512], ALU.mult))
                cop("vector", V, lambda e, n=n, dst_col=dst_col: e.tensor_tensor(
                    R2[:, dst_col + n * 512:dst_col + (n + 1) * 512],
                    SC[:, 0:512], SC[:, 512:1024], ALU.add))
            if store_to is not None:
                P.wait("sync", V, P.cur(V))
                dma("sync", store_ch, store_to, R2[:, dst_col:dst_col + T])
            return P.cur(V)

        for h in range(NH):
            proj_rot_xpos(wk, h, A_K, store_ch="kx" + tag,
                          store_to=kxf[h * 128:(h + 1) * 128, :])
            for t in range(TC):
                pt = PTB[t % 2]
                P.wait("tensor", V, P.cur(V))
                t_t = cop("tensor", TT_, lambda e, t=t, pt=pt: e.transpose(
                    pt, R2[:, A_K + t * 128:A_K + (t + 1) * 128], IDENT[:]))
                P.wait("vector", TT_, t_t)
                if t >= 2:
                    P.wait("vector", "ks" + tag, P.cur("ks" + tag))
                cop("vector", V, lambda e, t=t, pt=pt, h=h: e.tensor_scalar(
                    R2[:, A_KS + t * 128:A_KS + (t + 1) * 128], pt,
                    GKLC[:, h:h + 1], None, ALU.mult))
            P.wait("sync", V, P.cur(V))
            ks_st = dma("sync", "ks" + tag,
                        _ap(kst, h * 128, [[D, 128], [128 * D, TC], [1, 128]]),
                        R2[:, A_KS:A_KS + T])
            P.wait("gpsimd", TT_, P.cur(TT_))
            lw = dma("gpsimd", "wsl", WS3[:, 0:2048],
                     _ap(wv, h * 128, [[D, 128], [128 * D, KC], [1, 128]]))
            P.wait("tensor", "wsl", lw)
            P.wait("tensor", V, P.cur(V))
            def mmV(e):
                last = None
                for k in range(KC):
                    for n in range(4):
                        last = e.matmul(
                            psa(n), WS3[:, k * 128:(k + 1) * 128],
                            R1[:, k * T + n * 512:k * T + n * 512 + 512],
                            start=(k == 0), stop=(k == KC - 1))
                return last
            t_pv = cop("tensor", TT_, mmV)
            for n in range(4):
                P.wait("vector", TT_, t_pv)
                P.wait("vector", "kx" + tag, P.cur("kx" + tag))
                cop("vector", V, lambda e, n=n: e.tensor_copy(
                    R2[:, A_K + n * 512:A_K + (n + 1) * 512], psa(n)))
            for t in range(TC):
                pt = PTB[t % 2]
                P.wait("tensor", V, P.cur(V))
                t_t = cop("tensor", TT_, lambda e, t=t, pt=pt: e.transpose(
                    pt, R2[:, A_K + t * 128:A_K + (t + 1) * 128], IDENT[:]))
                P.wait("vector", TT_, t_t)
                if t >= 2:
                    P.wait("vector", "vs" + tag, P.cur("vs" + tag))
                cop("vector", V, lambda e, t=t, pt=pt: e.tensor_copy(
                    R2[:, A_VS + t * 128:A_VS + (t + 1) * 128], pt))
            P.wait("sync", V, P.cur(V))
            dma("sync", "vs" + tag,
                _ap(vst, h * 128, [[D, 128], [128 * D, TC], [1, 128]]),
                R2[:, A_VS:A_VS + T])
            P.wait("tensor", V, P.cur(V))
            tc_tok = {}
            for t in range(TC):
                tc_tok[t] = cop("tensor", TT_, lambda e, t=t: e.matmul(
                    PSA[:, t * 128:(t + 1) * 128],
                    R2[:, A_KS + t * 128:A_KS + (t + 1) * 128],
                    R2[:, A_VS + t * 128:A_VS + (t + 1) * 128],
                    start=True, stop=True))
            cop("vector", V, lambda e: e.memset(S_F[:], 0.0))
            for t in range(TC):
                P.wait("vector", TT_, tc_tok[min(t | 3, TC - 1)])
                cop("vector", V, lambda e, t=t, h=h: e.scalar_tensor_tensor(
                    S_F[:], S_F[:], ccol(GC[h]),
                    PSA[:, t * 128:(t + 1) * 128], ALU.mult, ALU.add))
            P.wait("sync", V, P.cur(V))
            if h < 8:
                dma("sync", "st" + tag, st_in[h * 128:(h + 1) * 128, :], S_F[:])
            else:
                dma("sync", "st" + tag,
                    st_in_b[(h - 8) * 128:(h - 7) * 128, :], S_F[:])
            P.wait("tensor", V, P.cur(V))
            if h == 7:
                P.wait("gpsimd", "st" + tag, P.cur("st" + tag))
                cc_st_a = coll("cs" + tag, "AllGather",
                               [st_in[0:1024, :]], [st_out[0:8192, :]])
        P.wait("gpsimd", "st" + tag, P.cur("st" + tag))
        if stopped("asum"):
            return None
        cc_st_b = coll("cs" + tag, "AllGather", [st_in_b], [st_out_b])

        P.wait("gpsimd", V, P.cur(V))
        dma("gpsimd", "tb2" + tag, R2[:, A_T1:A_T1 + T], cq_t)
        dma("gpsimd", "tb2" + tag, R2[:, A_T2:A_T2 + T], sq_t)
        P.wait("vector", "tb2" + tag, P.cur("tb2" + tag))

        for h in range(NH):
            proj_rot_xpos(wq, h, A_Q)
            P.wait("sync", TT_, P.cur(TT_))
            ldk = dma("sync", "mk" + tag, R2[:, A_K:A_K + T],
                      kxf[h * 128:(h + 1) * 128, :])
            dma("sync", "mk" + tag, R2[:, A_KS:A_KS + T],
                _ap(kst, h * 128, [[D, 128], [128 * D, TC], [1, 128]]))
            dma("sync", "mk" + tag, R2[:, A_VS:A_VS + T],
                _ap(vst, h * 128, [[D, 128], [128 * D, TC], [1, 128]]))
            cop("vector", V, lambda e: e.memset(S_F[:], 0.0))
            P.wait("scalar", "cs" + tag, cc_st_a if h < 8 else cc_st_b)
            st_src = st_out if h < 8 else st_out_b
            hh = h if h < 8 else h - 8
            for kk in range(NCORES):
                P.wait("scalar", V, P.cur(V))
                ldm = dma("scalar", "ms" + tag, MTILE[:],
                          st_src[kk * 8 * 128 + hh * 128:
                                 kk * 8 * 128 + (hh + 1) * 128, :])
                P.wait("vector", "ms" + tag, ldm)
                cop("vector", V, lambda e, h=h, kk=kk: e.scalar_tensor_tensor(
                    S_F[:], MTILE[:], WSPAN[:, 8 * h + kk:8 * h + kk + 1],
                    S_F[:], ALU.mult, ALU.add))
            sb_buf = [S_B, S_B2]
            ar_buf = [ARES, ARES2]
            t_sb = {}
            t_sb[-1] = cop("vector", V, lambda e: e.tensor_copy(S_B[:], S_F[:]))
            P.wait("tensor", "mk" + tag, P.cur("mk" + tag))
            yev = {}
            ymm = {}
            armm = {}
            tc_tok = {}
            for t in range(TC):
                half = t // 8
                if t % 8 == 0:
                    P.wait("tensor", V, P.cur(V) if t == 0 else t_sb[7])
                    for u in range(8):
                        tt = half * 8 + u
                        tc_tok[tt] = cop("tensor", TT_, lambda e, tt=tt, u=u:
                                         e.matmul(
                            PSB[0:128, u * 128:(u + 1) * 128],
                            R2[:, A_KS + tt * 128:A_KS + (tt + 1) * 128],
                            R2[:, A_VS + tt * 128:A_VS + (tt + 1) * 128],
                            start=True, stop=True))
                P.wait("vector", TT_, tc_tok[(t // 8) * 8 + 7])
                if t >= 1:
                    P.wait("vector", TT_, ymm[t - 1])
                cop("vector", V, lambda e, h=h, u=t % 8:
                    e.scalar_tensor_tensor(
                        S_F[:], S_F[:], ccol(GC[h]),
                        PSB[0:128, u * 128:u * 128 + 128],
                        ALU.mult, ALU.add))
                t_sb[t] = cop("vector", V,
                              lambda e, t=t: e.tensor_copy(
                                  sb_buf[(t + 1) % 2][:], S_F[:]))
                bk = t % 2
                ybank = (t // 4) % 2
                abank = 2 + t % 2
                if t >= 2:
                    P.wait("tensor", V, armm[t - 2])
                t_a = cop("tensor", TT_, lambda e, t=t, abank=abank: e.matmul(
                    PSA[:, abank * 512:abank * 512 + 128],
                    R2[:, A_K + t * 128:A_K + (t + 1) * 128],
                    R2[:, A_Q + t * 128:A_Q + (t + 1) * 128],
                    start=True, stop=True))
                P.wait("vector", TT_, t_a)
                t_ar = cop("vector", V, lambda e, t=t, abank=abank, h=h:
                           e.scalar_tensor_tensor(
                               ar_buf[t % 2][:],
                               PSA[:, abank * 512:abank * 512 + 128],
                               GINV[:, h:h + 1], MASKT[:], ALU.mult, ALU.mult))
                armm[t] = t_ar
                P.wait("tensor", V, t_ar)
                P.wait("tensor", V, t_sb[t - 1])
                if t % 4 == 0 and t >= 8:
                    P.wait("tensor", V, yev[ybank])
                def mmy(e, t=t, ybank=ybank):
                    o = PSA[:, ybank * 512 + (t % 4) * 128:
                            ybank * 512 + (t % 4) * 128 + 128]
                    e.matmul(o, ar_buf[t % 2][:],
                             R2[:, A_VS + t * 128:A_VS + (t + 1) * 128],
                             start=True, stop=False)
                    return e.matmul(o, R2[:, A_Q + t * 128:A_Q + (t + 1) * 128],
                                    sb_buf[t % 2][:], start=False, stop=True)
                t_y = cop("tensor", TT_, mmy)
                ymm[t] = t_y
                if t % 4 == 3:
                    P.wait("vector", TT_, t_y)
                    q = t - 3
                    for dt_ in range(4):
                        tk = cop("vector", V, lambda e, q=q, dt_=dt_, h=h,
                                 ybank=ybank: e.tensor_scalar(
                                     R2[:, (q + dt_) * T + h * 128:
                                        (q + dt_) * T + (h + 1) * 128],
                                     PSA[:, ybank * 512 + dt_ * 128:
                                         ybank * 512 + dt_ * 128 + 128],
                                     GPOW[:, h:h + 1], None, ALU.mult))
                    yev[ybank] = tk
        heads_done = P.cur(V)
        if stopped("amain"):
            return None

        P.wait("scalar", V, heads_done)
        for tcn in range(TC):
            for h in range(NH):
                col = tcn * NH + h
                cop("vector", V, lambda e, tcn=tcn, h=h, col=col:
                    e.reduce_sum(GN_S[:, col:col + 1],
                                 R2[:, tcn * T + h * 128:
                                    tcn * T + (h + 1) * 128], axis=AXX),
                    fence=False)
                cop("scalar", SCL, lambda e, tcn=tcn, h=h, col=col:
                    e.activation(DUM[:, 0:128],
                                 R2[:, tcn * T + h * 128:tcn * T + (h + 1) * 128],
                                 AF.Square, accum_out=GN_S[:, 256 + col:257 + col]),
                    fence=False)
        P.wait("vector", SCL, P.cur(SCL))
        P.wait("vector", V, P.cur(V))
        cop("vector", V, lambda e: e.tensor_scalar(
            GN_S[:, 0:256], GN_S[:, 0:256], ccol(1.0 / 128), None, ALU.mult))
        cop("vector", V, lambda e: e.tensor_mul(
            SC[:, 0:256], GN_S[:, 0:256], GN_S[:, 0:256]))
        cop("vector", V, lambda e: e.scalar_tensor_tensor(
            GN_S[:, 256:512], GN_S[:, 256:512], ccol(1.0 / 128),
            SC[:, 0:256], ALU.mult, ALU.subtract))
        cop("vector", V, lambda e: e.tensor_scalar(
            GN_S[:, 256:512], GN_S[:, 256:512], ccol(EPS), None, ALU.add))
        t_rc = cop("vector", V, lambda e: e.reciprocal(SC[:, 256:512],
                                                       GN_S[:, 256:512]))
        P.wait("scalar", V, t_rc)
        t_rs = cop("scalar", SCL, lambda e: e.activation(
            SC[:, 256:512], SC[:, 256:512], AF.Sqrt))
        P.wait("vector", SCL, t_rs)
        for tcn in range(TC):
            for h in range(NH):
                col = tcn * NH + h
                cop("vector", V, lambda e, tcn=tcn, h=h, col=col:
                    e.tensor_scalar(
                        BT0[:, h * 128:(h + 1) * 128],
                        R2[:, tcn * T + h * 128:tcn * T + (h + 1) * 128],
                        GN_S[:, col:col + 1], SC[:, 256 + col:257 + col],
                        ALU.subtract, ALU.mult), fence=False)
            t_n = P.cur(V)
            P.wait("sync", V, t_n)
            st = dma("sync", "yn" + tag, ynr[tcn * 128:(tcn + 1) * 128, :],
                     BT0[:])
            P.wait("vector", "yn" + tag, st)
        yn_done = P.cur("yn" + tag)
        if stopped("agn"):
            return None

        P.wait("sync", "yn" + tag, yn_done)
        gw_tok = {}
        gmm_tok = {}
        P.wait("tensor", V, P.cur(V))
        for m in range(NH if GLIMIT is None else GLIMIT):
            if m >= 1:
                P.wait("sync", V, P.cur(V))
            tky = dma("sync", "yf" + tag, R2[:, A_K:A_K + T],
                      ynr[:, m * 128:(m + 1) * 128], transpose=True)
            if m >= 2:
                P.wait("gpsimd", TT_, gmm_tok[m - 1])
            lw = wslice(wgm, m, KC, 128, WS1, (m % 2) * 2048)
            P.wait("vector", "yf" + tag, tky)
            t_aff = cop("vector", V, lambda e, m=m: e.tensor_scalar(
                R2[:, A_K:A_K + T], R2[:, A_K:A_K + T], GNW[:, m:m + 1],
                GNB[:, m:m + 1], ALU.mult, ALU.add))
            P.wait("tensor", "wsl", lw)
            if m >= 1:
                P.wait("tensor", V, gw_tok[m - 1])
            def mmG(e, m=m):
                last = None
                for k in range(KC):
                    for n in range(4):
                        last = e.matmul(
                            psa(n), WS1[:, (m % 2) * 2048 + k * 128:
                                         (m % 2) * 2048 + (k + 1) * 128],
                            R1[:, k * T + n * 512:k * T + n * 512 + 512],
                            start=(k == 0), stop=(k == KC - 1))
                return last
            t_g = cop("tensor", TT_, mmG)
            gmm_tok[m] = t_g
            for n in range(4):
                P.wait("scalar", TT_, t_g)
                t_sil = cop("scalar", SCL, lambda e, n=n: e.activation(
                    DUM[:, n * 512:(n + 1) * 512], psa(n), AF.Silu))
                P.wait("vector", SCL, t_sil)
                P.wait("vector", V, t_aff)
                gw_tok[m] = cop("vector", V, lambda e, m=m, n=n: e.tensor_tensor(
                    R2[:, m * T + n * 512:m * T + (n + 1) * 512],
                    DUM[:, n * 512:(n + 1) * 512],
                    R2[:, A_K + n * 512:A_K + (n + 1) * 512], ALU.mult))
        gated_done = P.cur(V)
        if stopped("agt"):
            return None

        P.wait("gpsimd", TT_, P.cur(TT_))
        for k in range(KC):
            dma("gpsimd", "wo" + tag, R1[:, k * T:(k + 1) * T],
                _ap(wo, k * 128 * D, [[D, 128], [1, D]]))
        wol = P.cur("wo" + tag)
        P.wait("tensor", "wo" + tag, wol)
        P.wait("tensor", V, gated_done)
        wo_ev = {}
        for m in range(TC):
            par = m % 2
            banks = [psa(0), psa(1), psa(2), psa(3)] if par == 0 else \
                    [psb(0), psb(1), psa(0), psa(1)]
            if m >= 1:
                P.wait("tensor", "we" + tag, wo_ev[m - 1])
            def mmO(e, m=m, banks=banks):
                last = None
                for k in range(KC):
                    for n in range(4):
                        last = e.matmul(
                            banks[n],
                            R2[:, k * T + m * 128:k * T + (m + 1) * 128],
                            R1[:, k * T + n * 512:k * T + n * 512 + 512],
                            start=(k == 0), stop=(k == KC - 1))
                return last
            t_m = cop("tensor", "wm" + tag, mmO)
            if m >= 1:
                P.wait("sync", "xo" + tag, P.cur("xo" + tag))
            ldx = dma("sync", "xw" + tag, XT[:],
                      xin_rows[m * 128:(m + 1) * 128, :])
            P.wait("vector", "wm" + tag, t_m)
            P.wait("vector", "xw" + tag, ldx)
            for n in range(4):
                cop("vector", V, lambda e, n=n, banks=banks: e.tensor_tensor(
                    XT[:, n * 512:(n + 1) * 512], banks[n],
                    XT[:, n * 512:(n + 1) * 512], ALU.add))
            wo_ev[m] = cop("vector", "we" + tag, lambda e: e.nop())
            P.wait("sync", "we" + tag, wo_ev[m])
            dma("sync", "xo" + tag, xout_rows[m * 128:(m + 1) * 128, :], XT[:])
        return P.cur("xo" + tag)

    r1 = ffn_pass("p1", x0, xr1)
    fin = None
    if r1 is not None and STOP != "p1":
        P.barrier("after_p1")
        ra = attn_pass(xr1, xr2)
        if ra is not None and STOP != "at":
            P.barrier("after_at")
            fin = ffn_pass("p3", xr2, out_x3)
    if fin is None:
        P.barrier("trunc")
        for c in range(TC):
            dma("gpsimd", "xop3", out_x3[c * 128:(c + 1) * 128, :],
                x0[c * 128:(c + 1) * 128, :])
        fin = P.cur("xop3")

    if DBG:
        P.barrier("dbgbar")
        for name in DBG:
            for c in range(0, dramv[name].tensor.shape[0], 128):
                n = min(128, dramv[name].tensor.shape[0] - c)
                dma("gpsimd", "dbgc", dbg[name][c:c + n, :], dramv[name][c:c + n, :])
        P.wait("gpsimd", "dbgc", P.cur("dbgc"))

    P.wait("sync", "xo", fin)
    P.wait("gpsimd", "xo", fin)

    def _simulate():
        sems = defaultdict(int)
        pcs = {e: 0 for e in Prog.ENGS}
        progress = True
        while progress:
            progress = False
            for e in Prog.ENGS:
                tr = P.trace[e]
                while pcs[e] < len(tr):
                    kind, sem, val = tr[pcs[e]]
                    if kind == "w" and sems[sem] < val:
                        break
                    if kind == "i":
                        sems[sem] += val
                    pcs[e] += 1
                    progress = True
        stuck = {e: pcs[e] for e in Prog.ENGS if pcs[e] < len(P.trace[e])}
        if stuck:
            for e, pc in stuck.items():
                print(f"DEADLOCK {e} at {pc}/{len(P.trace[e])}: "
                      f"{P.trace[e][pc]}  sem={sems[P.trace[e][pc][1]]}")
            raise RuntimeError("deadlock in recorded schedule")
    _simulate()

    with es, nc.Block() as block:
        semctx = contextlib.ExitStack()
        sems = {n: semctx.enter_context(nc.semaphore(n))
                for n in sorted(P.semnames)}
        with semctx:
            def run(e, key):
                e._sems = sems
                if key == "gpsimd":
                    e._regs = {2047: e.to_reg(2047), 4095: e.to_reg(4095)}
                for f in P.ops[key]:
                    f(e)

            @block.sync
            def _(s):
                run(s, "sync")

            @block.scalar
            def _(s):
                run(s, "scalar")

            @block.vector
            def _(v):
                run(v, "vector")

            @block.tensor
            def _(t):
                run(t, "tensor")

            @block.gpsimd
            def _(g):
                run(g, "gpsimd")

    return nc


def _host_constants():
    perm = np.empty(DH, np.int64)
    perm[:64] = np.arange(64) * 2
    perm[64:] = np.arange(64) * 2 + 1
    c = {"perm": perm}
    c["triu_i"] = np.triu(np.ones((128, 128), np.float32))
    c["triu_e"] = np.triu(np.ones((128, 128), np.float32), 1)
    c["ident"] = np.eye(128, dtype=ml_dtypes.bfloat16)
    c["identf"] = np.eye(8, dtype=np.float32)
    R = np.zeros((128, 128), np.float32)
    for i in range(64):
        R[64 + i, i] = -1.0
        R[i, 64 + i] = 1.0
    c["rrot"] = R.astype(ml_dtypes.bfloat16)
    c["ones1"] = np.ones((1, 128), np.float32)
    c["onesc"] = np.ones((128, 1), np.float32)
    c["iota8m8"] = np.broadcast_to(np.arange(8) - 8.0, (128, 8)).astype(np.float32).copy()
    c["iotarc"] = (np.arange(128)[:, None] + 128.0 * np.arange(32)[None, :]).astype(np.float32)
    c["maskt"] = (np.arange(128)[None, :] >= np.arange(128)[:, None]).astype(ml_dtypes.bfloat16)
    g = 1.0 - np.exp(np.linspace(np.log(1 / 32), np.log(1 / 512), NH))
    c["gammas"] = g
    l = np.arange(128)[:, None].astype(np.float64)
    c["gpow"] = (g[None, :] ** l).astype(np.float32)
    c["gklc"] = (g[None, :] ** (128 - l)).astype(np.float32)
    c["ginv"] = (g[None, :] ** (-l)).astype(np.float32)
    return c


def _xpos_tables(span_start):
    d, half = DH, DH // 2
    pos = np.arange(span_start, span_start + T, dtype=np.float64)
    xscale = (np.arange(0, d, 2).astype(np.float64) + 0.4 * d) / (1.4 * d)
    scale = xscale[None, :] ** (pos[:, None] / 512.0)
    inv_freq = 1.0 / 10000.0 ** (np.arange(half).astype(np.float64) / half)
    sinu = pos[:, None] * inv_freq[None, :]
    sin, cos = np.sin(sinu), np.cos(sinu)

    def permfm(v):
        out = np.empty((128, T), np.float64)
        out[:64] = v.T
        out[64:] = v.T
        return out
    return (permfm(cos * scale).astype(np.float32),
            permfm(sin * scale).astype(np.float32),
            permfm(cos / scale).astype(np.float32),
            permfm(sin / scale).astype(np.float32))


def _prep_inputs(inp, consts_vals):
    hc = _host_constants()
    perm = hc["perm"]
    X = np.asarray(inp["L0_attn_X"], np.float32).reshape(-1, D)
    WQ = np.asarray(inp["L0_attn_W_Q"], np.float32)[:, :, perm].reshape(NH * D, DH)
    WK = np.asarray(inp["L0_attn_W_K"], np.float32)[:, :, perm].reshape(NH * D, DH)
    WV = np.transpose(np.asarray(inp["L0_attn_W_V"], np.float32), (1, 0, 2)).reshape(D, D)
    gnw_c = np.asarray(inp["L0_attn_gn_weight"], np.float32).reshape(NH, 128).T.copy()
    gnb_c = np.asarray(inp["L0_attn_gn_bias"], np.float32).reshape(NH, 128).T.copy()
    Wg = np.asarray(inp["L0_ffn_Wg"], np.float32)
    wgate = Wg.reshape(KC, 128, 8).transpose(1, 0, 2).reshape(128, 128).copy()
    consts_arr = np.zeros((128, 64), np.float32)
    for i, v in enumerate(consts_vals):
        consts_arr[:, i] = v
    gam = hc["gammas"]
    L = float(T)

    in_maps = []
    for j in range(NCORES):
        s, b = j % 4, j // 4
        cq, sq, ck, sk = _xpos_tables(s * T)
        jone = np.zeros((128, 1), np.float32)
        jone[16 * j, 0] = 1.0
        wspan = np.zeros((128, 128), np.float32)
        for h in range(NH):
            for k in range(NCORES):
                kb, ks = k // 4, k % 4
                if kb == b and ks < s:
                    wspan[:, 8 * h + k] = gam[h] ** (L * (s - 1 - ks))
        m = dict(
            x0=np.ascontiguousarray(X[j * T:(j + 1) * T]),
            wq=WQ, wk=WK, wv=WV,
            wgm=np.asarray(inp["L0_attn_W_G"], np.float32),
            wo=np.asarray(inp["L0_attn_W_O"], np.float32),
            gnw=gnw_c, gnb=gnb_c, wgate=wgate,
            w1=np.ascontiguousarray(np.asarray(inp["L0_ffn_W1"], np.float32)[j]),
            w3=np.ascontiguousarray(np.asarray(inp["L0_ffn_W3"], np.float32)[j]),
            w2=np.ascontiguousarray(np.asarray(inp["L0_ffn_W2"], np.float32)[j]),
            sw1=np.asarray(inp["L0_ffn_sw1"], np.float32),
            sw3=np.asarray(inp["L0_ffn_sw3"], np.float32),
            sw2=np.asarray(inp["L0_ffn_sw2"], np.float32),
            cq=cq, sq=sq, ck=ck, sk=sk,
            gpow=hc["gpow"], gklc=hc["gklc"], ginv=hc["ginv"],
            maskt=hc["maskt"], triu_i=hc["triu_i"], triu_e=hc["triu_e"],
            ident=hc["ident"], identf=hc["identf"], rrot=hc["rrot"],
            ones1=hc["ones1"], onesc=hc["onesc"], iota8m8=hc["iota8m8"],
            iotarc=hc["iotarc"], jone=jone, wspan=wspan, constsin=consts_arr,
        )
        in_maps.append(m)
    return in_maps


def kernel(**inputs):
    consts_vals = []
    with allow_ici_all_to_all():
        nc = build(consts_vals)
        in_maps = _prep_inputs(inputs, consts_vals)
        res = run_bass_kernel_spmd(nc, in_maps, core_ids=list(range(NCORES)))
    out = np.concatenate([np.asarray(r["out"], np.float32) for r in res.results], 0)
    return out.reshape(2, 8192, D)


# version 08
# speedup vs baseline: 1.0768x; 1.0111x over previous
import contextlib
from collections import defaultdict

import numpy as np
import ml_dtypes

import concourse.bass as bass
import concourse.mybir as mybir
from concourse.bass import AP, IndirectOffsetOnAxis
from concourse.bass_utils import run_bass_kernel_spmd
from concourse.aot_env import allow_ici_all_to_all

F32 = mybir.dt.float32
BF16 = mybir.dt.bfloat16
I32 = mybir.dt.int32
AF = mybir.ActivationFunctionType
ALU = mybir.AluOpType
AXX = mybir.AxisListType.X

NCORES = 8
D = 2048
HID = 2816
T = 2048
NH, DH = 16, 128
KC = D // 128
MC = HID // 128
TC = T // 128
CAP = 2048
BLK = 512
SENT = 30000.0
EPS = 1e-5

DBG = []
STOP = None
GLIMIT = None
STOP_AFTER = None


class Prog:
    ENGS = ("sync", "scalar", "vector", "tensor", "gpsimd")

    def __init__(self):
        self.ops = defaultdict(list)
        self.semval = defaultdict(int)
        self.lastwait = {}
        self.semnames = set()
        self.trace = defaultdict(list)

    @staticmethod
    def canon(name):
        for suf in ("p1", "p3", "at"):
            if name.endswith(suf) and len(name) > len(suf):
                return name[:-len(suf)]
        return name

    def emit(self, eng, fn):
        self.ops[eng].append(fn)

    def inc(self, name, n=1):
        name = self.canon(name)
        self.semnames.add(name)
        self.semval[name] += n
        return self.semval[name]

    def cur(self, name):
        name = self.canon(name)
        self.semnames.add(name)
        return self.semval[name]

    def wait(self, eng, name, val=None):
        name = self.canon(name)
        if val is None:
            val = self.semval[name]
        if val <= 0:
            return
        key = (eng, name)
        if self.lastwait.get(key, -1) >= val:
            return
        self.lastwait[key] = val
        self.semnames.add(name)

        def f(e, _n=name, _v=val):
            e.wait_ge(e._sems[_n], _v)
        self.emit(eng, f)
        self.trace[eng].append(("w", name, val))

    def barrier(self, tag):
        toks = {}
        for eng in self.ENGS:
            ch = "bar_" + eng
            toks[eng] = self.inc(ch, 1)
            def f(e, ch=ch):
                e.nop().then_inc(e._sems[ch], 1)
            self.emit(eng, f)
            self.trace[eng].append(("i", ch, 1))
        for eng in self.ENGS:
            for src in self.ENGS:
                if src != eng:
                    self.wait(eng, "bar_" + src, toks[src])


def _ap(t, offset, dims):
    return AP(t.tensor, offset, [list(d) for d in dims])


def build(consts_vals):
    nc = bass.Bass("TRN2", target_bir_lowering=False, debug=False)
    P = Prog()

    def ein(name, shape, dt=F32):
        return nc.dram_tensor(name, list(shape), dt, kind="ExternalInput").ap()

    x0 = ein("x0", (T, D))
    wq = ein("wq", (NH * D, DH))
    wk = ein("wk", (NH * D, DH))
    wv = ein("wv", (D, D))
    wgm = ein("wgm", (D, D))
    wo = ein("wo", (D, D))
    gnw_i = ein("gnw", (128, NH))
    gnb_i = ein("gnb", (128, NH))
    wgate = ein("wgate", (128, 128))
    w1 = ein("w1", (D, HID))
    w3 = ein("w3", (D, HID))
    w2 = ein("w2", (HID, D))
    sw1 = ein("sw1", (D, HID))
    sw3 = ein("sw3", (D, HID))
    sw2 = ein("sw2", (HID, D))
    cq_t = ein("cq", (128, T))
    sq_t = ein("sq", (128, T))
    ck_t = ein("ck", (128, T))
    sk_t = ein("sk", (128, T))
    gpow_i = ein("gpow", (128, NH))
    gklc_i = ein("gklc", (128, NH))
    ginv_i = ein("ginv", (128, NH))
    maskt_i = ein("maskt", (128, 128), BF16)
    triu_ii = ein("triu_i", (128, 128))
    triu_ei = ein("triu_e", (128, 128))
    ident_i = ein("ident", (128, 128), BF16)
    identf_i = ein("identf", (8, 8))
    rrot_i = ein("rrot", (128, 128), BF16)
    ones1_i = ein("ones1", (1, 128))
    onesc_i = ein("onesc", (128, 1))
    iota8_i = ein("iota8m8", (128, 8))
    iotarc_i = ein("iotarc", (128, 32))
    jone_i = ein("jone", (128, 1))
    wspan_i = ein("wspan", (128, 128))
    consts_i = ein("constsin", (128, 64))

    out_x3 = nc.dram_tensor("out", [T, D], F32, kind="ExternalOutput").ap()

    def din(name, shape, dt):
        return nc.dram_tensor(name, list(shape), dt, kind="Internal").ap()

    xr1 = din("xr1", (T, D), F32)
    xr2 = din("xr2", (T, D), F32)
    xlnr = din("xlnr", (T, D), BF16)
    xlolo = din("xlolo", (T, D), BF16)
    lg_loc = din("lg_loc", (T, 8), F32)
    lg_out = din("lg_out", (NCORES * T, 8), F32)
    tot_rt = din("tot_rt", (128, 8), F32)
    a2a_s = din("a2a_s", (NCORES * BLK, D), BF16)
    a2a_r = din("a2a_r", (NCORES * BLK, D), BF16)
    mt_s = din("mt_s", (NCORES * BLK, 8), F32)
    mt_r = din("mt_r", (NCORES * BLK, 8), F32)
    disp = din("disp", (CAP, D), BF16)
    backp = din("backp", (CAP, 8), F32)
    shr = din("shr", (T, D), BF16)
    b2a_s = din("b2a_s", (NCORES * BLK, D), BF16)
    b2a_r = din("b2a_r", (NCORES * BLK, D), BF16)
    kxf = din("kxf", (D, T), BF16)
    kst = din("kst", (T, D), BF16)
    vst = din("vst", (T, D), BF16)
    st_in = din("st_in", (NH * 128, 128), F32)
    st_out = din("st_out", (NCORES * NH * 128, 128), F32)
    st_in_b = din("st_in_b", (NH * 64, 128), F32)
    st_out_b = din("st_out_b", (NCORES * NH * 64, 128), F32)
    ynr = din("ynr", (T, D), BF16)
    dramv = dict(xr1=xr1, xr2=xr2, xlnr=xlnr, xlolo=xlolo, lg_loc=lg_loc,
                 lg_out=lg_out, tot_rt=tot_rt, a2a_s=a2a_s, a2a_r=a2a_r,
                 mt_s=mt_s, mt_r=mt_r, disp=disp, backp=backp, shr=shr,
                 b2a_s=b2a_s, b2a_r=b2a_r, kxf=kxf, kst=kst, vst=vst,
                 st_in=st_in, st_out=st_out, ynr=ynr)
    dbg = {}
    for name in DBG:
        src = dramv[name]
        dbg[name] = nc.dram_tensor("dbg_" + name, list(src.tensor.shape),
                                   src.tensor.dtype, kind="ExternalOutput").ap()

    es = contextlib.ExitStack()

    def sbt(name, shape, dt):
        return es.enter_context(nc.sbuf_tensor(name, list(shape), dt))

    R1 = sbt("R1", (128, 32768), BF16)
    R2 = sbt("R2", (128, 45056), BF16)
    WS1 = sbt("WS1", (128, 4096), BF16)
    WS3 = sbt("WS3", (128, 4096), BF16)
    DUM = sbt("DUM", (128, 2048), BF16)
    XT = sbt("XT", (128, 2048), F32)
    BT0 = sbt("BT0", (128, 2048), BF16)
    BT1 = sbt("BT1", (128, 2048), BF16)
    SC = sbt("SCR", (128, 2048), F32)
    GN_S = sbt("GN_S", (128, 512), F32)
    CONSTS = sbt("CONSTS", (128, 64), F32)
    STAT = sbt("STAT", (128, 64), F32)
    EQ = sbt("EQ", (128, 32), F32)
    OHB = sbt("OHB", (128, 1024), BF16)
    ONESC_B = sbt("ONESC_Bs", (128, 1), BF16)
    LGT = sbt("LGT", (128, 16), F32)
    TOTS = sbt("TOTS", (128, 8), F32)
    OFFS = sbt("OFFS", (128, 8), F32)
    COFF = sbt("COFF", (128, 8), F32)
    LOC = sbt("LOC", (1, 32), F32)
    LSLOT = sbt("LSLOT", (128, 16), F32)
    LSEND = sbt("LSEND", (128, 16), F32)
    LGK = sbt("LGK", (128, 16), F32)
    MR = sbt("MR", (128, 8), F32)
    BP8 = sbt("BP8", (128, 8), F32)
    IDXS = [sbt(f"IDXS{i}", (128, 1), I32) for i in range(4)]
    OOBT = sbt("OOBT", (128, 128), F32)
    WGH = sbt("WGH", (128, 128), BF16)
    WGL = sbt("WGL", (128, 128), BF16)
    TRIU_I = sbt("TRIU_Is", (128, 128), F32)
    TRIU_E = sbt("TRIU_Es", (128, 128), F32)
    IOTA8 = sbt("IOTA8s", (128, 8), F32)
    IOTARC = sbt("IOTARCs", (128, 32), F32)
    IDENT = sbt("IDENTs", (128, 128), BF16)
    IDENTF = sbt("IDENTFs", (8, 8), F32)
    RROT = sbt("RROTs", (128, 128), BF16)
    MASKT = sbt("MASKTs", (128, 128), BF16)
    GPOW = sbt("GPOWs", (128, NH), F32)
    GKLC = sbt("GKLCs", (128, NH), F32)
    GINV = sbt("GINVs", (128, NH), F32)
    GNW = sbt("GNWs", (128, NH), F32)
    GNB = sbt("GNBs", (128, NH), F32)
    JONE = sbt("JONEs", (128, 1), F32)
    WSPAN = sbt("WSPANs", (128, 128), F32)
    ONES1 = sbt("ONES1s", (1, 128), F32)
    ONESC = sbt("ONESCs", (128, 1), F32)
    ARES = sbt("ARES", (128, 128), BF16)
    ARES2 = sbt("ARES2", (128, 128), BF16)
    S_F = sbt("S_F", (128, 128), F32)
    S_B = sbt("S_B", (128, 128), BF16)
    S_B2 = sbt("S_B2", (128, 128), BF16)
    MTILE = sbt("MTILE", (128, 128), F32)

    A_Q, A_K, A_KS, A_VS, A_T1, A_T2 = 32768, 34816, 36864, 38912, 40960, 43008

    PSA = es.enter_context(nc.psum_tensor("PSA", [128, 2048], F32))
    PSB = es.enter_context(nc.psum_tensor("PSB", [128, 1024], F32))
    PTB_full = [es.enter_context(nc.psum_tensor(f"PTB{i}", [128, 1024], BF16))
                for i in range(2)]
    PTB = [t[:, 0:128] for t in PTB_full]

    def psa(i):
        return PSA[:, i * 512:(i + 1) * 512]

    def psb(i):
        return PSB[:, i * 512:(i + 1) * 512]

    cmap = {}

    def ccol(v):
        v = float(v)
        if v not in cmap:
            assert len(consts_vals) < 64
            cmap[v] = len(consts_vals)
            consts_vals.append(v)
        i = cmap[v]
        return CONSTS[:, i:i + 1]

    gammas = 1.0 - np.exp(np.linspace(np.log(1 / 32), np.log(1 / 512), NH))
    GC = [float(g ** 128) for g in gammas]

    def dma(q, sem, out, in_, transpose=False):
        sem = P.canon(sem)
        tok = P.inc(sem, 16)
        P.trace[q].append(("i", sem, 16))
        def f(e, out=out, in_=in_, tp=transpose, sem=sem):
            if tp:
                e.dma_start(out=out, in_=in_, transpose=True).then_inc(e._sems[sem], 16)
            else:
                e.dma_start(out=out, in_=in_).then_inc(e._sems[sem], 16)
        P.emit(q, f)
        return tok

    def icall(sem, out, out_off, in_, in_off, bound, elem_off=0):
        sem = P.canon(sem)
        tok = P.inc(sem, 16)
        P.trace["gpsimd"].append(("i", sem, 16))
        def f(e, out=out, oo=out_off, in_=in_, io=in_off, b=bound, eo=elem_off,
              sem=sem):
            e.indirect_dma_start(out, oo, in_, io, element_offset=eo,
                                 bounds_check=e._regs[b],
                                 oob_is_err=False).then_inc(e._sems[sem], 16)
        P.emit("gpsimd", f)
        return tok

    def cop(eng, sem, fn, n=1, fence=True):
        sem = P.canon(sem)
        tok = P.inc(sem, n)
        P.trace[eng].append(("i", sem, n))
        def f(e, fn=fn, sem=sem, n=n):
            fn(e).then_inc(e._sems[sem], n)
        P.emit(eng, f)
        if fence and eng in ("vector", "gpsimd", "scalar"):
            P.wait(eng, sem, tok)
        return tok

    def coll(sem, kind, ins, outs):
        sem = P.canon(sem)
        tok = P.inc(sem, 1)
        P.trace["gpsimd"].append(("i", sem, 1))
        def f(e, kind=kind, ins=ins, outs=outs, sem=sem):
            e.collective_compute(kind, ALU.bypass,
                                 replica_groups=[list(range(NCORES))],
                                 ins=ins, outs=outs).then_inc(e._sems[sem], 1)
        P.emit("gpsimd", f)
        return tok

    def wslice(w, m, kchunks, width, dst, dstcol, ncol=None):
        if ncol is None:
            ncol = w.tensor.shape[1]
        src = _ap(w, m * width, [[ncol, 128], [128 * ncol, kchunks], [1, width]])
        return dma("gpsimd", "wsl", dst[:, dstcol:dstcol + kchunks * width], src)

    for dst, src in [
        (CONSTS[:], consts_i), (ONES1[:], ones1_i), (ONESC[:], onesc_i),
        (TRIU_I[:], triu_ii), (TRIU_E[:], triu_ei), (IOTA8[:], iota8_i),
        (IOTARC[:], iotarc_i), (IDENT[:], ident_i), (IDENTF[:], identf_i),
        (RROT[:], rrot_i), (MASKT[:], maskt_i),
        (GPOW[:], gpow_i), (GKLC[:], gklc_i), (GINV[:], ginv_i),
        (GNW[:], gnw_i), (GNB[:], gnb_i), (JONE[:], jone_i),
        (WSPAN[:], wspan_i), (SC[:, 0:128].with_dtype(F32) if False else SC[:, 0:128], wgate),
    ]:
        dma("gpsimd", "boot", dst, src)
    P.wait("vector", "boot")
    cop("vector", "vb", lambda e: e.tensor_copy(WGH[:], SC[:, 0:128]))
    cop("vector", "vb", lambda e: e.tensor_copy(SC[:, 128:256], WGH[:]))
    cop("vector", "vb", lambda e: e.tensor_sub(SC[:, 256:384], SC[:, 0:128], SC[:, 128:256]))
    cop("vector", "vb", lambda e: e.tensor_copy(WGL[:], SC[:, 256:384]))
    cop("vector", "vb", lambda e: e.tensor_copy(ONESC_B[:], ONESC[:]))
    cop("vector", "vb", lambda e: e.memset(OOBT[:], SENT))
    cop("vector", "vb", lambda e: e.memset(BT1[:], 0.0))
    P.wait("gpsimd", "vb")
    for buf, nrow in ((a2a_s, NCORES * BLK), (a2a_r, NCORES * BLK),
                      (b2a_s, NCORES * BLK), (b2a_r, NCORES * BLK), (disp, CAP)):
        for c in range(0, nrow, 128):
            dma("gpsimd", "zero", buf[c:c + 128, :], BT1[:])
    P.barrier("boot")

    def ffn_pass(tag, xin_rows, xout_rows):
        trdone = {}

        def stopped(name):
            return STOP is not None and STOP == name

        V, SCL, TT_, GQ = "v" + tag, "s" + tag, "t" + tag, "g" + tag

        dma("gpsimd", "fl" + tag, _ap(mt_s, 0, [[128, 128], [1, 128]]), OOBT[:])
        dma("gpsimd", "fl" + tag, _ap(mt_s, 128 * 128, [[128, 128], [1, 128]]), OOBT[:])
        dma("gpsimd", "fl" + tag, _ap(backp, 0, [[128, 128], [1, 128]]),
            OOBT[:])

        xt_free = {}
        for c in range(TC):
            cs, cq, cv2, cr = c % 16, 16 + c % 16, 32 + c % 16, 48 + c % 16
            if c >= 1:
                P.wait("sync", V, xt_free[c - 1][0])
                P.wait("sync", SCL, xt_free[c - 1][1])
            ld = dma("sync", "xl" + tag, XT[:], xin_rows[c * 128:(c + 1) * 128, :])
            P.wait("vector", "xl" + tag, ld)
            cop("vector", V, lambda e, cs=cs: e.reduce_sum(
                STAT[:, cs:cs + 1], XT[:], axis=AXX), fence=False)
            P.wait("scalar", "xl" + tag, ld)
            t_sq = cop("scalar", SCL, lambda e, cq=cq: e.activation(
                DUM[:], XT[:], AF.Square, accum_out=STAT[:, cq:cq + 1]))
            if c >= 1:
                P.wait("vector", "xs" + tag, P.cur("xs" + tag) - 32)
            t_xc = cop("vector", V, lambda e: e.tensor_copy(SC[:], XT[:]),
                       fence=False)
            xt_free[c] = (t_xc, t_sq)
            P.wait("vector", SCL, t_sq)
            cop("vector", V, lambda e, cs=cs: e.tensor_scalar(
                STAT[:, cs:cs + 1], STAT[:, cs:cs + 1], ccol(1.0 / D), None,
                ALU.mult))
            cop("vector", V, lambda e, cs=cs, cv2=cv2: e.tensor_mul(
                STAT[:, cv2:cv2 + 1], STAT[:, cs:cs + 1], STAT[:, cs:cs + 1]))
            cop("vector", V, lambda e, cq=cq, cv2=cv2: e.scalar_tensor_tensor(
                STAT[:, cv2:cv2 + 1], STAT[:, cq:cq + 1], ccol(1.0 / D),
                STAT[:, cv2:cv2 + 1], ALU.mult, ALU.subtract))
            cop("vector", V, lambda e, cv2=cv2: e.tensor_scalar(
                STAT[:, cv2:cv2 + 1], STAT[:, cv2:cv2 + 1], ccol(EPS), None,
                ALU.add))
            t_rc = cop("vector", V, lambda e, cv2=cv2, cr=cr: e.reciprocal(
                STAT[:, cr:cr + 1], STAT[:, cv2:cv2 + 1]))
            P.wait("scalar", V, t_rc)
            t_rs = cop("scalar", SCL, lambda e, cr=cr: e.activation(
                STAT[:, cr:cr + 1], STAT[:, cr:cr + 1], AF.Sqrt))
            P.wait("vector", SCL, t_rs)
            cop("vector", V, lambda e, cs=cs, cr=cr: e.tensor_scalar(
                SC[:], SC[:], STAT[:, cs:cs + 1], STAT[:, cr:cr + 1],
                ALU.subtract, ALU.mult), fence=False)
            cop("vector", V, lambda e: e.tensor_copy(BT0[:], SC[:]), fence=False)
            t_n = cop("vector", V, lambda e: e.tensor_tensor(
                BT1[:], SC[:], BT0[:], ALU.subtract), fence=False)
            P.wait("sync", V, t_n)
            dma("sync", "xs" + tag, xlnr[c * 128:(c + 1) * 128, :], BT0[:])
            st = dma("sync", "xs" + tag, xlolo[c * 128:(c + 1) * 128, :], BT1[:])
        P.wait("vector", "xs" + tag, P.cur("xs" + tag))
        ln_done = P.cur("xs" + tag)
        if stopped("ln"):
            return None

        P.wait("sync", "xs" + tag, ln_done)
        P.wait("scalar", "xs" + tag, ln_done)
        for k in range(KC):
            dma("sync", "th" + tag, R1[:, k * T:(k + 1) * T],
                xlnr[:, k * 128:(k + 1) * 128], transpose=True)
        for k in range(KC):
            dma("sync", "tl" + tag, R2[:, k * T:(k + 1) * T],
                xlolo[:, k * 128:(k + 1) * 128], transpose=True)
        thi = P.cur("th" + tag)
        tlo = P.cur("tl" + tag)

        P.wait("tensor", "th" + tag, thi)
        P.wait("tensor", "tl" + tag, tlo)
        tp_ev = {}
        for c in range(TC):
            bk = c % 2
            if c >= 2:
                P.wait("tensor", V, tp_ev[c - 2])
            def rmm(e, c=c, bk=bk):
                o = PSB[0:128, bk * 512:bk * 512 + 8]
                last = None
                for src, wgt in ((R1, WGH), (R1, WGL), (R2, WGH)):
                    for k in range(KC):
                        first = (src is R1 and wgt is WGH and k == 0)
                        lastf = (src is R2 and k == KC - 1)
                        last = e.matmul(
                            o, src[:, k * T + c * 128:k * T + (c + 1) * 128],
                            wgt[:, k * 8:k * 8 + 8],
                            start=first, stop=lastf)
                return last
            t_mm = cop("tensor", TT_, rmm)
            P.wait("vector", TT_, t_mm)
            tp_ev[c] = cop("vector", V, lambda e, c=c, bk=bk: e.tensor_copy(
                LGT[:, 8 * bk:8 * bk + 8], PSB[0:128, bk * 512:bk * 512 + 8]))
            P.wait("sync", V, tp_ev[c])
            stg = dma("sync", "lg" + tag, lg_loc[c * 128:(c + 1) * 128, :],
                      LGT[:, 8 * bk:8 * bk + 8])
            P.wait("vector", "lg" + tag, stg - 16)
        P.wait("gpsimd", "lg" + tag, P.cur("lg" + tag))
        if stopped("rt"):
            return None
        cc_lg = coll("cl" + tag, "AllGather", [lg_loc], [lg_out])

        def h_phase(wA, wB, pref, first_bank_wait=None, per_m_hook=None):
            sil_tok = {}
            tt_tok = {}
            for m in range(MC):
                if per_m_hook is not None:
                    per_m_hook(m)
                buf = m % 2
                if m >= 2:
                    P.wait("gpsimd", pref + "tt", 4 * (m - 1))
                wslice(wA, m, KC, 128, WS1, buf * 2048)
                lB = wslice(wB, m, KC, 128, WS3, buf * 2048)
                P.wait("tensor", "wsl", lB)
                if m == 0 and first_bank_wait is not None:
                    first_bank_wait()
                if m >= 1:
                    for n in range(4):
                        P.wait("tensor", pref + "sl", sil_tok[(m - 1, n)])
                    P.wait("tensor", pref + "tt", tt_tok[(m - 1, 3)])
                def mmA(e, buf=buf):
                    last = None
                    for k in range(KC):
                        for n in range(4):
                            last = e.matmul(
                                psa(n),
                                WS1[:, buf * 2048 + k * 128:buf * 2048 + (k + 1) * 128],
                                R1[:, k * T + n * 512:k * T + n * 512 + 512],
                                start=(k == 0), stop=(k == KC - 1))
                    return last
                tA = cop("tensor", pref + "mm", mmA)
                for n in range(4):
                    P.wait("scalar", pref + "mm", tA)
                    sil_tok[(m, n)] = cop("scalar", pref + "sl",
                                          lambda e, n=n: e.activation(
                                              DUM[:, n * 512:(n + 1) * 512],
                                              psa(n), AF.Silu))
                P.wait("tensor", pref + "sl", sil_tok[(m, 1)])
                if m >= 1:
                    P.wait("tensor", pref + "tt", tt_tok[(m - 1, 1)])
                h3bank = [psb(0), psb(1), psa(0), psa(1)]
                def mmB(e, buf=buf, h3bank=h3bank):
                    last = None
                    for k in range(KC):
                        for n in range(4):
                            last = e.matmul(
                                h3bank[n],
                                WS3[:, buf * 2048 + k * 128:buf * 2048 + (k + 1) * 128],
                                R1[:, k * T + n * 512:k * T + n * 512 + 512],
                                start=(k == 0), stop=(k == KC - 1))
                    return last
                tB = cop("tensor", pref + "mm", mmB)
                for n in range(4):
                    P.wait("vector", pref + "mm", tB)
                    P.wait("vector", pref + "sl", sil_tok[(m, n)])
                    tt_tok[(m, n)] = cop("vector", pref + "tt", fence=False) if False else cop("vector", pref + "tt",
                                         lambda e, m=m, n=n, h3bank=h3bank:
                                         e.tensor_tensor(
                                             R2[:, m * T + n * 512:m * T + n * 512 + 512],
                                             DUM[:, n * 512:(n + 1) * 512],
                                             h3bank[n], ALU.mult))
            return tt_tok

        lread = {}
        f4a_state = {"started": False}

        def emit_f4a_chunk(g):
            if not f4a_state["started"]:
                P.wait("vector", "cl" + tag, cc_lg)
                P.wait("scalar", "lg" + tag, P.cur("lg" + tag))
                P.wait("scalar", "cl" + tag, cc_lg)
                f4a_state["started"] = True
            sl = 8 * (g % 2)
            if g >= 1:
                P.wait("scalar", V, lread[g - 1])
            ld = dma("scalar", "gl" + tag, LGT[:, sl:sl + 8],
                     lg_out[g * 128:(g + 1) * 128, :])
            P.wait("vector", "gl" + tag, ld)
            L = LGT[:, sl:sl + 8]
            O = OHB[:, 8 * g:8 * g + 8]
            cop("vector", V, lambda e, L=L: e.tensor_tensor(
                STAT[:, 28:32], L[:, 0:4], L[:, 4:8], ALU.max))
            cop("vector", V, lambda e: e.tensor_tensor(
                STAT[:, 28:30], STAT[:, 28:30], STAT[:, 30:32], ALU.max))
            cop("vector", V, lambda e: e.tensor_tensor(
                STAT[:, 26:27], STAT[:, 28:29], STAT[:, 29:30], ALU.max))
            lread[g] = cop("vector", V, lambda e, L=L, O=O: e.tensor_scalar(
                O, L, STAT[:, 26:27], None, ALU.is_equal))
            cop("vector", V, lambda e, O=O: e.tensor_mul(EQ[:, 16:24], O,
                                                         IOTA8[:]))
            cop("vector", V, lambda e: e.tensor_tensor(
                STAT[:, 28:32], EQ[:, 16:20], EQ[:, 20:24], ALU.min))
            cop("vector", V, lambda e: e.tensor_tensor(
                STAT[:, 28:30], STAT[:, 28:30], STAT[:, 30:32], ALU.min))
            cop("vector", V, lambda e: e.tensor_tensor(
                STAT[:, 27:28], STAT[:, 28:29], STAT[:, 29:30], ALU.min))
            cop("vector", V, lambda e, O=O: e.tensor_scalar(
                O, IOTA8[:], STAT[:, 27:28], None, ALU.is_equal))

        def f4a_hook(m):
            lo_g = m * 8
            hi_g = min(128, lo_g + 8)
            for g in range(lo_g, hi_g):
                emit_f4a_chunk(g)

        P.wait("vector", TT_, P.cur(TT_))
        h_phase(sw1, sw3, "sh" + tag, per_m_hook=f4a_hook)
        for g in range(8 * MC, 128):
            emit_f4a_chunk(g)
        f4a_done = P.cur(V)
        sh_tt_all = P.cur("sh" + tag + "tt")
        sh_mm_all = P.cur("sh" + tag + "mm")

        P.wait("tensor", V, f4a_done)
        P.wait("tensor", "sh" + tag + "tt", sh_tt_all)
        for g in range(128):
            cop("tensor", "gt" + tag, lambda e, g=g: e.matmul(
                PSB[0:1, 8 * (g % 64) + 512 * (g // 64):
                    8 * (g % 64) + 512 * (g // 64) + 8],
                ONESC_B[:], OHB[:, 8 * g:8 * g + 8],
                start=(g % 64 == 0), stop=(g % 64 == 63)))
        t_tot = P.cur("gt" + tag)
        P.wait("vector", "gt" + tag, t_tot)
        t_te = cop("vector", V, lambda e: e.tensor_copy(SC[0:1, 0:1024],
                                                        PSB[0:1, 0:1024]))
        P.wait("sync", V, t_te)
        tr1 = dma("sync", "tr" + tag, _ap(tot_rt, 0, [[1024, 1], [1, 1024]]),
                  SC[0:1, 0:1024])
        trdone["tok"] = tr1
        P.wait("sync", "tr" + tag, tr1)
        tr2 = dma("sync", "tr" + tag, TOTS[:], tot_rt)
        P.wait("tensor", "tr" + tag, tr2)
        P.wait("tensor", V, t_te)
        t_off = cop("tensor", "gt" + tag, lambda e: e.matmul(
            PSB[0:128, 0:8], TRIU_E[:], TOTS[:], start=True, stop=True))
        P.wait("vector", "gt" + tag, t_off)
        t_offe = cop("vector", V, lambda e: e.tensor_copy(OFFS[:],
                                                          PSB[0:128, 0:8]))
        P.wait("tensor", V, t_offe)
        t_co = cop("tensor", "gt" + tag, lambda e: e.matmul(
            PSB[0:1, 8:16], JONE[:], OFFS[:], start=True, stop=True))
        P.wait("vector", "gt" + tag, t_co)
        t_coe = cop("vector", V, lambda e: e.tensor_copy(LOC[0:1, 16:24],
                                                         PSB[0:1, 8:16]))
        P.wait("tensor", V, t_coe)
        t_cb = cop("tensor", "gt" + tag, lambda e: e.matmul(
            PSB[0:128, 8:16], ONES1[:], LOC[0:1, 16:24], start=True, stop=True))
        P.wait("vector", "gt" + tag, t_cb)
        t_coffe = cop("vector", V, lambda e: e.tensor_copy(COFF[:],
                                                           PSB[0:128, 8:16]))
        cop("vector", V, lambda e: e.memset(LOC[0:1, 0:8], 0.0))

        bpread = {}
        f5_tok = {}

        def emit_f5_chunk(lc):
            if lc == 0:
                P.wait("vector", "tr" + tag, trdone["tok"])
            if lc >= 1:
                P.wait("scalar", V, bpread[lc - 1])
            ld = dma("scalar", "ll" + tag, BP8[:],
                     lg_loc[lc * 128:(lc + 1) * 128, :])
            P.wait("vector", "ll" + tag, ld)
            cop("vector", V, lambda e: e.reduce_max(STAT[:, 8:9], BP8[:],
                                                    axis=AXX))
            cop("vector", V, lambda e: e.tensor_scalar(
                EQ[:, 24:32], BP8[:], STAT[:, 8:9], None, ALU.is_equal))
            cop("vector", V, lambda e: e.tensor_mul(SC[:, 16:24], EQ[:, 24:32],
                                                    IOTA8[:]))
            cop("vector", V, lambda e: e.tensor_reduce(
                STAT[:, 9:10], SC[:, 16:24], axis=AXX, op=ALU.min))
            t_oh = cop("vector", V, lambda e: e.tensor_scalar(
                EQ[:, 24:32], IOTA8[:], STAT[:, 9:10], None, ALU.is_equal))
            bpread[lc] = cop("vector", V, lambda e: e.tensor_scalar(
                SC[:, 0:8], BP8[:], STAT[:, 8:9], None, ALU.subtract))
            P.wait("scalar", V, P.cur(V))
            P.wait("scalar", "sh" + tag + "tt", min(4, sh_tt_all))
            t_e = cop("scalar", SCL, lambda e: e.activation(
                GN_S[:, 0:8], SC[:, 0:8], AF.Exp, accum_out=STAT[:, 10:11]))
            P.wait("vector", SCL, t_e)
            cop("vector", V, lambda e: e.reciprocal(STAT[:, 11:12],
                                                    STAT[:, 10:11]))
            P.wait("tensor", V, t_oh)
            if lc == 0:
                P.wait("tensor", V, t_coffe)
            bk = lc % 2
            t_pc = cop("tensor", "lt" + tag, lambda e, bk=bk: (
                e.matmul(PSB[0:1, 512 * bk + 24:512 * bk + 32], ONESC[:],
                         EQ[:, 24:32], start=True, stop=True),
                e.matmul(PSB[0:128, 512 * bk + 32:512 * bk + 40], TRIU_I[:],
                         EQ[:, 24:32], start=True, stop=False),
                e.matmul(PSB[0:128, 512 * bk + 32:512 * bk + 40], ONES1[:],
                         LOC[0:1, 0:8], start=False, stop=True))[-1])
            P.wait("vector", "lt" + tag, t_pc)
            cop("vector", V, lambda e, bk=bk: e.tensor_tensor(
                LOC[0:1, 0:8], LOC[0:1, 0:8],
                PSB[0:1, 512 * bk + 24:512 * bk + 32], ALU.add))
            cop("vector", V, lambda e, bk=bk: e.tensor_tensor(
                SC[:, 8:16], EQ[:, 24:32],
                PSB[0:128, 512 * bk + 32:512 * bk + 40], ALU.mult))
            cop("vector", V, lambda e: e.reduce_sum(STAT[:, 12:13], SC[:, 8:16],
                                                    axis=AXX))
            cop("vector", V, lambda e: e.tensor_mul(SC[:, 8:16], EQ[:, 24:32],
                                                    COFF[:]))
            cop("vector", V, lambda e: e.reduce_sum(STAT[:, 13:14], SC[:, 8:16],
                                                    axis=AXX))
            cop("vector", V, lambda e: e.tensor_add(STAT[:, 14:15],
                                                    STAT[:, 12:13],
                                                    STAT[:, 13:14]))
            cop("vector", V, lambda e: e.tensor_scalar(
                STAT[:, 15:16], STAT[:, 14:15], ccol(float(CAP)), None,
                ALU.is_le))
            cop("vector", V, lambda e: e.tensor_scalar(
                STAT[:, 17:18], STAT[:, 14:15], ccol(1.0), None, ALU.subtract))
            cop("vector", V, lambda e: e.scalar_tensor_tensor(
                STAT[:, 18:19], STAT[:, 17:18], ccol(SENT), STAT[:, 15:16],
                ALU.subtract, ALU.mult))
            cop("vector", V, lambda e, lc=lc: e.tensor_scalar(
                LSLOT[:, lc:lc + 1], STAT[:, 18:19], ccol(SENT), None, ALU.add))
            cop("vector", V, lambda e: e.tensor_scalar(
                STAT[:, 19:20], STAT[:, 9:10], ccol(512.0), None, ALU.mult))
            cop("vector", V, lambda e: e.scalar_tensor_tensor(
                STAT[:, 20:21], STAT[:, 12:13], ccol(4095.0), STAT[:, 19:20],
                ALU.add, ALU.add))
            cop("vector", V, lambda e: e.tensor_scalar(
                STAT[:, 21:22], STAT[:, 12:13], ccol(512.0), None, ALU.is_le))
            cop("vector", V, lambda e: e.scalar_tensor_tensor(
                STAT[:, 22:23], STAT[:, 20:21], ccol(SENT), STAT[:, 21:22],
                ALU.subtract, ALU.mult))
            cop("vector", V, lambda e, lc=lc: e.tensor_scalar(
                LSEND[:, lc:lc + 1], STAT[:, 22:23], ccol(SENT), None, ALU.add))
            f5_tok[lc] = cop("vector", V, lambda e, lc=lc: e.scalar_tensor_tensor(
                LGK[:, lc:lc + 1], STAT[:, 11:12], ccol(0.5), STAT[:, 15:16],
                ALU.mult, ALU.mult))

        if stopped("f5"):
            for lc in range(TC):
                emit_f5_chunk(lc)
            return None

        ds_tok = {}
        cp_tok = {}
        cp_ix = {}
        cc_a_box = {}

        def emit_f8_send(lc):
            ib = lc % 2
            buf = DUM[:] if ib == 0 else XT[:]
            P.wait("vector", V, f5_tok[lc])
            if lc >= 2:
                P.wait("vector", "ds" + tag, ds_tok[lc - 2])
                P.wait("scalar", "ds" + tag, ds_tok[lc - 2])
            t_ix = cop("vector", V, lambda e, lc=lc, ib=ib: e.tensor_copy(
                IDXS[ib][:], LSEND[:, lc:lc + 1]))
            lr = dma("scalar" if ib == 0 else "gpsimd", "dr" + tag, buf,
                     xlnr[lc * 128:(lc + 1) * 128, :])
            P.wait("gpsimd", V, t_ix)
            P.wait("gpsimd", "dr" + tag, lr)
            icall("ds" + tag, a2a_s, IndirectOffsetOnAxis(IDXS[ib][:], 0),
                  buf, None, 4095)
            ds_tok[lc] = icall("ds" + tag, mt_s,
                               IndirectOffsetOnAxis(IDXS[ib][:], 0),
                               LSLOT[:, lc:lc + 1], None, 4095)

        def emit_a2a():
            P.wait("gpsimd", "fl" + tag, P.cur("fl" + tag))
            P.wait("gpsimd", "ds" + tag, P.cur("ds" + tag))
            cc_a = coll("ca" + tag, "AllToAll", [a2a_s], [a2a_r])
            cc_a = coll("ca" + tag, "AllToAll", [mt_s], [mt_r])
            cc_a_box["t"] = cc_a

        def emit_f9_chunk(rc):
            ib = 2 + rc % 2
            buf = DUM[:] if rc % 2 == 0 else XT[:]
            if rc == 0:
                P.wait("scalar", "ca" + tag, cc_a_box["t"])
            if rc >= 2:
                P.wait("scalar", "cp" + tag, cp_tok[rc - 2])
                P.wait("vector", "cp" + tag, cp_tok[rc - 2])
            if rc >= 1:
                P.wait("scalar", V, cp_ix[rc - 1])
            ldm = dma("scalar", "mr" + tag, MR[:],
                      mt_r[rc * 128:(rc + 1) * 128, :])
            ldr = dma("scalar" if rc % 2 == 0 else "gpsimd", "rr" + tag, buf,
                      a2a_r[rc * 128:(rc + 1) * 128, :])
            P.wait("vector", "mr" + tag, ldm)
            t_ix = cop("vector", V, lambda e, ib=ib: e.tensor_copy(
                IDXS[ib][:], MR[:, 0:1]))
            cp_ix[rc] = t_ix
            P.wait("gpsimd", V, t_ix)
            P.wait("gpsimd", "rr" + tag, ldr)
            icall("cp" + tag, disp, IndirectOffsetOnAxis(IDXS[ib][:], 0),
                  buf, None, 2047)
            cp_tok[rc] = icall("cp" + tag, backp,
                               IndirectOffsetOnAxis(IDXS[ib][:], 0),
                               IOTARC[:, rc:rc + 1], None, 2047)

        P.wait("gpsimd", "sh" + tag + "mm", sh_mm_all)
        P.wait("tensor", "sh" + tag + "tt", sh_tt_all)
        sw2_ev = {}
        shs_tok = {}
        s2m_base = P.cur("s2m" + tag)
        for half in range(2):
            if half == 1:
                P.wait("gpsimd", "s2m" + tag, s2m_base + TC)
            dma("gpsimd", "w2l" + tag, R1[:, 0:MC * 1024],
                _ap(sw2, half * 1024, [[D, 128], [128 * D, MC], [1, 1024]]))
            wl = P.cur("w2l" + tag)
            P.wait("tensor", "w2l" + tag, wl)
            for m in range(TC):
                par = m % 2
                if m >= 2:
                    P.wait("tensor", "s2e" + tag, sw2_ev[(half, m - 2)])
                elif half == 1:
                    P.wait("tensor", "s2e" + tag, sw2_ev[(0, TC - 2 + m)])
                def mm2(e, m=m, par=par):
                    last = None
                    for k in range(MC):
                        for n in range(2):
                            last = e.matmul(
                                psa(par * 2 + n),
                                R2[:, k * T + m * 128:k * T + (m + 1) * 128],
                                R1[:, k * 1024 + n * 512:k * 1024 + (n + 1) * 512],
                                start=(k == 0), stop=(k == MC - 1))
                    return last
                t_m = cop("tensor", "s2m" + tag, mm2)
                P.wait("vector", "s2m" + tag, t_m)
                if (half, m - 2) in shs_tok:
                    P.wait("vector", "shs" + tag, shs_tok[(half, m - 2)])
                elif half == 1 and (0, TC - 2 + m) in shs_tok:
                    P.wait("vector", "shs" + tag, shs_tok[(0, TC - 2 + m)])
                bt = BT0 if par == 0 else BT1
                cop("vector", V, lambda e, par=par, bt=bt: e.tensor_copy(
                    bt[:, 0:512], psa(par * 2)))
                t_ev = cop("vector", "s2e" + tag,
                           lambda e, par=par, bt=bt: e.tensor_copy(
                               bt[:, 512:1024], psa(par * 2 + 1)))
                sw2_ev[(half, m)] = t_ev
                P.wait("sync", "s2e" + tag, t_ev)
                shs_tok[(half, m)] = dma(
                    "sync", "shs" + tag,
                    _ap(shr, m * 128 * D + half * 1024, [[D, 128], [1, 1024]]),
                    bt[:, 0:1024])
                if half == 0:
                    emit_f5_chunk(m)
                    emit_f8_send(m)
                    if m == TC - 1:
                        emit_a2a()
                else:
                    emit_f9_chunk(2 * m)
                    emit_f9_chunk(2 * m + 1)

        compact_done = P.cur("cp" + tag)
        if stopped("sh"):
            return None
        if stopped("disp"):
            return None

        P.wait("sync", "s2m" + tag, P.cur("s2m" + tag))
        P.wait("sync", "cp" + tag, compact_done)
        for k in range(KC):
            dma("sync", "xd" + tag, R1[:, k * T:(k + 1) * T],
                disp[:, k * 128:(k + 1) * 128], transpose=True)
        P.wait("tensor", "xd" + tag, P.cur("xd" + tag))
        P.wait("vector", "s2m" + tag, P.cur("s2m" + tag))
        s2e_all = P.cur("s2e" + tag)

        def exp_bank_wait():
            P.wait("tensor", "s2e" + tag, s2e_all)
        h_phase(w1, w3, "ex" + tag, first_bank_wait=exp_bank_wait)
        ex_tt_all = P.cur("ex" + tag + "tt")
        ex_mm_all = P.cur("ex" + tag + "mm")

        P.wait("gpsimd", "ex" + tag + "mm", ex_mm_all)
        e2m_base = P.cur("e2m" + tag)
        e2s_tok = {}
        e2_ix = {"last": None}
        for half in range(2):
            if half == 1:
                P.wait("gpsimd", "e2m" + tag, e2m_base + TC)
            dma("gpsimd", "e2l" + tag, R1[:, 0:MC * 1024],
                _ap(w2, half * 1024, [[D, 128], [128 * D, MC], [1, 1024]]))
            wl = P.cur("e2l" + tag)
            P.wait("tensor", "e2l" + tag, wl)
            for m in range(TC):
                par = m % 2
                ib = par
                if m >= 2:
                    P.wait("tensor", "e2s" + tag, e2s_tok[(half, m - 2)])
                elif half == 1:
                    P.wait("tensor", "e2s" + tag, e2s_tok[(0, TC - 2 + m)])
                if m == 0 and half == 0:
                    P.wait("tensor", "ex" + tag + "tt", ex_tt_all)
                def mm2(e, m=m, par=par):
                    last = None
                    for k in range(MC):
                        for n in range(2):
                            last = e.matmul(
                                psa(par * 2 + n),
                                R2[:, k * T + m * 128:k * T + (m + 1) * 128],
                                R1[:, k * 1024 + n * 512:k * 1024 + (n + 1) * 512],
                                start=(k == 0), stop=(k == MC - 1))
                    return last
                t_m = cop("tensor", "e2m" + tag, mm2)
                if e2_ix.get("last") is not None:
                    P.wait("scalar", V, e2_ix["last"])
                ldb = dma("scalar", "bp" + tag, BP8[:],
                          backp[m * 128:(m + 1) * 128, :])
                P.wait("vector", "bp" + tag, ldb)
                P.wait("vector", "e2m" + tag, t_m)
                if (half, m - 2) in e2s_tok:
                    P.wait("vector", "e2s" + tag, e2s_tok[(half, m - 2)])
                elif half == 1 and (0, TC - 2 + m) in e2s_tok:
                    P.wait("vector", "e2s" + tag, e2s_tok[(0, TC - 2 + m)])
                bt = BT0 if par == 0 else BT1
                cop("vector", V, lambda e, par=par, bt=bt: e.tensor_copy(
                    bt[:, 0:512], psa(par * 2)))
                cop("vector", V, lambda e, par=par, bt=bt: e.tensor_copy(
                    bt[:, 512:1024], psa(par * 2 + 1)))
                t_ix = cop("vector", V, lambda e, ib=ib: e.tensor_copy(
                    IDXS[ib][:], BP8[:, 0:1]))
                e2_ix["last"] = t_ix
                P.wait("gpsimd", V, t_ix)
                e2s_tok[(half, m)] = icall(
                    "e2s" + tag, b2a_s, IndirectOffsetOnAxis(IDXS[ib][:], 0),
                    bt[:, 0:1024], None, 4095, elem_off=half * 1024)
        P.wait("gpsimd", "e2s" + tag, P.cur("e2s" + tag))
        if stopped("exp"):
            return None
        cc_b = coll("cb" + tag, "AllToAll", [b2a_s], [b2a_r])

        P.wait("vector", "cb" + tag, cc_b)
        co_tok = {}
        for lc in range(TC):
            ib = lc % 2
            if lc >= 2:
                P.wait("gpsimd", V, co_tok[lc - 2])
            cop("vector", V, lambda e, lc=lc: e.tensor_scalar(
                STAT[:, 24:25], LSEND[:, lc:lc + 1], ccol(4095.0), None,
                ALU.min))
            t_ix = cop("vector", V, lambda e, ib=ib: e.tensor_copy(
                IDXS[ib][:], STAT[:, 24:25]))
            P.wait("gpsimd", V, t_ix)
            bt = BT0 if ib == 0 else BT1
            t_g = icall("cg" + tag, bt[:], None, b2a_r,
                        IndirectOffsetOnAxis(IDXS[ib][:], 0), 4095)
            lds = dma("scalar", "s2" + tag, DUM[:],
                      shr[lc * 128:(lc + 1) * 128, :])
            if lc >= 1:
                P.wait("sync", V, co_tok[lc - 1])
            ldx = dma("sync", "xc" + tag, XT[:],
                      xin_rows[lc * 128:(lc + 1) * 128, :])
            P.wait("vector", "cg" + tag, t_g)
            P.wait("vector", "s2" + tag, lds)
            if lc >= 1:
                P.wait("vector", "xo" + tag, P.cur("xo" + tag))
            cop("vector", V, lambda e, bt=bt, lc=lc: e.tensor_scalar(
                SC[:], bt[:], LGK[:, lc:lc + 1], None, ALU.mult))
            cop("vector", V, lambda e: e.scalar_tensor_tensor(
                SC[:], DUM[:], ccol(0.5), SC[:], ALU.mult, ALU.add))
            P.wait("vector", "xc" + tag, ldx)
            t_o = cop("vector", V, lambda e: e.tensor_add(SC[:], XT[:], SC[:]))
            co_tok[lc] = t_o
            P.wait("sync", V, t_o)
            dma("sync", "xo" + tag, xout_rows[lc * 128:(lc + 1) * 128, :], SC[:])
            P.wait("scalar", V, t_o)
        return P.cur("xo" + tag)

    def attn_pass(xin_rows, xout_rows):
        tag = "at"
        V, SCL, TT_ = "vat", "sat", "tat"

        def stopped(name):
            return STOP is not None and STOP == name

        xt_free = {}
        for c in range(TC):
            cs, cq, cv2, cr = c % 16, 16 + c % 16, 32 + c % 16, 48 + c % 16
            if c >= 1:
                P.wait("sync", V, xt_free[c - 1][0])
                P.wait("sync", SCL, xt_free[c - 1][1])
            ld = dma("sync", "xl" + tag, XT[:], xin_rows[c * 128:(c + 1) * 128, :])
            P.wait("vector", "xl" + tag, ld)
            cop("vector", V, lambda e, cs=cs: e.reduce_sum(
                STAT[:, cs:cs + 1], XT[:], axis=AXX), fence=False)
            P.wait("scalar", "xl" + tag, ld)
            t_sq = cop("scalar", SCL, lambda e, cq=cq: e.activation(
                DUM[:], XT[:], AF.Square, accum_out=STAT[:, cq:cq + 1]))
            if c >= 1:
                P.wait("vector", "xs" + tag, P.cur("xs" + tag) - 16)
            t_xc = cop("vector", V, lambda e: e.tensor_copy(SC[:], XT[:]),
                       fence=False)
            xt_free[c] = (t_xc, t_sq)
            P.wait("vector", SCL, t_sq)
            cop("vector", V, lambda e, cs=cs: e.tensor_scalar(
                STAT[:, cs:cs + 1], STAT[:, cs:cs + 1], ccol(1.0 / D), None,
                ALU.mult))
            cop("vector", V, lambda e, cs=cs, cv2=cv2: e.tensor_mul(
                STAT[:, cv2:cv2 + 1], STAT[:, cs:cs + 1], STAT[:, cs:cs + 1]))
            cop("vector", V, lambda e, cq=cq, cv2=cv2: e.scalar_tensor_tensor(
                STAT[:, cv2:cv2 + 1], STAT[:, cq:cq + 1], ccol(1.0 / D),
                STAT[:, cv2:cv2 + 1], ALU.mult, ALU.subtract))
            cop("vector", V, lambda e, cv2=cv2: e.tensor_scalar(
                STAT[:, cv2:cv2 + 1], STAT[:, cv2:cv2 + 1], ccol(EPS), None,
                ALU.add))
            t_rc = cop("vector", V, lambda e, cv2=cv2, cr=cr: e.reciprocal(
                STAT[:, cr:cr + 1], STAT[:, cv2:cv2 + 1]))
            P.wait("scalar", V, t_rc)
            t_rs = cop("scalar", SCL, lambda e, cr=cr: e.activation(
                STAT[:, cr:cr + 1], STAT[:, cr:cr + 1], AF.Sqrt))
            P.wait("vector", SCL, t_rs)
            t_n = cop("vector", V, lambda e, cs=cs, cr=cr: e.tensor_scalar(
                BT0[:], SC[:], STAT[:, cs:cs + 1], STAT[:, cr:cr + 1],
                ALU.subtract, ALU.mult), fence=False)
            P.wait("sync", V, t_n)
            st = dma("sync", "xs" + tag, xlnr[c * 128:(c + 1) * 128, :], BT0[:])
            P.wait("vector", "xs" + tag, st - 16)
        P.wait("sync", "xs" + tag, P.cur("xs" + tag))
        for k in range(KC):
            dma("sync", "th" + tag, R1[:, k * T:(k + 1) * T],
                xlnr[:, k * 128:(k + 1) * 128], transpose=True)
        thi = P.cur("th" + tag)
        dma("gpsimd", "tb" + tag, R2[:, A_T1:A_T1 + T], ck_t)
        dma("gpsimd", "tb" + tag, R2[:, A_T2:A_T2 + T], sk_t)
        P.wait("tensor", "th" + tag, thi)
        P.wait("vector", "tb" + tag, P.cur("tb" + tag))
        if stopped("aln"):
            return None

        def proj_rot_xpos(wsrc, h, dst_col, store_ch=None, store_to=None):
            P.wait("gpsimd", TT_, P.cur(TT_))
            lw = dma("gpsimd", "wsl", WS1[:, 0:2048],
                     _ap(wsrc, h * D * DH, [[DH, 128], [128 * DH, KC], [1, 128]]))
            P.wait("tensor", "wsl", lw)
            P.wait("tensor", V, P.cur(V))
            def mmP(e):
                last = None
                for k in range(KC):
                    for n in range(4):
                        last = e.matmul(
                            psa(n), WS1[:, k * 128:(k + 1) * 128],
                            R1[:, k * T + n * 512:k * T + n * 512 + 512],
                            start=(k == 0), stop=(k == KC - 1))
                return last
            t_p = cop("tensor", TT_, mmP)
            for n in range(4):
                P.wait("vector", TT_, t_p)
                if store_ch is not None:
                    P.wait("vector", store_ch, P.cur(store_ch))
                cop("vector", V, lambda e, n=n, dst_col=dst_col: e.tensor_copy(
                    R2[:, dst_col + n * 512:dst_col + (n + 1) * 512], psa(n)))
            t_raw = P.cur(V)
            for n in range(4):
                bk = n % 2
                P.wait("tensor", V, t_raw if n < 2 else P.cur(V))
                t_r = cop("tensor", TT_, lambda e, n=n, bk=bk, dst_col=dst_col:
                          e.matmul(psb(bk), RROT[:],
                                   R2[:, dst_col + n * 512:dst_col + (n + 1) * 512],
                                   start=True, stop=True))
                P.wait("vector", TT_, t_r)
                cop("vector", V, lambda e, n=n, bk=bk: e.tensor_tensor(
                    SC[:, 0:512], psb(bk),
                    R2[:, A_T2 + n * 512:A_T2 + (n + 1) * 512], ALU.mult))
                cop("vector", V, lambda e, n=n, dst_col=dst_col: e.tensor_tensor(
                    SC[:, 512:1024],
                    R2[:, dst_col + n * 512:dst_col + (n + 1) * 512],
                    R2[:, A_T1 + n * 512:A_T1 + (n + 1) * 512], ALU.mult))
                cop("vector", V, lambda e, n=n, dst_col=dst_col: e.tensor_tensor(
                    R2[:, dst_col + n * 512:dst_col + (n + 1) * 512],
                    SC[:, 0:512], SC[:, 512:1024], ALU.add))
            if store_to is not None:
                P.wait("sync", V, P.cur(V))
                dma("sync", store_ch, store_to, R2[:, dst_col:dst_col + T])
            return P.cur(V)

        for h in range(NH):
            proj_rot_xpos(wk, h, A_K, store_ch="kx" + tag,
                          store_to=kxf[h * 128:(h + 1) * 128, :])
            for t in range(TC):
                pt = PTB[t % 2]
                P.wait("tensor", V, P.cur(V))
                t_t = cop("tensor", TT_, lambda e, t=t, pt=pt: e.transpose(
                    pt, R2[:, A_K + t * 128:A_K + (t + 1) * 128], IDENT[:]))
                P.wait("vector", TT_, t_t)
                if t >= 2:
                    P.wait("vector", "ks" + tag, P.cur("ks" + tag))
                cop("vector", V, lambda e, t=t, pt=pt, h=h: e.tensor_scalar(
                    R2[:, A_KS + t * 128:A_KS + (t + 1) * 128], pt,
                    GKLC[:, h:h + 1], None, ALU.mult))
            P.wait("sync", V, P.cur(V))
            ks_st = dma("sync", "ks" + tag,
                        _ap(kst, h * 128, [[D, 128], [128 * D, TC], [1, 128]]),
                        R2[:, A_KS:A_KS + T])
            P.wait("gpsimd", TT_, P.cur(TT_))
            lw = dma("gpsimd", "wsl", WS3[:, 0:2048],
                     _ap(wv, h * 128, [[D, 128], [128 * D, KC], [1, 128]]))
            P.wait("tensor", "wsl", lw)
            P.wait("tensor", V, P.cur(V))
            def mmV(e):
                last = None
                for k in range(KC):
                    for n in range(4):
                        last = e.matmul(
                            psa(n), WS3[:, k * 128:(k + 1) * 128],
                            R1[:, k * T + n * 512:k * T + n * 512 + 512],
                            start=(k == 0), stop=(k == KC - 1))
                return last
            t_pv = cop("tensor", TT_, mmV)
            for n in range(4):
                P.wait("vector", TT_, t_pv)
                P.wait("vector", "kx" + tag, P.cur("kx" + tag))
                cop("vector", V, lambda e, n=n: e.tensor_copy(
                    R2[:, A_K + n * 512:A_K + (n + 1) * 512], psa(n)))
            for t in range(TC):
                pt = PTB[t % 2]
                P.wait("tensor", V, P.cur(V))
                t_t = cop("tensor", TT_, lambda e, t=t, pt=pt: e.transpose(
                    pt, R2[:, A_K + t * 128:A_K + (t + 1) * 128], IDENT[:]))
                P.wait("vector", TT_, t_t)
                if t >= 2:
                    P.wait("vector", "vs" + tag, P.cur("vs" + tag))
                cop("vector", V, lambda e, t=t, pt=pt: e.tensor_copy(
                    R2[:, A_VS + t * 128:A_VS + (t + 1) * 128], pt))
            P.wait("sync", V, P.cur(V))
            dma("sync", "vs" + tag,
                _ap(vst, h * 128, [[D, 128], [128 * D, TC], [1, 128]]),
                R2[:, A_VS:A_VS + T])
            P.wait("tensor", V, P.cur(V))
            tc_tok = {}
            for t in range(TC):
                tc_tok[t] = cop("tensor", TT_, lambda e, t=t: e.matmul(
                    PSA[:, t * 128:(t + 1) * 128],
                    R2[:, A_KS + t * 128:A_KS + (t + 1) * 128],
                    R2[:, A_VS + t * 128:A_VS + (t + 1) * 128],
                    start=True, stop=True))
            cop("vector", V, lambda e: e.memset(S_F[:], 0.0))
            for t in range(TC):
                P.wait("vector", TT_, tc_tok[min(t | 3, TC - 1)])
                cop("vector", V, lambda e, t=t, h=h: e.scalar_tensor_tensor(
                    S_F[:], S_F[:], ccol(GC[h]),
                    PSA[:, t * 128:(t + 1) * 128], ALU.mult, ALU.add))
            P.wait("sync", V, P.cur(V))
            if h < 8:
                dma("sync", "st" + tag, st_in[h * 128:(h + 1) * 128, :], S_F[:])
            else:
                dma("sync", "st" + tag,
                    st_in_b[(h - 8) * 128:(h - 7) * 128, :], S_F[:])
            P.wait("tensor", V, P.cur(V))
            if h == 7:
                P.wait("gpsimd", "st" + tag, P.cur("st" + tag))
                cc_st_a = coll("cs" + tag, "AllGather",
                               [st_in[0:1024, :]], [st_out[0:8192, :]])
        P.wait("gpsimd", "st" + tag, P.cur("st" + tag))
        if stopped("asum"):
            return None
        cc_st_b = coll("cs" + tag, "AllGather", [st_in_b], [st_out_b])

        P.wait("gpsimd", V, P.cur(V))
        dma("gpsimd", "tb2" + tag, R2[:, A_T1:A_T1 + T], cq_t)
        dma("gpsimd", "tb2" + tag, R2[:, A_T2:A_T2 + T], sq_t)
        P.wait("vector", "tb2" + tag, P.cur("tb2" + tag))

        for h in range(NH):
            proj_rot_xpos(wq, h, A_Q)
            P.wait("sync", TT_, P.cur(TT_))
            ldk = dma("sync", "mk" + tag, R2[:, A_K:A_K + T],
                      kxf[h * 128:(h + 1) * 128, :])
            dma("sync", "mk" + tag, R2[:, A_KS:A_KS + T],
                _ap(kst, h * 128, [[D, 128], [128 * D, TC], [1, 128]]))
            dma("sync", "mk" + tag, R2[:, A_VS:A_VS + T],
                _ap(vst, h * 128, [[D, 128], [128 * D, TC], [1, 128]]))
            cop("vector", V, lambda e: e.memset(S_F[:], 0.0))
            P.wait("scalar", "cs" + tag, cc_st_a if h < 8 else cc_st_b)
            st_src = st_out if h < 8 else st_out_b
            hh = h if h < 8 else h - 8
            for kk in range(NCORES):
                P.wait("scalar", V, P.cur(V))
                ldm = dma("scalar", "ms" + tag, MTILE[:],
                          st_src[kk * 8 * 128 + hh * 128:
                                 kk * 8 * 128 + (hh + 1) * 128, :])
                P.wait("vector", "ms" + tag, ldm)
                cop("vector", V, lambda e, h=h, kk=kk: e.scalar_tensor_tensor(
                    S_F[:], MTILE[:], WSPAN[:, 8 * h + kk:8 * h + kk + 1],
                    S_F[:], ALU.mult, ALU.add))
            sb_buf = [S_B, S_B2]
            ar_buf = [ARES, ARES2]
            t_sb = {}
            t_sb[-1] = cop("vector", V, lambda e: e.tensor_copy(S_B[:], S_F[:]))
            P.wait("tensor", "mk" + tag, P.cur("mk" + tag))
            yev = {}
            ymm = {}
            armm = {}
            tc_tok = {}
            for t in range(TC):
                half = t // 8
                if t % 8 == 0:
                    P.wait("tensor", V, P.cur(V) if t == 0 else t_sb[7])
                    for u in range(8):
                        tt = half * 8 + u
                        tc_tok[tt] = cop("tensor", TT_, lambda e, tt=tt, u=u:
                                         e.matmul(
                            PSB[0:128, u * 128:(u + 1) * 128],
                            R2[:, A_KS + tt * 128:A_KS + (tt + 1) * 128],
                            R2[:, A_VS + tt * 128:A_VS + (tt + 1) * 128],
                            start=True, stop=True))
                P.wait("vector", TT_, tc_tok[(t // 8) * 8 + 7])
                if t >= 1:
                    P.wait("vector", TT_, ymm[t - 1])
                cop("vector", V, lambda e, h=h, u=t % 8:
                    e.scalar_tensor_tensor(
                        S_F[:], S_F[:], ccol(GC[h]),
                        PSB[0:128, u * 128:u * 128 + 128],
                        ALU.mult, ALU.add))
                t_sb[t] = cop("vector", V,
                              lambda e, t=t: e.tensor_copy(
                                  sb_buf[(t + 1) % 2][:], S_F[:]))
                bk = t % 2
                ybank = (t // 4) % 2
                abank = 2 + t % 2
                if t >= 2:
                    P.wait("tensor", V, armm[t - 2])
                t_a = cop("tensor", TT_, lambda e, t=t, abank=abank: e.matmul(
                    PSA[:, abank * 512:abank * 512 + 128],
                    R2[:, A_K + t * 128:A_K + (t + 1) * 128],
                    R2[:, A_Q + t * 128:A_Q + (t + 1) * 128],
                    start=True, stop=True))
                P.wait("vector", TT_, t_a)
                t_ar = cop("vector", V, lambda e, t=t, abank=abank, h=h:
                           e.scalar_tensor_tensor(
                               ar_buf[t % 2][:],
                               PSA[:, abank * 512:abank * 512 + 128],
                               GINV[:, h:h + 1], MASKT[:], ALU.mult, ALU.mult))
                armm[t] = t_ar
                P.wait("tensor", V, t_ar)
                P.wait("tensor", V, t_sb[t - 1])
                if t % 4 == 0 and t >= 8:
                    P.wait("tensor", V, yev[ybank])
                def mmy(e, t=t, ybank=ybank):
                    o = PSA[:, ybank * 512 + (t % 4) * 128:
                            ybank * 512 + (t % 4) * 128 + 128]
                    e.matmul(o, ar_buf[t % 2][:],
                             R2[:, A_VS + t * 128:A_VS + (t + 1) * 128],
                             start=True, stop=False)
                    return e.matmul(o, R2[:, A_Q + t * 128:A_Q + (t + 1) * 128],
                                    sb_buf[t % 2][:], start=False, stop=True)
                t_y = cop("tensor", TT_, mmy)
                ymm[t] = t_y
                if t % 4 == 3:
                    P.wait("vector", TT_, t_y)
                    q = t - 3
                    for dt_ in range(4):
                        tk = cop("vector", V, lambda e, q=q, dt_=dt_, h=h,
                                 ybank=ybank: e.tensor_scalar(
                                     R2[:, (q + dt_) * T + h * 128:
                                        (q + dt_) * T + (h + 1) * 128],
                                     PSA[:, ybank * 512 + dt_ * 128:
                                         ybank * 512 + dt_ * 128 + 128],
                                     GPOW[:, h:h + 1], None, ALU.mult))
                    yev[ybank] = tk
                    P.wait("scalar", V, tk)
                    for dt_ in range(4):
                        col = (q + dt_) * NH + h
                        cop("scalar", SCL, lambda e, q=q, dt_=dt_, h=h,
                            col=col: e.activation(
                                DUM[:, 0:128],
                                R2[:, (q + dt_) * T + h * 128:
                                   (q + dt_) * T + (h + 1) * 128],
                                AF.Square,
                                accum_out=GN_S[:, 256 + col:257 + col]),
                            fence=False)
        heads_done = P.cur(V)
        if stopped("amain"):
            return None

        P.wait("scalar", V, heads_done)
        for tcn in range(TC):
            for h in range(NH):
                col = tcn * NH + h
                cop("vector", V, lambda e, tcn=tcn, h=h, col=col:
                    e.reduce_sum(GN_S[:, col:col + 1],
                                 R2[:, tcn * T + h * 128:
                                    tcn * T + (h + 1) * 128], axis=AXX),
                    fence=False)
        P.wait("vector", SCL, P.cur(SCL))
        P.wait("vector", V, P.cur(V))
        cop("vector", V, lambda e: e.tensor_scalar(
            GN_S[:, 0:256], GN_S[:, 0:256], ccol(1.0 / 128), None, ALU.mult))
        cop("vector", V, lambda e: e.tensor_mul(
            SC[:, 0:256], GN_S[:, 0:256], GN_S[:, 0:256]))
        cop("vector", V, lambda e: e.scalar_tensor_tensor(
            GN_S[:, 256:512], GN_S[:, 256:512], ccol(1.0 / 128),
            SC[:, 0:256], ALU.mult, ALU.subtract))
        cop("vector", V, lambda e: e.tensor_scalar(
            GN_S[:, 256:512], GN_S[:, 256:512], ccol(EPS), None, ALU.add))
        t_rc = cop("vector", V, lambda e: e.reciprocal(SC[:, 256:512],
                                                       GN_S[:, 256:512]))
        P.wait("scalar", V, t_rc)
        t_rs = cop("scalar", SCL, lambda e: e.activation(
            SC[:, 256:512], SC[:, 256:512], AF.Sqrt))
        P.wait("vector", SCL, t_rs)
        for tcn in range(TC):
            for h in range(NH):
                col = tcn * NH + h
                cop("vector", V, lambda e, tcn=tcn, h=h, col=col:
                    e.tensor_scalar(
                        BT0[:, h * 128:(h + 1) * 128],
                        R2[:, tcn * T + h * 128:tcn * T + (h + 1) * 128],
                        GN_S[:, col:col + 1], SC[:, 256 + col:257 + col],
                        ALU.subtract, ALU.mult), fence=False)
            t_n = P.cur(V)
            P.wait("sync", V, t_n)
            st = dma("sync", "yn" + tag, ynr[tcn * 128:(tcn + 1) * 128, :],
                     BT0[:])
            P.wait("vector", "yn" + tag, st)
        yn_done = P.cur("yn" + tag)
        if stopped("agn"):
            return None

        P.wait("sync", "yn" + tag, yn_done)
        gw_tok = {}
        gmm_tok = {}
        P.wait("tensor", V, P.cur(V))
        for m in range(NH if GLIMIT is None else GLIMIT):
            if m >= 1:
                P.wait("sync", V, P.cur(V))
            tky = dma("sync", "yf" + tag, R2[:, A_K:A_K + T],
                      ynr[:, m * 128:(m + 1) * 128], transpose=True)
            if m >= 2:
                P.wait("gpsimd", TT_, gmm_tok[m - 1])
            lw = wslice(wgm, m, KC, 128, WS1, (m % 2) * 2048)
            P.wait("vector", "yf" + tag, tky)
            t_aff = cop("vector", V, lambda e, m=m: e.tensor_scalar(
                R2[:, A_K:A_K + T], R2[:, A_K:A_K + T], GNW[:, m:m + 1],
                GNB[:, m:m + 1], ALU.mult, ALU.add))
            P.wait("tensor", "wsl", lw)
            if m >= 1:
                P.wait("tensor", V, gw_tok[m - 1])
            def mmG(e, m=m):
                last = None
                for k in range(KC):
                    for n in range(4):
                        last = e.matmul(
                            psa(n), WS1[:, (m % 2) * 2048 + k * 128:
                                         (m % 2) * 2048 + (k + 1) * 128],
                            R1[:, k * T + n * 512:k * T + n * 512 + 512],
                            start=(k == 0), stop=(k == KC - 1))
                return last
            t_g = cop("tensor", TT_, mmG)
            gmm_tok[m] = t_g
            for n in range(4):
                P.wait("scalar", TT_, t_g)
                t_sil = cop("scalar", SCL, lambda e, n=n: e.activation(
                    DUM[:, n * 512:(n + 1) * 512], psa(n), AF.Silu))
                P.wait("vector", SCL, t_sil)
                P.wait("vector", V, t_aff)
                gw_tok[m] = cop("vector", V, lambda e, m=m, n=n: e.tensor_tensor(
                    R2[:, m * T + n * 512:m * T + (n + 1) * 512],
                    DUM[:, n * 512:(n + 1) * 512],
                    R2[:, A_K + n * 512:A_K + (n + 1) * 512], ALU.mult))
        gated_done = P.cur(V)
        if stopped("agt"):
            return None

        P.wait("gpsimd", TT_, P.cur(TT_))
        for k in range(KC):
            dma("gpsimd", "wo" + tag, R1[:, k * T:(k + 1) * T],
                _ap(wo, k * 128 * D, [[D, 128], [1, D]]))
        wol = P.cur("wo" + tag)
        P.wait("tensor", "wo" + tag, wol)
        P.wait("tensor", V, gated_done)
        wo_ev = {}
        for m in range(TC):
            par = m % 2
            banks = [psa(0), psa(1), psa(2), psa(3)] if par == 0 else \
                    [psb(0), psb(1), psa(0), psa(1)]
            if m >= 1:
                P.wait("tensor", "we" + tag, wo_ev[m - 1])
            def mmO(e, m=m, banks=banks):
                last = None
                for k in range(KC):
                    for n in range(4):
                        last = e.matmul(
                            banks[n],
                            R2[:, k * T + m * 128:k * T + (m + 1) * 128],
                            R1[:, k * T + n * 512:k * T + n * 512 + 512],
                            start=(k == 0), stop=(k == KC - 1))
                return last
            t_m = cop("tensor", "wm" + tag, mmO)
            if m >= 1:
                P.wait("sync", "xo" + tag, P.cur("xo" + tag))
            ldx = dma("sync", "xw" + tag, XT[:],
                      xin_rows[m * 128:(m + 1) * 128, :])
            P.wait("vector", "wm" + tag, t_m)
            P.wait("vector", "xw" + tag, ldx)
            for n in range(4):
                cop("vector", V, lambda e, n=n, banks=banks: e.tensor_tensor(
                    XT[:, n * 512:(n + 1) * 512], banks[n],
                    XT[:, n * 512:(n + 1) * 512], ALU.add))
            wo_ev[m] = cop("vector", "we" + tag, lambda e: e.nop())
            P.wait("sync", "we" + tag, wo_ev[m])
            dma("sync", "xo" + tag, xout_rows[m * 128:(m + 1) * 128, :], XT[:])
        return P.cur("xo" + tag)

    r1 = ffn_pass("p1", x0, xr1)
    fin = None
    if r1 is not None and STOP != "p1":
        P.barrier("after_p1")
        ra = attn_pass(xr1, xr2)
        if ra is not None and STOP != "at":
            P.barrier("after_at")
            fin = ffn_pass("p3", xr2, out_x3)
    if fin is None:
        P.barrier("trunc")
        for c in range(TC):
            dma("gpsimd", "xop3", out_x3[c * 128:(c + 1) * 128, :],
                x0[c * 128:(c + 1) * 128, :])
        fin = P.cur("xop3")

    if DBG:
        P.barrier("dbgbar")
        for name in DBG:
            for c in range(0, dramv[name].tensor.shape[0], 128):
                n = min(128, dramv[name].tensor.shape[0] - c)
                dma("gpsimd", "dbgc", dbg[name][c:c + n, :], dramv[name][c:c + n, :])
        P.wait("gpsimd", "dbgc", P.cur("dbgc"))

    P.wait("sync", "xo", fin)
    P.wait("gpsimd", "xo", fin)

    def _simulate():
        sems = defaultdict(int)
        pcs = {e: 0 for e in Prog.ENGS}
        progress = True
        while progress:
            progress = False
            for e in Prog.ENGS:
                tr = P.trace[e]
                while pcs[e] < len(tr):
                    kind, sem, val = tr[pcs[e]]
                    if kind == "w" and sems[sem] < val:
                        break
                    if kind == "i":
                        sems[sem] += val
                    pcs[e] += 1
                    progress = True
        stuck = {e: pcs[e] for e in Prog.ENGS if pcs[e] < len(P.trace[e])}
        if stuck:
            for e, pc in stuck.items():
                print(f"DEADLOCK {e} at {pc}/{len(P.trace[e])}: "
                      f"{P.trace[e][pc]}  sem={sems[P.trace[e][pc][1]]}")
            raise RuntimeError("deadlock in recorded schedule")
    _simulate()

    with es, nc.Block() as block:
        semctx = contextlib.ExitStack()
        sems = {n: semctx.enter_context(nc.semaphore(n))
                for n in sorted(P.semnames)}
        with semctx:
            def run(e, key):
                e._sems = sems
                if key == "gpsimd":
                    e._regs = {2047: e.to_reg(2047), 4095: e.to_reg(4095)}
                for f in P.ops[key]:
                    f(e)

            @block.sync
            def _(s):
                run(s, "sync")

            @block.scalar
            def _(s):
                run(s, "scalar")

            @block.vector
            def _(v):
                run(v, "vector")

            @block.tensor
            def _(t):
                run(t, "tensor")

            @block.gpsimd
            def _(g):
                run(g, "gpsimd")

    return nc


def _host_constants():
    perm = np.empty(DH, np.int64)
    perm[:64] = np.arange(64) * 2
    perm[64:] = np.arange(64) * 2 + 1
    c = {"perm": perm}
    c["triu_i"] = np.triu(np.ones((128, 128), np.float32))
    c["triu_e"] = np.triu(np.ones((128, 128), np.float32), 1)
    c["ident"] = np.eye(128, dtype=ml_dtypes.bfloat16)
    c["identf"] = np.eye(8, dtype=np.float32)
    R = np.zeros((128, 128), np.float32)
    for i in range(64):
        R[64 + i, i] = -1.0
        R[i, 64 + i] = 1.0
    c["rrot"] = R.astype(ml_dtypes.bfloat16)
    c["ones1"] = np.ones((1, 128), np.float32)
    c["onesc"] = np.ones((128, 1), np.float32)
    c["iota8m8"] = np.broadcast_to(np.arange(8) - 8.0, (128, 8)).astype(np.float32).copy()
    c["iotarc"] = (np.arange(128)[:, None] + 128.0 * np.arange(32)[None, :]).astype(np.float32)
    c["maskt"] = (np.arange(128)[None, :] >= np.arange(128)[:, None]).astype(ml_dtypes.bfloat16)
    g = 1.0 - np.exp(np.linspace(np.log(1 / 32), np.log(1 / 512), NH))
    c["gammas"] = g
    l = np.arange(128)[:, None].astype(np.float64)
    c["gpow"] = (g[None, :] ** l).astype(np.float32)
    c["gklc"] = (g[None, :] ** (128 - l)).astype(np.float32)
    c["ginv"] = (g[None, :] ** (-l)).astype(np.float32)
    return c


def _xpos_tables(span_start):
    d, half = DH, DH // 2
    pos = np.arange(span_start, span_start + T, dtype=np.float64)
    xscale = (np.arange(0, d, 2).astype(np.float64) + 0.4 * d) / (1.4 * d)
    scale = xscale[None, :] ** (pos[:, None] / 512.0)
    inv_freq = 1.0 / 10000.0 ** (np.arange(half).astype(np.float64) / half)
    sinu = pos[:, None] * inv_freq[None, :]
    sin, cos = np.sin(sinu), np.cos(sinu)

    def permfm(v):
        out = np.empty((128, T), np.float64)
        out[:64] = v.T
        out[64:] = v.T
        return out
    return (permfm(cos * scale).astype(np.float32),
            permfm(sin * scale).astype(np.float32),
            permfm(cos / scale).astype(np.float32),
            permfm(sin / scale).astype(np.float32))


def _prep_inputs(inp, consts_vals):
    hc = _host_constants()
    perm = hc["perm"]
    X = np.asarray(inp["L0_attn_X"], np.float32).reshape(-1, D)
    WQ = np.asarray(inp["L0_attn_W_Q"], np.float32)[:, :, perm].reshape(NH * D, DH)
    WK = np.asarray(inp["L0_attn_W_K"], np.float32)[:, :, perm].reshape(NH * D, DH)
    WV = np.transpose(np.asarray(inp["L0_attn_W_V"], np.float32), (1, 0, 2)).reshape(D, D)
    gnw_c = np.asarray(inp["L0_attn_gn_weight"], np.float32).reshape(NH, 128).T.copy()
    gnb_c = np.asarray(inp["L0_attn_gn_bias"], np.float32).reshape(NH, 128).T.copy()
    Wg = np.asarray(inp["L0_ffn_Wg"], np.float32)
    wgate = Wg.reshape(KC, 128, 8).transpose(1, 0, 2).reshape(128, 128).copy()
    consts_arr = np.zeros((128, 64), np.float32)
    for i, v in enumerate(consts_vals):
        consts_arr[:, i] = v
    gam = hc["gammas"]
    L = float(T)

    in_maps = []
    for j in range(NCORES):
        s, b = j % 4, j // 4
        cq, sq, ck, sk = _xpos_tables(s * T)
        jone = np.zeros((128, 1), np.float32)
        jone[16 * j, 0] = 1.0
        wspan = np.zeros((128, 128), np.float32)
        for h in range(NH):
            for k in range(NCORES):
                kb, ks = k // 4, k % 4
                if kb == b and ks < s:
                    wspan[:, 8 * h + k] = gam[h] ** (L * (s - 1 - ks))
        m = dict(
            x0=np.ascontiguousarray(X[j * T:(j + 1) * T]),
            wq=WQ, wk=WK, wv=WV,
            wgm=np.asarray(inp["L0_attn_W_G"], np.float32),
            wo=np.asarray(inp["L0_attn_W_O"], np.float32),
            gnw=gnw_c, gnb=gnb_c, wgate=wgate,
            w1=np.ascontiguousarray(np.asarray(inp["L0_ffn_W1"], np.float32)[j]),
            w3=np.ascontiguousarray(np.asarray(inp["L0_ffn_W3"], np.float32)[j]),
            w2=np.ascontiguousarray(np.asarray(inp["L0_ffn_W2"], np.float32)[j]),
            sw1=np.asarray(inp["L0_ffn_sw1"], np.float32),
            sw3=np.asarray(inp["L0_ffn_sw3"], np.float32),
            sw2=np.asarray(inp["L0_ffn_sw2"], np.float32),
            cq=cq, sq=sq, ck=ck, sk=sk,
            gpow=hc["gpow"], gklc=hc["gklc"], ginv=hc["ginv"],
            maskt=hc["maskt"], triu_i=hc["triu_i"], triu_e=hc["triu_e"],
            ident=hc["ident"], identf=hc["identf"], rrot=hc["rrot"],
            ones1=hc["ones1"], onesc=hc["onesc"], iota8m8=hc["iota8m8"],
            iotarc=hc["iotarc"], jone=jone, wspan=wspan, constsin=consts_arr,
        )
        in_maps.append(m)
    return in_maps


def kernel(**inputs):
    consts_vals = []
    with allow_ici_all_to_all():
        nc = build(consts_vals)
        in_maps = _prep_inputs(inputs, consts_vals)
        res = run_bass_kernel_spmd(nc, in_maps, core_ids=list(range(NCORES)))
    out = np.concatenate([np.asarray(r["out"], np.float32) for r in res.results], 0)
    return out.reshape(2, 8192, D)
